# version 04
# baseline (speedup 1.0000x reference)
import numpy as np
import concourse.bass as bass
import concourse.mybir as mybir
from concourse.bass_utils import run_bass_kernel_spmd
from concourse.bass import IndirectOffsetOnAxis
from contextlib import ExitStack

F32 = mybir.dt.float32
F32R = mybir.dt.float32r
BF16 = mybir.dt.bfloat16
I32 = mybir.dt.int32
U32 = mybir.dt.uint32
OP = mybir.AluOpType
AF = mybir.ActivationFunctionType
AX = mybir.AxisListType.X

NB, N, D, C, H, DH = 2, 4096, 1024, 2048, 16, 64
NT = 16
ROUNDS = 24
DBG = False
SKIP_EF = False
PHASE_LIM = 99
SETUP_LIM = 99
A_TILES = 16
A_STAGE = 99
DUMP_XC = False
BIGF = float(1 << 22)
EPS = 1e-5
INVCM1 = 1.0 / (D - 1)

ENGINES = ("sync", "scalar", "vector", "tensor", "gpsimd")


class Prog:
    def __init__(self):
        self.ops = {e: [] for e in ENGINES}
        self.counts = {}
        self.hw = {e: {} for e in ENGINES}

    def emit(self, eng, fn, waits=(), inc=None, k=1):
        w = []
        for ev in waits:
            if ev is None:
                continue
            s, v = ev
            if self.hw[eng].get(s, 0) < v:
                w.append((s, v))
                self.hw[eng][s] = v
        ev_out = None
        v = 0
        if inc is not None:
            self.counts[inc] = self.counts.get(inc, 0) + k
            v = self.counts[inc]
            ev_out = (inc, v)
        self.ops[eng].append((fn, tuple(w), inc, k, v))
        return ev_out

    def barrier(self):
        snap = {s: v for s, v in self.counts.items() if s in ("tn", "ve", "ac", "gp")}
        self.counts["bar"] = self.counts.get("bar", 0) + 5
        tgt = self.counts["bar"]
        for e in ENGINES:
            self.ops[e].append(("BARRIER", (), tgt, 1, tgt))
            hw = self.hw[e]
            for s, v in snap.items():
                hw[s] = max(hw.get(s, 0), v)
            hw["bar"] = tgt


    def finalize(self):
        waited = {}
        for e in ENGINES:
            for op in self.ops[e]:
                if op[0] == "BARRIER":
                    continue
                for (s, v) in op[1]:
                    waited.setdefault(s, set()).add(v)
        remap = {}
        for s in ("tn", "ve", "ac", "gp"):
            vals = sorted(waited.get(s, ()))
            remap[s] = {v: i + 1 for i, v in enumerate(vals)}
        for e in ENGINES:
            newops = []
            for fn, w, inc, k, v in self.ops[e]:
                if fn == "BARRIER":
                    newops.append((fn, w, inc, k, v))
                    continue
                w2 = tuple((s, remap[s][x]) if s in remap else (s, x) for (s, x) in w)
                if inc in remap:
                    if v in remap[inc]:
                        newops.append((fn, w2, inc, k, remap[inc][v]))
                    else:
                        newops.append((fn, w2, None, 0, 0))
                else:
                    newops.append((fn, w2, inc, k, v))
            self.ops[e] = newops


class Vec:

    def __init__(self, v):
        self.v = v
        self.dirty = set()

    def _nm(self, ap):
        t = getattr(ap, "tensor", None)
        return t.name if t is not None else None

    def _pre(self, aps):
        names = {self._nm(a) for a in aps if a is not None} - {None}
        if self.dirty & names:
            self.v.drain()
            self.dirty.clear()

    def _post(self, out, extra=None):
        self.dirty.add(self._nm(out))
        if extra is not None:
            self.dirty.add(self._nm(extra))

    def ts(self, out, in0, s1, s2, op0, op1=None, accum_out=None):
        ins = [in0, out] + [s for s in (s1, s2) if hasattr(s, "tensor")]
        self._pre(ins)
        if op1 is None:
            r = self.v.tensor_scalar(out, in0, s1, s2, op0, accum_out=accum_out)
        else:
            r = self.v.tensor_scalar(out, in0, s1, s2, op0, op1, accum_out=accum_out)
        self._post(out, accum_out)
        return r

    def tt(self, out, a, b, op):
        self._pre([a, b, out])
        r = self.v.tensor_tensor(out, a, b, op)
        self._post(out)
        return r

    def stt(self, out, in0, scalar, in1, op0, op1, accum_out=None):
        ins = [in0, in1, out] + ([scalar] if hasattr(scalar, "tensor") else [])
        self._pre(ins)
        r = self.v.scalar_tensor_tensor(out, in0, scalar, in1, op0, op1, accum_out=accum_out)
        self._post(out, accum_out)
        return r

    def red(self, out, in_, op):
        self._pre([in_, out])
        r = self.v.tensor_reduce(out, in_, AX, op)
        self._post(out)
        return r

    def copy(self, out, in_):
        self._pre([in_, out])
        r = self.v.tensor_copy(out, in_)
        self._post(out)
        return r

    def memset(self, out, c):
        self._pre([out])
        r = self.v.memset(out, c)
        self._post(out)
        return r

    def cpred(self, out, mask, data):
        self._pre([out, mask, data])
        r = self.v.copy_predicated(out, mask, data)
        self._post(out)
        return r

    def recip(self, out, in_):
        self._pre([in_, out])
        r = self.v.reciprocal(out, in_)
        self._post(out)
        return r


def build_nc():
    nc = bass.Bass("TRN2", target_bir_lowering=False)
    es = ExitStack()
    P = Prog()
    dt_ = nc.dram_tensor

    xT = dt_("xT", [D, NB * N], F32R, kind="ExternalInput").ap()
    x_tm = dt_("x_tm", [NB * N, D], F32, kind="ExternalInput").ap()
    qkv_w = dt_("qkv_w", [D, 3 * D], F32R, kind="ExternalInput").ap()
    proj_w = dt_("proj_w", [D, D], F32R, kind="ExternalInput").ap()
    ch_w1 = dt_("ch_w1", [D, 256], F32R, kind="ExternalInput").ap()
    ch_w2 = dt_("ch_w2", [256, D], F32R, kind="ExternalInput").ap()
    ch_b1 = dt_("ch_b1", [256], F32, kind="ExternalInput").ap()
    ch_b2 = dt_("ch_b2", [D], F32, kind="ExternalInput").ap()
    proj_b = dt_("proj_b", [D], F32, kind="ExternalInput").ap()
    wg_in = dt_("wg_in", [D, 1], F32, kind="ExternalInput").ap()
    wq_in = dt_("wq_in", [D, D], F32, kind="ExternalInput").ap()
    wk_in = dt_("wk_in", [D, D], F32, kind="ExternalInput").ap()
    wv_in = dt_("wv_in", [D, D], F32, kind="ExternalInput").ap()
    wo_in = dt_("wo_in", [D, D], F32, kind="ExternalInput").ap()
    sm_in = {}
    for nm, shp in [("c1", [1]), ("c2", [1]), ("conv1", [4, 1, 3]), ("gn_g", [4]),
                    ("gn_b", [4]), ("conv2", [1, 4, 3]), ("conv2b", [1])]:
        sm_in[nm] = dt_("in_" + nm, shp, F32, kind="ExternalInput").ap()
    tri_in = dt_("tri_in", [128, 128], F32, kind="ExternalInput").ap()
    onescf_in = dt_("onescf_in", [128, 1], F32, kind="ExternalInput").ap()
    onescr_in = dt_("onescr_in", [128, 1], F32R, kind="ExternalInput").ap()
    onesrf_in = dt_("onesrf_in", [1, 128], F32, kind="ExternalInput").ap()
    onesrr_in = dt_("onesrr_in", [1, 128], F32R, kind="ExternalInput").ap()
    onesall_in = dt_("onesall_in", [128, 128], F32, kind="ExternalInput").ap()
    ident_in = dt_("ident_in", [128, 128], F32, kind="ExternalInput").ap()
    slotid_in = dt_("slotid_in", [128, 16], F32, kind="ExternalInput").ap()

    out_t = dt_("out_t", [NB * N, D], F32, kind="ExternalOutput").ap()
    vT_sp = dt_("vT_sp", [D, NB * N], F32R, kind="ExternalOutput" if DBG else "Internal").ap()
    x2_sp = dt_("x2_sp", [NB * N, D], F32, kind="ExternalOutput" if DBG else "Internal").ap()
    ln2_sp = dt_("ln2_sp", [NB * N, D], BF16, kind="ExternalOutput" if DBG else "Internal").ap()
    ab_pad = dt_("ab_pad", [NB, N + 128], F32, kind="Internal").ap()
    hh_pad = dt_("hh_pad", [NB, 4, N + 128], F32, kind="Internal").ap()
    comp = dt_("comp", [NB * C, D], BF16, kind="ExternalOutput" if DBG else "Internal").ap()
    qT_sp = dt_("qT_sp", [NB, D, C], BF16, kind="ExternalOutput" if DBG else "Internal").ap()
    kT_sp = dt_("kT_sp", [NB, D, C], BF16, kind="Internal").ap()
    ffn_sp = dt_("ffn_sp", [NB * C, D], F32, kind="ExternalOutput" if DBG else "Internal").ap()
    tokid_sp = dt_("tokid_sp", [NB * C, 1], F32, kind="Internal").ap()
    wo_bf = dt_("wo_bf", [D, D], BF16, kind="Internal").ap()
    dbg_st = dt_("dbg_st", [16, 128, 64], F32, kind="ExternalOutput").ap() if DBG else None
    dbg_big = dt_("dbg_big", [1152, 2048], F32, kind="ExternalOutput").ap() if DBG else None

    sbuf = lambda *a: es.enter_context(nc.sbuf_tensor(*a))
    AR = 166 * 1024
    t_arena = es.enter_context(nc.sbuf_tensor("arena", [128, AR], mybir.dt.uint8))
    arena_base = nc.lookup_mloc(t_arena).addr

    def place(name, shape, dtype, off_kb):
        import functools, operator
        hnd = nc.virtual_sbuf_tensor(name, shape, dtype)
        nc._tensor(hnd.name, hnd.shape, hnd.dtype, type="SB")
        mloc = nc.lookup_mloc(hnd)
        mloc.allocated = True
        mloc.addr = arena_base + int(off_kb * 1024)
        pp = functools.reduce(operator.mul, shape[1:], 1) * mybir.dt.size(dtype)
        assert int(off_kb * 1024) + pp <= AR, (name, off_kb, pp)
        return hnd

    t_wqk = place("t_wqk", [128, 8, 2048], F32R, 0)
    t_wv = place("t_wv", [128, 8, 1024], F32R, 64)
    t_proj = place("t_proj", [128, 8, 1024], F32R, 0)
    t_chw1 = place("t_chw1", [128, 8, 256], F32R, 64)
    t_chw2 = place("t_chw2", [128, 2, 1024], F32R, 72)
    t_attnblk = place("t_attnblk", [128, 4, 1024], F32, 32)
    t_x2blk = place("t_x2blk", [128, 4, 1024], F32, 48)
    t_xt = [place(f"t_xt{j}", [128, 8, 512], F32R, 96 + 16 * j) for j in range(2)]
    t_xtm = [place(f"t_xtm{j}", [128, 4, 1024], F32, 128 + 16 * j) for j in range(2)]
    ode_names = ["am1", "a0", "ap1", "hh0", "hh1", "hh2", "hh3", "tmp", "yy", "fa"]
    t_ode = {nm: place("t_ode_" + nm, [32, 128], F32, 160 + 0.5 * k)
             for k, nm in enumerate(ode_names)}
    t_l2sT = place("t_l2sT", [128, 8, 2048], BF16, 0)
    t_wffn = place("t_wffn", [128, 8, 1024], BF16, 32)
    t_vffn = place("t_vffn", [128, 16, 1040], BF16, 64)
    t_vffn4 = place("t_vffn4", [128, 16, 16, 65], BF16, 64)
    t_attnblkB = place("t_attnblkB", [128, 4, 1024], F32, 64)
    t_aoT = place("t_aoT", [64, 16, 2048], BF16, 0)
    t_expS = place("t_expS", [128, 16, 512], BF16, 97)
    t_wo2 = place("t_wo2", [64, 16, 1024], BF16, 113)
    t_kh = place("t_kh", [64, 2048], BF16, 145)
    t_qh = place("t_qh", [64, 2048], BF16, 149)
    t_orow = place("t_orow", [128, 1024], F32, 153)
    t_drow = place("t_drow", [128, 2048], F32, 157)
    t_zeroE = place("t_zeroE", [128, 1024], BF16, 97)
    t_rows = place("t_rows", [128, 1024], BF16, 99)
    t_rows1 = place("t_rows1", [128, 1024], BF16, 101)
    t_facc = place("t_facc", [128, 1024], F32, 0)
    t_fx2 = place("t_fx2", [128, 1024], F32, 4)
    t_x2t = place("t_x2t", [128, 1024], F32, 80)
    t_biasbc = place("t_biasbc", [128, 1024], F32, 88)
    t_wgbc = place("t_wgbc", [128, 1024], F32, 92)
    t_bias_r = place("t_bias_r", [1, 1024], F32, 80)
    t_wgrow = place("t_wgrow", [1, 1024], F32, 84)
    t_ln2t = place("t_ln2t", [128, 1024], BF16, 84)
    t_gsel = place("t_gsel", [1, 2048], F32, 97)

    t_vout = sbuf("t_vout", [128, 2, 512], F32R)
    t_xm = sbuf("t_xm", [128, 1024], F32R)
    t_qksc = sbuf("t_qksc", [128, 1024], BF16)
    t_qksc2 = sbuf("t_qksc2", [128, 1024], BF16)
    t_zscr = sbuf("t_zscr", [128, 1024], BF16)
    t_lnrow = sbuf("t_lnrow", [128, 1024], F32)
    stat_names = ["mu1", "nmu1", "rstd1", "sqm", "skm", "sqq", "skk", "sqk",
                  "ab", "ssc", "nmu2", "rstd2", "zz", "dest", "scr1", "scr2"]
    st = {nm: sbuf("st_" + nm, [128, 64], F32) for nm in stat_names}
    t_desti = sbuf("t_desti", [128, 64], I32)
    t_smalls = sbuf("t_smalls", [1, 64], F32)
    t_smbc = sbuf("t_smbc", [128, 64], F32)
    t_nmrow = sbuf("t_nmrow", [1, 512], F32R)
    t_onescf = sbuf("t_onescf", [128, 1], F32)
    t_onescr = sbuf("t_onescr", [128, 1], F32R)
    t_onesrf = sbuf("t_onesrf", [1, 128], F32)
    t_onesrr = sbuf("t_onesrr", [1, 128], F32R)
    t_onesall = sbuf("t_onesall", [128, 128], F32)
    t_tri = sbuf("t_tri", [128, 128], F32)
    t_ident = sbuf("t_ident", [128, 128], F32)
    t_slotid = sbuf("t_slotid", [128, 16], F32)
    t_macc = sbuf("t_macc", [1, 2048], F32)
    t_mrow = sbuf("t_mrow", [128, 8, 2], F32R)
    t_wbar = sbuf("t_wbar", [128, 8, 2], F32R)
    t_hrel = sbuf("t_hrel", [128, 2, 2], F32R)
    t_chT = sbuf("t_chT", [128, 8, 2], F32)
    t_wgbf = sbuf("t_wgbf", [128, 8, 1], BF16)
    t_odes = sbuf("t_odes", [32, 16], F32)
    t_vred = sbuf("t_vred", [128, 32], F32)
    t_sc1 = sbuf("t_sc1", [1, 32], F32)
    t_sc2 = sbuf("t_sc2", [1, 32], F32)
    t_lo = sbuf("t_lo", [1, 2], F32)
    t_hi = sbuf("t_hi", [1, 2], F32)
    t_tm = sbuf("t_tm", [1, 2], F32)
    t_cn = sbuf("t_cn", [1, 2], F32)
    t_pr2 = sbuf("t_pr2", [1, 2], U32)
    t_tbc = sbuf("t_tbc", [128, 2], F32)
    t_mask = sbuf("t_mask", [128, 64], F32)
    t_umask = sbuf("t_umask", [128, 64], U32)
    t_pcnt = sbuf("t_pcnt", [128, 2], F32)
    t_sa = sbuf("t_sa", [128, 64], F32)
    t_sb2 = sbuf("t_sb2", [128, 64], F32)
    t_bigt = sbuf("t_bigt", [128, 64], F32)
    t_negt = sbuf("t_negt", [128, 1], F32)
    t_poff = sbuf("t_poff", [128, 2], F32)
    t_cntb = sbuf("t_cntb", [128, 2], F32)
    t_kbias = sbuf("t_kbias", [128, 32], F32)
    t_dcol = sbuf("t_dcol", [128, 1], I32)
    t_dcol1 = sbuf("t_dcol1", [128, 1], I32)
    t_dcol2 = sbuf("t_dcol2", [128, 1], I32)
    t_tok64 = sbuf("t_tok64", [128, 64], F32)
    t_stok = sbuf("t_stok", [128, 32], I32)
    t_stokT = sbuf("t_stokT", [16, 128], F32)
    t_big32 = sbuf("t_big32", [32, 128], F32)
    t_gden = sbuf("t_gden", [128, 16], F32)
    t_srowD = sbuf("t_srowD", [1, 512], F32R)

    ps_a = es.enter_context(nc.psum_tensor("ps_a", [128, 2048], F32))
    ps_b = es.enter_context(nc.psum_tensor("ps_b", [128, 2048], F32))

    sems = {nm: es.enter_context(nc.semaphore(nm)) for nm in
            ["bar", "dms", "dmsc", "dmg", "tn", "ve", "ac", "gp"]}

    VEC = [None]

    def dma(eng, out, in_, waits=(), **kw):
        stream = {"sync": "dms", "scalar": "dmsc", "gpsimd": "dmg"}[eng]
        return P.emit(eng, lambda e: e.dma_start(out, in_, **kw), waits, inc=stream, k=16)

    def dmat(eng, out, in_, waits=()):
        stream = {"sync": "dms", "scalar": "dmsc"}[eng]
        return P.emit(eng, lambda e: e.dma_start(out, in_, transpose=True), waits, inc=stream, k=16)

    def mm(lfn, waits=()):
        return P.emit("tensor", lfn, waits, inc="tn")

    def act(lfn, waits=()):
        return P.emit("scalar", lfn, waits, inc="ac")

    def vec(lfn, waits=()):
        return P.emit("vector", lambda e: lfn(VEC[0]), waits, inc="ve")

    def gps(lfn, waits=()):
        return P.emit("gpsimd", lfn, waits, inc="gp")

    def tile_bt(i):
        return i // 8, (i % 8) * 512

    for o, i_ in [(t_tri[:], tri_in), (t_onescf[:], onescf_in), (t_onescr[:], onescr_in),
                  (t_onesrf[:], onesrf_in), (t_onesrr[:], onesrr_in),
                  (t_onesall[:], onesall_in), (t_ident[:], ident_in),
                  (t_slotid[:], slotid_in)]:
        dma("sync", o, i_)
    if SETUP_LIM >= 2:
        dma("sync", t_wqk[:], qkv_w[:, 0:2048].rearrange("(c p) o -> p c o", p=128))
        dma("sync", t_wv[:], qkv_w[:, 2048:3072].rearrange("(c p) o -> p c o", p=128))
    if SETUP_LIM < 3:
        _finish(nc, P, VEC, locals());  return _replay(nc, P, VEC, locals(), es, sems)
    dma("sync", t_smalls[:, 0:1], sm_in["c1"].rearrange("(o a) -> o a", o=1))
    dma("sync", t_smalls[:, 1:2], sm_in["c2"].rearrange("(o a) -> o a", o=1))
    dma("sync", t_smalls[:, 2:14], sm_in["conv1"].rearrange("a b c -> b (a c)"))
    dma("sync", t_smalls[:, 14:18], sm_in["gn_g"].rearrange("(o a) -> o a", o=1))
    dma("sync", t_smalls[:, 18:22], sm_in["gn_b"].rearrange("(o a) -> o a", o=1))
    dma("sync", t_smalls[:, 22:34], sm_in["conv2"].rearrange("a b c -> a (b c)"))
    dma("sync", t_smalls[:, 34:35], sm_in["conv2b"].rearrange("(o a) -> o a", o=1))
    if SETUP_LIM < 4:
        _finish(nc, P, VEC, locals());  return _replay(nc, P, VEC, locals(), es, sems)
    if PHASE_LIM > 5:
        dma("gpsimd", t_wgbf[:], wg_in.rearrange("(c p) o -> p c o", p=128),
            allow_slow_non_contiguous=True)
    P.barrier()

    vec(lambda v: v.memset(t_macc[:], 0.0))
    vec(lambda v: v.memset(t_negt[:], -1e9))
    P.barrier()
    if PHASE_LIM <= 0:
        _finish(nc, P, VEC, locals());  return _replay(nc, P, VEC, locals(), es, sems)

    e_tile_done = [()] * NT
    e_prev_stats = []
    for i in range(min(NT, A_TILES)):
        b, t0 = tile_bt(i)
        col = b * N + t0
        sc0 = b * 32 + (t0 // 128)
        bufw = list(e_tile_done[i - 2]) if i >= 2 else []
        e_xt = dma("sync", t_xt[i % 2][:], xT[:, col:col + 512].rearrange("(c p) t -> p c t", p=128), waits=bufw)
        e_xtm = dma("sync", t_xtm[i % 2][:], x_tm[col:col + 512, :].rearrange("(c p) d -> p c d", p=128), waits=bufw)
        xc = t_xt[i % 2]
        xtm = t_xtm[i % 2]
        e_sums = []
        for cl in range(4):
            sc = sc0 + cl
            act(lambda s, cl=cl, sc=sc, xtm=xtm: s.activation(
                t_qksc[:, 0:1024], xtm[:, cl, :], AF.Identity,
                accum_out=st["scr1"][:, sc:sc + 1]), waits=[e_xtm])
            e_sums.append(act(lambda s, cl=cl, sc=sc, xtm=xtm: s.activation(
                t_qksc[:, 0:1024], xtm[:, cl, :], AF.Square,
                accum_out=st["scr2"][:, sc:sc + 1]), waits=[e_xtm]))
        s4 = slice(sc0, sc0 + 4)
        ev = vec(lambda v, s4=s4: v.ts(st["mu1"][:, s4], st["scr1"][:, s4], 1.0 / D, None, OP.mult), waits=e_sums)
        vec(lambda v, s4=s4: v.ts(st["nmu1"][:, s4], st["mu1"][:, s4], -1.0, None, OP.mult))
        vec(lambda v, s4=s4: v.tt(st["scr1"][:, s4], st["mu1"][:, s4], st["mu1"][:, s4], OP.mult))
        vec(lambda v, s4=s4: v.ts(st["scr2"][:, s4], st["scr2"][:, s4], 1.0 / D, None, OP.mult))
        vec(lambda v, s4=s4: v.tt(st["scr2"][:, s4], st["scr2"][:, s4], st["scr1"][:, s4], OP.subtract))
        e_var = vec(lambda v, s4=s4: v.ts(st["scr2"][:, s4], st["scr2"][:, s4], EPS, None, OP.add))
        e_sq = act(lambda s, s4=s4: s.activation(st["scr1"][:, s4], st["scr2"][:, s4], AF.Sqrt), waits=[e_var])
        e_rs = vec(lambda v, s4=s4: v.recip(st["rstd1"][:, s4], st["scr1"][:, s4]), waits=[e_sq])
        if A_STAGE < 2:
            e_tile_done[i] = (e_rs, e_rs, e_sums[-1]);  e_prev_stats = [e_rs];  continue
        e_tp = None
        for r in range(4):
            e_tp = mm(lambda t, r=r, sc0=sc0: t.transpose(ps_a[0:1, 128 * r:128 * (r + 1)],
                                                          st["nmu1"][:, sc0 + r:sc0 + r + 1], t_ident[:]),
                      waits=([ev, e_rs] + e_prev_stats) if r == 0 else [])
        e_nr = vec(lambda v: v.copy(t_nmrow[:], ps_a[0:1, 0:512]), waits=[e_tp])
        e_bc = None
        for r in range(4):
            e_bc = mm(lambda t, r=r: t.matmul(ps_a[:, 512 + 128 * r:640 + 128 * r],
                                              t_onesrr[0:1, :], t_nmrow[0:1, 128 * r:128 * (r + 1)],
                                              start=True, stop=True), waits=[e_nr])
        e_xc = []
        for cc in range(8):
            e_xc.append(vec(lambda v, cc=cc, xc=xc: v.tt(xc[:, cc, :], xc[:, cc, :],
                                                         ps_a[:, 512:1024], OP.add),
                            waits=[e_bc, e_xt]))
        if A_STAGE < 3:
            e_tile_done[i] = (e_xc[-1], e_xc[-1], e_sums[-1]);  e_prev_stats = [e_xc[-1]];  continue
        if DUMP_XC:
            dma("gpsimd", x2_sp[col:col + 512, :].rearrange("(c p) d -> p c d", p=128),
                xc[:, 0:8, :], waits=e_xc)
        e_qkstat = []
        for cl in range(4):
            sc = sc0 + cl
            par = cl % 2
            pso = ps_a if par == 0 else ps_b
            q_ps = pso[:, 0:1024]
            k_ps = pso[:, 1024:2048]
            w_pre = list(e_xc)
            if cl >= 2:
                w_pre = [e_qkstat[cl - 2][-1], e_qkstat[cl - 2][-2]]
            e_mm = None
            for cc in range(8):
                lhs = (lambda xc=xc, cc=cc, cl=cl: xc[:, cc, 128 * cl:128 * (cl + 1)])()
                mm(lambda t, lhs=lhs, cc=cc, q_ps=q_ps: t.matmul(
                    q_ps[:, 0:512], lhs, t_wqk[:, cc, 0:512],
                    start=(cc == 0), stop=(cc == 7)), waits=w_pre if cc == 0 else [])
                mm(lambda t, lhs=lhs, cc=cc, q_ps=q_ps: t.matmul(
                    q_ps[:, 512:1024], lhs, t_wqk[:, cc, 512:1024],
                    start=(cc == 0), stop=(cc == 7)))
                mm(lambda t, lhs=lhs, cc=cc, k_ps=k_ps: t.matmul(
                    k_ps[:, 0:512], lhs, t_wqk[:, cc, 1024:1536],
                    start=(cc == 0), stop=(cc == 7)))
                e_mm = mm(lambda t, lhs=lhs, cc=cc, k_ps=k_ps: t.matmul(
                    k_ps[:, 512:1024], lhs, t_wqk[:, cc, 1536:2048],
                    start=(cc == 0), stop=(cc == 7)))
            if A_STAGE >= 31:
                e1 = act(lambda s, q_ps=q_ps, sc=sc: s.activation(
                    t_qksc[:, 0:1024], q_ps, AF.Identity, accum_out=st["sqm"][:, sc:sc + 1]), waits=[e_mm])
                if DUMP_XC and i == 0 and cl == 0:
                    e_dq = dma("gpsimd", ffn_sp[0:128, :], t_qksc[:, 0:1024], waits=[e1])
                    e_dw = dma("gpsimd", ffn_sp[128:256, :], t_wqk[:, 0, 0:1024])
                    e_dw2 = dma("gpsimd", ffn_sp[256:384, :], t_wqk[:, 7, 0:1024])
                    e1 = e_dq
                e2 = act(lambda s, k_ps=k_ps, sc=sc: s.activation(
                    t_qksc[:, 0:1024], k_ps, AF.Identity, accum_out=st["skm"][:, sc:sc + 1]))
            else:
                e1 = e2 = e_mm
            if A_STAGE >= 32:
                e3 = act(lambda s, q_ps=q_ps, sc=sc: s.activation(
                    t_qksc[:, 0:1024], q_ps, AF.Square, accum_out=st["sqq"][:, sc:sc + 1]), waits=[e_mm])
                e4 = act(lambda s, k_ps=k_ps, sc=sc: s.activation(
                    t_qksc[:, 0:1024], k_ps, AF.Square, accum_out=st["skk"][:, sc:sc + 1]))
            else:
                e3 = e4 = e_mm
            if A_STAGE >= 33:
                e5a = vec(lambda v, q_ps=q_ps: v.copy(t_qksc2[:, 0:1024], q_ps), waits=[e_mm, e4])
                e5 = vec(lambda v, k_ps=k_ps, sc=sc: v.stt(
                    t_qksc2[:, 0:1024], t_qksc2[:, 0:1024], 1.0, k_ps, OP.mult, OP.mult,
                    accum_out=st["sqk"][:, sc:sc + 1]), waits=[e5a])
            else:
                e5 = e_mm
            e_qkstat.append([e1, e2, e3, e4, e5])
        if A_STAGE < 4 or (30 <= A_STAGE < 40):
            e_tile_done[i] = (e_qkstat[3][4], e_qkstat[3][4], e_qkstat[3][3])
            e_prev_stats = e_qkstat[2] + e_qkstat[3];  continue
        e_vev = [None, None]
        e_vdma = []
        for vc in range(8):
            vps = ps_b[:, 512 * (vc % 2):512 * (vc % 2) + 512]
            if vc < 2:
                w_pre = e_qkstat[3]
            else:
                w_pre = [e_vev[vc % 2]]
            e_mm = None
            for cc in range(8):
                e_mm = mm(lambda t, vc=vc, cc=cc, vps=vps, xc=xc: t.matmul(
                    vps, t_wv[:, cc, 128 * vc:128 * (vc + 1)], xc[:, cc, :],
                    start=(cc == 0), stop=(cc == 7)), waits=w_pre if cc == 0 else [])
            e_ev = vec(lambda v, vc=vc, vps=vps: v.copy(t_vout[:, vc % 2, :], vps), waits=[e_mm])
            if DUMP_XC and i == 0 and vc == 0:
                e_ev = dma("gpsimd", ffn_sp[384:512, 0:512], t_vout[:, 0, :], waits=[e_ev])
            e_vev[vc % 2] = e_ev
            e_vdma.append(dma("scalar", vT_sp[128 * vc:128 * (vc + 1), col:col + 512],
                              t_vout[:, vc % 2, :], waits=[e_ev]))
        if A_STAGE < 5:
            e_tile_done[i] = (e_vev[0], e_vdma[-1], e_qkstat[3][3])
            e_prev_stats = e_qkstat[2] + e_qkstat[3] + [e_vev[0], e_vev[1]];  continue
        e_mps = None
        for cl in range(4):
            sc = sc0 + cl
            exm = vec(lambda v, cl=cl, sc=sc, xtm=xtm: v.ts(
                t_xm[:, 0:1024], xtm[:, cl, :], st["nmu1"][:, sc:sc + 1],
                st["rstd1"][:, sc:sc + 1], OP.add, OP.mult),
                waits=[e_vev[0], e_vev[1], e_rs] if cl == 0 else [e_mps])
            for half in range(2):
                e_mps = mm(lambda t, half=half, cl=cl: t.matmul(
                    ps_b[0:1, 1024 + 512 * half:1024 + 512 * (half + 1)], t_onescr[:],
                    t_xm[:, 512 * half:512 * (half + 1)],
                    start=(cl == 0), stop=(cl == 3)), waits=[exm])
        e_mac = vec(lambda v, b=b: v.tt(t_macc[0:1, 1024 * b:1024 * (b + 1)],
                                        t_macc[0:1, 1024 * b:1024 * (b + 1)],
                                        ps_b[0:1, 1024:2048], OP.add), waits=[e_mps])
        e_tile_done[i] = (e_mac, e_vdma[-1], e_qkstat[3][3])
        e_prev_stats = e_qkstat[2] + e_qkstat[3] + [e_mac]
        P.barrier()
    if PHASE_LIM <= 1:
        _finish(nc, P, VEC, locals());  return _replay(nc, P, VEC, locals(), es, sems)

    dma("sync", t_proj[:], proj_w.rearrange("(c p) o -> p c o", p=128))
    dma("sync", t_chw1[:], ch_w1.rearrange("(c p) o -> p c o", p=128))
    dma("sync", t_chw2[:], ch_w2.rearrange("(c p) o -> p c o", p=128))
    e_wgr = dma("sync", t_wgrow[:], wg_in.rearrange("a b -> b a"))
    e_pbr = dma("sync", t_bias_r[:], proj_b.rearrange("(o a) -> o a", o=1))
    e_swg = vec(lambda v: v.red(t_sc1[0:1, 0:1], t_wgrow[0:1, :], OP.add), waits=[e_wgr])
    e_swg2 = vec(lambda v: v.copy(t_smalls[0:1, 35:36], t_sc1[0:1, 0:1]), waits=[e_swg])
    e1 = mm(lambda t: t.matmul(ps_b[:, 0:64], t_onesrf[0:1, :], t_smalls[0:1, :], start=True, stop=True), waits=[e_swg2])
    vec(lambda v: v.copy(t_smbc[:], ps_b[:, 0:64]), waits=[e1])
    e1 = mm(lambda t: t.matmul(ps_b[:, 512:1024], t_onesrf[0:1, :], t_bias_r[0:1, 0:512], start=True, stop=True), waits=[e_pbr])
    e2 = mm(lambda t: t.matmul(ps_b[:, 1024:1536], t_onesrf[0:1, :], t_bias_r[0:1, 512:1024], start=True, stop=True))
    vec(lambda v: v.copy(t_biasbc[:], ps_b[:, 512:1536]), waits=[e1, e2])
    e1 = mm(lambda t: t.matmul(ps_b[:, 1536:2048], t_onesrf[0:1, :], t_wgrow[0:1, 0:512], start=True, stop=True))
    vec(lambda v: v.copy(t_wgbc[:, 0:512], ps_b[:, 1536:2048]), waits=[e1])
    e2 = mm(lambda t: t.matmul(ps_a[:, 0:512], t_onesrf[0:1, :], t_wgrow[0:1, 512:1024], start=True, stop=True))
    vec(lambda v: v.copy(t_wgbc[:, 512:1024], ps_a[:, 0:512]), waits=[e2])
    P.barrier()

    A = st
    sm = lambda k_: t_smbc[:, k_:k_ + 1]
    vec(lambda v: v.tt(A["scr1"][:], A["rstd1"][:], A["rstd1"][:], OP.mult))
    vec(lambda v: v.ts(A["sqm"][:], A["sqm"][:], 1.0 / D, None, OP.mult))
    vec(lambda v: v.ts(A["skm"][:], A["skm"][:], 1.0 / D, None, OP.mult))
    vec(lambda v: v.tt(A["sqm"][:], A["sqm"][:], A["rstd1"][:], OP.mult))
    vec(lambda v: v.tt(A["skm"][:], A["skm"][:], A["rstd1"][:], OP.mult))
    vec(lambda v: v.tt(A["sqq"][:], A["sqq"][:], A["scr1"][:], OP.mult))
    vec(lambda v: v.tt(A["skk"][:], A["skk"][:], A["scr1"][:], OP.mult))
    vec(lambda v: v.tt(A["sqk"][:], A["sqk"][:], A["scr1"][:], OP.mult))
    vec(lambda v: v.tt(A["scr1"][:], A["sqm"][:], A["sqm"][:], OP.mult))
    vec(lambda v: v.tt(A["scr2"][:], A["skm"][:], A["skm"][:], OP.mult))
    vec(lambda v: v.stt(A["sqq"][:], A["scr1"][:], -float(D), A["sqq"][:], OP.mult, OP.add))
    vec(lambda v: v.stt(A["skk"][:], A["scr2"][:], -float(D), A["skk"][:], OP.mult, OP.add))
    vec(lambda v: v.tt(t_sa[:], A["sqm"][:], A["skm"][:], OP.mult))
    vec(lambda v: v.stt(A["sqk"][:], t_sa[:], -float(D), A["sqk"][:], OP.mult, OP.add))
    vec(lambda v: v.ts(t_sb2[:], t_sa[:], 2.0, None, OP.mult))
    vec(lambda v: v.ts(t_sb2[:], t_sb2[:], sm(0), None, OP.add))
    vec(lambda v: v.ts(A["sqk"][:], A["sqk"][:], 2.0 * INVCM1, None, OP.mult))
    vec(lambda v: v.ts(A["sqk"][:], A["sqk"][:], sm(1), None, OP.add))
    vec(lambda v: v.tt(t_sb2[:], t_sb2[:], A["sqk"][:], OP.mult))
    vec(lambda v: v.tt(A["scr1"][:], A["scr1"][:], A["scr2"][:], OP.add))
    vec(lambda v: v.ts(A["scr1"][:], A["scr1"][:], sm(0), None, OP.add))
    vec(lambda v: v.tt(A["sqq"][:], A["sqq"][:], A["skk"][:], OP.add))
    vec(lambda v: v.ts(A["sqq"][:], A["sqq"][:], INVCM1, None, OP.mult))
    vec(lambda v: v.ts(A["sqq"][:], A["sqq"][:], sm(1), None, OP.add))
    vec(lambda v: v.tt(A["scr1"][:], A["scr1"][:], A["sqq"][:], OP.mult))
    vec(lambda v: v.ts(A["scr1"][:], A["scr1"][:], 1e-7, None, OP.add))
    e_r = vec(lambda v: v.recip(A["scr2"][:], A["scr1"][:]))
    vec(lambda v: v.tt(t_sb2[:], t_sb2[:], A["scr2"][:], OP.mult))
    e_ab = vec(lambda v: v.tt(A["ab"][:], t_sb2[:], t_sb2[:], OP.mult))

    e_wsmT = []
    for b in range(NB):
        bs = slice(32 * b, 32 * b + 32)
        e_t = mm(lambda t, bs=bs: t.transpose(ps_a[0:32, 0:128], A["ab"][:, bs], t_ident[:]),
                 waits=[e_ab] + (e_wsmT[-1:] if e_wsmT else []))
        e_c = vec(lambda v: v.copy(t_ode["a0"][:], ps_a[0:32, 0:128]), waits=[e_t])
        vec(lambda v: v.memset(t_ode["tmp"][:, 0:1], 0.0))
        e_z1 = vec(lambda v: v.copy(t_ode["tmp"][0:1, 1:2], t_ode["tmp"][0:1, 0:1]))
        e_sp = dma("scalar", ab_pad[b, 64:64 + N].rearrange("(c p) -> c p", p=128), t_ode["a0"][:], waits=[e_c])
        e_zl = dma("scalar", ab_pad[b, 63:64].rearrange("(o a) -> o a", o=1), t_ode["tmp"][0:1, 0:1], waits=[e_z1])
        e_zr = dma("scalar", ab_pad[b, 64 + N:65 + N].rearrange("(o a) -> o a", o=1), t_ode["tmp"][0:1, 0:1], waits=[e_z1])
        e_m1 = dma("sync", t_ode["am1"][:], ab_pad[b, 63:63 + N].rearrange("(c p) -> c p", p=128), waits=[e_sp, e_zl, e_zr])
        e_p1 = dma("sync", t_ode["ap1"][:], ab_pad[b, 65:65 + N].rearrange("(c p) -> c p", p=128), waits=[e_sp, e_zl, e_zr])
        e_h = []
        for o in range(4):
            hh = t_ode[f"hh{o}"]
            vec(lambda v, hh=hh, o=o: v.ts(hh[:], t_ode["am1"][:], t_smbc[0:32, 2 + 3 * o:3 + 3 * o], None, OP.mult), waits=[e_m1, e_p1])
            vec(lambda v, hh=hh, o=o: v.stt(hh[:], t_ode["a0"][:], t_smbc[0:32, 3 + 3 * o:4 + 3 * o], hh[:], OP.mult, OP.add))
            e_h.append(vec(lambda v, hh=hh, o=o: v.stt(hh[:], t_ode["ap1"][:], t_smbc[0:32, 4 + 3 * o:5 + 3 * o], hh[:], OP.mult, OP.add)))
        e_gs = []
        for o in range(4):
            e1_ = vec(lambda v, o=o: v.red(t_vred[0:32, 18 + o:19 + o], t_ode[f"hh{o}"][:], OP.add), waits=[e_h[o]])
            e2_ = act(lambda s, o=o: s.activation(t_ode["tmp"][:], t_ode[f"hh{o}"][:], AF.Square,
                                                  accum_out=t_vred[0:32, 22 + o:23 + o]), waits=[e_h[o]])
            e_gs += [e1_, e2_]
        e_gm = mm(lambda t: t.matmul(ps_a[0:1, 0:8], t_onescf[0:32, :], t_vred[0:32, 18:26], start=True, stop=True), waits=e_gs)
        e_cp = vec(lambda v: v.copy(t_sc1[0:1, 8:16], ps_a[0:1, 0:8]), waits=[e_gm])
        vec(lambda v: v.tt(t_sc1[0:1, 16:17], t_sc1[0:1, 8:9], t_sc1[0:1, 9:10], OP.add))
        vec(lambda v: v.tt(t_sc1[0:1, 17:18], t_sc1[0:1, 10:11], t_sc1[0:1, 11:12], OP.add))
        vec(lambda v: v.tt(t_sc1[0:1, 18:19], t_sc1[0:1, 12:13], t_sc1[0:1, 13:14], OP.add))
        vec(lambda v: v.tt(t_sc1[0:1, 19:20], t_sc1[0:1, 14:15], t_sc1[0:1, 15:16], OP.add))
        vec(lambda v: v.ts(t_sc1[0:1, 16:20], t_sc1[0:1, 16:20], 1.0 / (2 * N), None, OP.mult))
        vec(lambda v: v.tt(t_sc2[0:1, 0:2], t_sc1[0:1, 16:18], t_sc1[0:1, 16:18], OP.mult))
        vec(lambda v: v.tt(t_sc2[0:1, 0:2], t_sc1[0:1, 18:20], t_sc2[0:1, 0:2], OP.subtract))
        e_v2 = vec(lambda v: v.ts(t_sc2[0:1, 0:2], t_sc2[0:1, 0:2], EPS, None, OP.add))
        e_sqg = act(lambda s: s.activation(t_sc2[0:1, 2:4], t_sc2[0:1, 0:2], AF.Sqrt), waits=[e_v2])
        e_rg = vec(lambda v: v.recip(t_sc2[0:1, 4:6], t_sc2[0:1, 2:4]), waits=[e_sqg])
        for ch_ in range(4):
            g_ = ch_ // 2
            vec(lambda v, ch_=ch_, g_=g_: v.tt(t_sc2[0:1, 8 + ch_:9 + ch_], t_sc2[0:1, 4 + g_:5 + g_], t_smalls[0:1, 14 + ch_:15 + ch_], OP.mult))
            vec(lambda v, ch_=ch_, g_=g_: v.tt(t_sc2[0:1, 12 + ch_:13 + ch_], t_sc2[0:1, 8 + ch_:9 + ch_], t_sc1[0:1, 16 + g_:17 + g_], OP.mult))
            vec(lambda v, ch_=ch_: v.tt(t_sc2[0:1, 12 + ch_:13 + ch_], t_smalls[0:1, 18 + ch_:19 + ch_], t_sc2[0:1, 12 + ch_:13 + ch_], OP.subtract))
        e_ab8 = vec(lambda v: v.copy(t_sc2[0:1, 16:24], t_sc2[0:1, 8:16]))
        e_bc8 = mm(lambda t: t.matmul(ps_a[0:32, 16:24], t_onesrf[0:1, 0:32], t_sc2[0:1, 16:24], start=True, stop=True), waits=[e_ab8])
        e_od = vec(lambda v: v.copy(t_odes[:, 0:8], ps_a[0:32, 16:24]), waits=[e_bc8])
        e_hr = []
        for o in range(4):
            e_hr.append(act(lambda s, o=o: s.activation(
                t_ode[f"hh{o}"][:], t_ode[f"hh{o}"][:], AF.Relu,
                bias=t_odes[:, 4 + o:5 + o], scale=t_odes[:, o:o + 1]), waits=[e_od]))
        e_yy = vec(lambda v: v.memset(t_ode["yy"][:], 0.0))
        for o in range(4):
            e_sp2 = dma("scalar", hh_pad[b, o, 64:64 + N].rearrange("(c p) -> c p", p=128), t_ode[f"hh{o}"][:], waits=[e_hr[o]])
            dma("scalar", hh_pad[b, o, 63:64].rearrange("(o a) -> o a", o=1), t_ode["tmp"][0:1, 0:1])
            e_zz2 = dma("scalar", hh_pad[b, o, 64 + N:65 + N].rearrange("(o a) -> o a", o=1), t_ode["tmp"][0:1, 0:1])
            for tap, off in [(0, 63), (1, 64), (2, 65)]:
                e_ld = dma("sync", t_ode["am1"][:] if tap == 0 else (t_ode["a0"][:] if tap == 1 else t_ode["ap1"][:]),
                           hh_pad[b, o, off:off + N].rearrange("(c p) -> c p", p=128), waits=[e_sp2, e_zz2, e_yy])
                src = {0: "am1", 1: "a0", 2: "ap1"}[tap]
                e_yy = vec(lambda v, src=src, o=o, tap=tap: v.stt(
                    t_ode["yy"][:], t_ode[src][:], t_smbc[0:32, 22 + 3 * o + tap:23 + 3 * o + tap],
                    t_ode["yy"][:], OP.mult, OP.add), waits=[e_ld, e_yy])
        vec(lambda v: v.ts(t_ode["yy"][:], t_ode["yy"][:], t_smbc[0:32, 34:35], None, OP.add), waits=[e_yy])
        e_t2 = mm(lambda t, bs=bs: t.transpose(ps_a[0:32, 32:160], A["ab"][:, bs], t_ident[:]), waits=[e_od])
        e_c2 = vec(lambda v: v.tt(t_ode["yy"][:], t_ode["yy"][:], ps_a[0:32, 32:160], OP.add), waits=[e_t2])
        if DUMP_XC and b == 0:
            e_c2 = dma("gpsimd", ffn_sp[512:544, 0:128], t_ode["yy"][:], waits=[e_c2])
            e_hh = dma("gpsimd", ffn_sp[544:576, 0:128], t_ode["hh0"][:], waits=[e_c2])
        e_fa = act(lambda s: s.activation(t_ode["fa"][:], t_ode["yy"][:], AF.Sigmoid), waits=[e_c2])
        e_ex = act(lambda s: s.activation(t_ode["tmp"][:], t_ode["fa"][:], AF.Exp), waits=[e_fa])
        e_rd = vec(lambda v: v.red(t_vred[0:32, 26:27], t_ode["tmp"][:], OP.add), waits=[e_ex])
        e_sm = mm(lambda t: t.matmul(ps_a[0:1, 24:25], t_onescf[0:32, :], t_vred[0:32, 26:27], start=True, stop=True), waits=[e_rd])
        e_rc = vec(lambda v: v.copy(t_sc1[0:1, 20:21], ps_a[0:1, 24:25]), waits=[e_sm])
        e_rc2 = vec(lambda v: v.recip(t_sc1[0:1, 21:22], t_sc1[0:1, 20:21]), waits=[e_rc])
        e_bc2 = mm(lambda t: t.matmul(ps_a[0:32, 25:26], t_onesrf[0:1, 0:32], t_sc1[0:1, 21:22], start=True, stop=True), waits=[e_rc2])
        e_cb2 = vec(lambda v: v.copy(t_vred[0:32, 27:28], ps_a[0:32, 25:26]), waits=[e_bc2])
        e_w = vec(lambda v: v.ts(t_ode["fa"][:], t_ode["tmp"][:], t_vred[0:32, 27:28], None, OP.mult), waits=[e_cb2])
        e_t3 = mm(lambda t: t.transpose(ps_b[0:128, 0:32], t_ode["fa"][:], t_ident[0:32, 0:32]), waits=[e_w])
        e_wT = vec(lambda v, bs=bs: v.copy(st["ssc"][:, bs], ps_b[0:128, 0:32]), waits=[e_t3])
        e_wsmT.append(e_wT)
    e_ssc = vec(lambda v: v.tt(st["ssc"][:], st["ssc"][:], st["rstd1"][:], OP.mult), waits=e_wsmT)
    P.barrier()
    if PHASE_LIM <= 2:
        _finish(nc, P, VEC, locals());  return _replay(nc, P, VEC, locals(), es, sems)

    vec(lambda v: v.ts(t_macc[:], t_macc[:], 1.0 / N, None, OP.mult))
    e_mT = []
    for b in range(NB):
        for cc in range(8):
            e_t = mm(lambda t, b=b, cc=cc: t.transpose(
                ps_a[0:128, 0:1], t_macc[0:1, 1024 * b + 128 * cc:1024 * b + 128 * (cc + 1)], t_ident[0:1, 0:1]),
                waits=e_mT[-1:])
            e_mT.append(vec(lambda v, b=b, cc=cc: v.copy(t_mrow[:, cc, b:b + 1], ps_a[0:128, 0:1]), waits=[e_t]))
    e_b1 = dma("sync", t_vred[:, 18:20], ch_b1.rearrange("(c p) -> p c", p=128), allow_slow_non_contiguous=True)
    e_b2 = dma("sync", t_vred[:, 20:28], ch_b2.rearrange("(c p) -> p c", p=128), allow_slow_non_contiguous=True)
    for hc in range(2):
        e_mm2 = None
        for cc in range(8):
            e_mm2 = mm(lambda t, hc=hc, cc=cc: t.matmul(
                ps_a[0:128, 4 + 2 * hc:6 + 2 * hc], t_chw1[:, cc, 128 * hc:128 * (hc + 1)],
                t_mrow[:, cc, :], start=(cc == 0), stop=(cc == 7)), waits=e_mT + [e_b1])
        act(lambda s, hc=hc: s.activation(t_hrel[:, hc, :], ps_a[0:128, 4 + 2 * hc:6 + 2 * hc],
                                          AF.Relu, bias=t_vred[:, 18 + hc:19 + hc]), waits=[e_mm2])
    e_hr2 = P.emit("scalar", lambda s: s.nop(), inc="ac")
    for oc in range(8):
        e_mm3 = None
        for hc in range(2):
            e_mm3 = mm(lambda t, oc=oc, hc=hc: t.matmul(
                ps_a[0:128, 8 + 2 * oc:10 + 2 * oc], t_chw2[:, hc, 128 * oc:128 * (oc + 1)],
                t_hrel[:, hc, :], start=(hc == 0), stop=(hc == 1)), waits=[e_hr2, e_b2])
        act(lambda s, oc=oc: s.activation(t_chT[:, oc, :], ps_a[0:128, 8 + 2 * oc:10 + 2 * oc],
                                          AF.Sigmoid, bias=t_vred[:, 20 + oc:21 + oc]), waits=[e_mm3])
    P.barrier()
    if PHASE_LIM <= 3:
        _finish(nc, P, VEC, locals());  return _replay(nc, P, VEC, locals(), es, sems)

    e_tile_done2 = [()] * NT
    prev_attn_rd = [[None] * 4, [None] * 4]
    prev_x2_rd = [None] * 4
    prev_ln2_dma = [None]
    prev_x2t = [None]
    dstate = [None] * NT
    x2_last = [None]
    rs2_last = [None]

    def d_stage1(i):
        b, t0 = tile_bt(i)
        col = b * N + t0
        sc0 = b * 32 + (t0 // 128)
        s4 = slice(sc0, sc0 + 4)
        tp = i % 2
        ablk = t_attnblk if tp == 0 else t_attnblkB
        bufw = list(e_tile_done2[i - 2]) if i >= 2 else []
        e_vt = dma("sync", t_xt[i % 2][:], vT_sp[:, col:col + 512].rearrange("(c p) t -> p c t", p=128), waits=bufw)
        e_xtm = dma("sync", t_xtm[i % 2][:], x_tm[col:col + 512, :].rearrange("(c p) d -> p c d", p=128), waits=bufw)
        vt = t_xt[i % 2]
        xtm = t_xtm[i % 2]
        e_ol = []
        for cc in range(8):
            e_ol.append(vec(lambda v, cc=cc, vt=vt, b=b: v.ts(
                vt[:, cc, :], vt[:, cc, :], t_chT[:, cc, b:b + 1], None, OP.mult),
                waits=[e_vt]))
        e_pr = []
        for cl in range(4):
            par = cl % 2
            pps = ps_a[:, 1024:2048] if par == 0 else ps_b[:, 0:1024]
            w_pre = list(e_ol)
            if cl >= 2:
                w_pre += [e_pr[cl - 2]]
            e_mm4 = None
            for cc in range(8):
                for ot in range(2):
                    e_mm4 = mm(lambda t, cc=cc, cl=cl, ot=ot, pps=pps, vt=vt: t.matmul(
                        pps[:, 512 * ot:512 * (ot + 1)], vt[:, cc, 128 * cl:128 * (cl + 1)],
                        t_proj[:, cc, 512 * ot:512 * (ot + 1)],
                        start=(cc == 0), stop=(cc == 7)), waits=w_pre if (cc == 0 and ot == 0) else [])
            sc = sc0 + cl
            e_ev = vec(lambda v, cl=cl, sc=sc, pps=pps, ablk=ablk: v.stt(
                ablk[:, cl, :], pps, st["ssc"][:, sc:sc + 1], t_biasbc[:],
                OP.mult, OP.add, accum_out=st["scr1"][:, sc:sc + 1]),
                waits=[e_mm4, prev_attn_rd[tp][cl]])
            e_pr.append(e_ev)
            eb = act(lambda s, cl=cl, sc=sc, ablk=ablk: s.activation(
                t_qksc[:, 0:1024], ablk[:, cl, :], AF.Square,
                accum_out=st["scr2"][:, sc:sc + 1]), waits=[e_ev])
            e_pr[-1] = eb
        dstate[i] = dict(b=b, col=col, sc0=sc0, s4=s4, tp=tp, ablk=ablk,
                         xtm=xtm, e_xtm=e_xtm, e_pr=e_pr)

    def d_stage2(i):
        ds = dstate[i]
        col, sc0, s4, tp, ablk = ds["col"], ds["sc0"], ds["s4"], ds["tp"], ds["ablk"]
        xtm, e_xtm, e_pr = ds["xtm"], ds["e_xtm"], ds["e_pr"]
        ev = vec(lambda v, s4=s4: v.ts(st["scr1"][:, s4], st["scr1"][:, s4], 1.0 / D, None, OP.mult), waits=e_pr)
        vec(lambda v, s4=s4: v.ts(st["scr2"][:, s4], st["scr2"][:, s4], 1.0 / D, None, OP.mult))
        vec(lambda v, s4=s4: v.tt(t_sa[:, s4], st["scr1"][:, s4], st["scr1"][:, s4], OP.mult))
        vec(lambda v, s4=s4: v.tt(st["scr2"][:, s4], st["scr2"][:, s4], t_sa[:, s4], OP.subtract))
        e_var = vec(lambda v, s4=s4: v.ts(st["scr2"][:, s4], st["scr2"][:, s4], EPS, None, OP.add))
        e_sq = act(lambda s, s4=s4: s.activation(t_sb2[:, s4], st["scr2"][:, s4], AF.Sqrt), waits=[e_var])
        e_rs = vec(lambda v, s4=s4: v.recip(t_sb2[:, s4], t_sb2[:, s4]), waits=[e_sq])
        e_nm = vec(lambda v, s4=s4: v.ts(st["scr1"][:, s4], st["scr1"][:, s4], -1.0, None, OP.mult), waits=[e_rs])
        e_nb2 = vec(lambda v, s4=s4: v.tt(st["scr1"][:, s4], st["scr1"][:, s4], t_sb2[:, s4], OP.mult))
        e_ch_done = []
        for cl in range(4):
            sc = sc0 + cl
            e_l = act(lambda s, cl=cl, sc=sc, ablk=ablk: s.activation(
                t_x2t[:, 0:1024], ablk[:, cl, :], AF.Identity,
                bias=st["scr1"][:, sc:sc + 1], scale=t_sb2[:, sc:sc + 1]),
                waits=[e_nb2, prev_x2t[0]])
            prev_attn_rd[tp][cl] = e_l
            e_x2 = vec(lambda v, cl=cl, sc=sc, xtm=xtm: v.stt(
                t_x2blk[:, cl, :], t_x2t[:, 0:1024], 1.0, xtm[:, cl, :],
                OP.mult, OP.add, accum_out=st["sqm"][:, sc:sc + 1]),
                waits=[e_xtm, prev_x2_rd[cl], e_l])
            prev_x2t[0] = e_x2
            e_st = dma("scalar", out_t[col + 128 * cl:col + 128 * (cl + 1), :], t_x2blk[:, cl, :], waits=[e_x2])
            x2_last[0] = e_st
            if DBG:
                dma("scalar", x2_sp[col + 128 * cl:col + 128 * (cl + 1), :], t_x2blk[:, cl, :], waits=[e_x2])
            eb = act(lambda s, cl=cl, sc=sc: s.activation(
                t_qksc[:, 0:1024], t_x2blk[:, cl, :], AF.Square,
                accum_out=st["skm"][:, sc:sc + 1]), waits=[e_x2])
            e_ch_done.append((e_st, eb))
        e_prs = [tup[1] for tup in e_ch_done]
        ev = vec(lambda v, s4=s4: v.ts(st["sqm"][:, s4], st["sqm"][:, s4], 1.0 / D, None, OP.mult), waits=e_prs)
        vec(lambda v, s4=s4: v.ts(st["nmu2"][:, s4], st["sqm"][:, s4], -1.0, None, OP.mult))
        vec(lambda v, s4=s4: v.tt(t_sa[:, s4], st["sqm"][:, s4], st["sqm"][:, s4], OP.mult))
        vec(lambda v, s4=s4: v.ts(st["skm"][:, s4], st["skm"][:, s4], 1.0 / D, None, OP.mult))
        vec(lambda v, s4=s4: v.tt(st["skm"][:, s4], st["skm"][:, s4], t_sa[:, s4], OP.subtract))
        e_var2 = vec(lambda v, s4=s4: v.ts(st["skm"][:, s4], st["skm"][:, s4], EPS, None, OP.add))
        e_sq2 = act(lambda s, s4=s4: s.activation(st["rstd2"][:, s4], st["skm"][:, s4], AF.Sqrt), waits=[e_var2])
        e_rs2 = vec(lambda v, s4=s4: v.recip(st["rstd2"][:, s4], st["rstd2"][:, s4]), waits=[e_sq2])
        rs2_last[0] = e_rs2
        e_zi = []
        for cl in range(4):
            sc = sc0 + cl
            e_l2 = vec(lambda v, cl=cl, sc=sc: v.ts(
                t_ln2t[:, 0:1024], t_x2blk[:, cl, :], st["nmu2"][:, sc:sc + 1],
                st["rstd2"][:, sc:sc + 1], OP.add, OP.mult), waits=[e_rs2, prev_ln2_dma[0]])
            e_sl = dma("scalar", ln2_sp[col + 128 * cl:col + 128 * (cl + 1), :], t_ln2t[:, 0:1024], waits=[e_l2])
            prev_ln2_dma[0] = e_sl
            e_zr = vec(lambda v, cl=cl, sc=sc: v.stt(
                t_zscr[:, 0:1024], t_x2blk[:, cl, :], 1.0, t_wgbc[:], OP.mult, OP.mult,
                accum_out=st["zz"][:, sc:sc + 1]))
            prev_x2_rd[cl] = e_zr
            e_zi.append((e_sl, e_zr))
        e_tile_done2[i] = (e_zi[3][0], e_zi[3][1], e_ch_done[3][0])

    for i in range(NT):
        d_stage1(i)
        if i >= 1:
            d_stage2(i - 1)
    d_stage2(NT - 1)
    e_rs2 = rs2_last[0]
    e_x2_last = x2_last[0]
    e_z1 = vec(lambda v: v.ts(t_sa[:], st["nmu2"][:], t_smbc[:, 35:36], None, OP.mult))
    e_z2 = vec(lambda v: v.tt(st["zz"][:], st["zz"][:], t_sa[:], OP.add), waits=[e_z1])
    e_z3 = vec(lambda v: v.tt(st["zz"][:], st["zz"][:], st["rstd2"][:], OP.mult), waits=[e_rs2])
    P.barrier()
    if PHASE_LIM <= 4:
        _finish(nc, P, VEC, locals());  return _replay(nc, P, VEC, locals(), es, sems)

    vec(lambda v: v.memset(t_lo[:], -64.0))
    vec(lambda v: v.memset(t_hi[:], 64.0))
    vec(lambda v: v.memset(t_tm[:], 0.0))
    vec(lambda v: v.memset(t_tbc[:], 0.0))
    e_big = vec(lambda v: v.memset(t_bigt[:], BIGF))
    e_zz = vec(lambda v: v.memset(t_zeroE[:], 0.0))
    for r in range(ROUNDS):
        e_pc = None
        for b in range(NB):
            bs = slice(32 * b, 32 * b + 32)
            vec(lambda v, b=b, bs=bs: v.ts(t_mask[:, bs], st["zz"][:, bs], t_tbc[:, b:b + 1], None, OP.is_gt))
            e_pc = vec(lambda v, b=b, bs=bs: v.red(t_pcnt[:, b:b + 1], t_mask[:, bs], OP.add))
        e_cm = mm(lambda t: t.matmul(ps_a[0:1, 0:2], t_onescf[:], t_pcnt[:], start=True, stop=True), waits=[e_pc])
        e_cc = vec(lambda v: v.copy(t_cn[:], ps_a[0:1, 0:2]), waits=[e_cm])
        vec(lambda v: v.ts(t_pr2[:], t_cn[:], float(C), None, OP.is_ge))
        vec(lambda v: v.cpred(t_lo[:], t_pr2[:], t_tm[:]))
        vec(lambda v: v.ts(t_pr2[:], t_cn[:], float(C), None, OP.is_lt))
        vec(lambda v: v.cpred(t_hi[:], t_pr2[:], t_tm[:]))
        vec(lambda v: v.tt(t_tm[:], t_lo[:], t_hi[:], OP.add))
        e_t = vec(lambda v: v.ts(t_tm[:], t_tm[:], 0.5, None, OP.mult))
        e_bc = mm(lambda t: t.matmul(ps_a[0:128, 4:6], t_onesrf[0:1, :], t_tm[0:1, :], start=True, stop=True), waits=[e_t])
        vec(lambda v: v.copy(t_tbc[:], ps_a[0:128, 4:6]), waits=[e_bc])
    e_lo = vec(lambda v: v.copy(t_tm[:], t_lo[:]))
    e_bc = mm(lambda t: t.matmul(ps_a[0:128, 4:6], t_onesrf[0:1, :], t_tm[0:1, :], start=True, stop=True), waits=[e_lo])
    vec(lambda v: v.copy(t_tbc[:], ps_a[0:128, 4:6]), waits=[e_bc])
    e_pc = None
    for b in range(NB):
        bs = slice(32 * b, 32 * b + 32)
        vec(lambda v, b=b, bs=bs: v.ts(t_mask[:, bs], st["zz"][:, bs], t_tbc[:, b:b + 1], None, OP.is_gt))
        e_pc = vec(lambda v, b=b, bs=bs: v.red(t_pcnt[:, b:b + 1], t_mask[:, bs], OP.add))
    e_off = mm(lambda t: t.matmul(ps_a[0:128, 8:10], t_tri[:], t_pcnt[:], start=True, stop=True), waits=[e_pc])
    e_tot = mm(lambda t: t.matmul(ps_a[0:1, 0:2], t_onescf[:], t_pcnt[:], start=True, stop=True))
    vec(lambda v: v.copy(t_poff[:], ps_a[0:128, 8:10]), waits=[e_off])
    e_cc = vec(lambda v: v.copy(t_cn[:], ps_a[0:1, 0:2]), waits=[e_tot])
    e_cmin = vec(lambda v: v.ts(t_cn[:], t_cn[:], float(C), None, OP.min), waits=[e_cc])
    e_cb = mm(lambda t: t.matmul(ps_a[0:128, 12:14], t_onesrf[0:1, :], t_cn[0:1, :], start=True, stop=True), waits=[e_cmin])
    e_cbv = vec(lambda v: v.copy(t_cntb[:], ps_a[0:128, 12:14]), waits=[e_cb])
    vec(lambda v: v.copy(t_sa[:], t_mask[:]))
    cur, nxt = t_sa, t_sb2
    for sh in [1, 2, 4, 8, 16]:
        for b in range(NB):
            o = 32 * b
            vec(lambda v, cur=cur, nxt=nxt, o=o, sh=sh: v.copy(nxt[:, o:o + sh], cur[:, o:o + sh]))
            vec(lambda v, cur=cur, nxt=nxt, o=o, sh=sh: v.tt(nxt[:, o + sh:o + 32], cur[:, o + sh:o + 32], cur[:, o:o + 32 - sh], OP.add))
        cur, nxt = nxt, cur
    vec(lambda v, cur=cur: v.tt(cur[:], cur[:], t_mask[:], OP.subtract))
    for b in range(NB):
        bs = slice(32 * b, 32 * b + 32)
        vec(lambda v, cur=cur, b=b, bs=bs: v.ts(st["dest"][:, bs], cur[:, bs], t_poff[:, b:b + 1], None, OP.add))
        if b == 1:
            vec(lambda v, bs=bs: v.ts(st["dest"][:, bs], st["dest"][:, bs], float(C), None, OP.add))
    vec(lambda v: v.ts(t_umask[:], t_mask[:], 0.5, None, OP.is_lt))
    vec(lambda v: v.cpred(st["dest"][:], t_umask[:], t_bigt[:]), waits=[e_big])
    for b in range(NB):
        bs = slice(32 * b, 32 * b + 32)
        vec(lambda v, b=b, bs=bs: v.ts(t_umask[:, bs], st["dest"][:, bs], float((b + 1) * C), None, OP.is_ge))
        vec(lambda v, bs=bs: v.cpred(st["dest"][:, bs], t_umask[:, bs], t_bigt[:, bs]))
    e_di = vec(lambda v: v.copy(t_desti[:], st["dest"][:]))
    e_tk = None
    for k4 in range(4):
        e_tk = vec(lambda v, k4=k4: v.ts(t_tok64[:, 16 * k4:16 * (k4 + 1)], t_slotid[:],
                                         2048.0 * k4, None, OP.add))
    e_b32 = vec(lambda v: v.memset(t_big32[:], BIGF))
    e_ti = [dma("gpsimd", tokid_sp.rearrange("(c p) o -> c (p o)", p=128), t_big32[:], waits=[e_b32])]
    e_kb = []
    for b in range(NB):
        for kc in range(16):
            cix = 16 * b + kc
            vec(lambda v, cix=cix, kc=kc, b=b: v.ts(t_umask[:, cix:cix + 1], t_slotid[:, kc:kc + 1], t_cntb[:, b:b + 1], None, OP.is_ge), waits=[e_cbv])
            vec(lambda v, cix=cix: v.memset(t_kbias[:, cix:cix + 1], 0.0))
            e_kb.append(vec(lambda v, cix=cix: v.cpred(t_kbias[:, cix:cix + 1], t_umask[:, cix:cix + 1], t_negt[:])))
    P.barrier()

    for j in range(32):
        dma("gpsimd", comp[128 * j:128 * (j + 1), :], t_zeroE[:])
    e_breg = [None]
    e_breg2 = [None]

    def setup_breg(g):
        r = g.alloc_register()
        g.reg_mov(r, NB * C - 1)
        e_breg[0] = r
        r2 = g.alloc_register()
        g.reg_mov(r2, NB * N - 1)
        e_breg2[0] = r2
        return None
    P.emit("gpsimd", lambda g: setup_breg(g))
    e_sc_chain = [None, None]
    rows_b = [t_rows, t_rows1]
    dcol_b = [t_dcol, t_dcol1]
    for j in range(64):
        b, cix = j // 32, j % 32
        col = b * N + 128 * cix
        scc = b * 32 + cix
        par = j % 2
        rows, dcol = rows_b[par], dcol_b[par]
        e_r = dma("gpsimd", rows[:], ln2_sp[col:col + 128, :], waits=[e_sc_chain[par]])
        e_dc = gps(lambda g, scc=scc, dcol=dcol: g.tensor_copy(dcol[:], t_desti[:, scc:scc + 1]))
        e_sc_chain[par] = P.emit("gpsimd", lambda g, rows=rows, dcol=dcol: g.indirect_dma_start(
            comp, IndirectOffsetOnAxis(dcol[:], 0), rows[:], None,
            bounds_check=e_breg[0], oob_is_err=False), waits=[e_r, e_dc], inc="dmg", k=16)
        if j == 31:
            e_sc_b0 = list(e_sc_chain)
    e_sc_prev = e_sc_chain
    e_sc_bybatch = [e_sc_b0, e_sc_chain]
    P.barrier()
    if PHASE_LIM <= 5:
        _finish(nc, P, VEC, locals());  return _replay(nc, P, VEC, locals(), es, sems)

    if not SKIP_EF:
        _phase_E_F(nc, P, VEC, locals())
    _finish(nc, P, VEC, locals())
    return _replay(nc, P, VEC, locals(), es, sems)


def _finish(nc, P, VEC, env):
    if DBG:
        dma, st, dbg_st, t_macc, t_gden = env["dma"], env["st"], env["dbg_st"], env["t_macc"], env["t_gden"]
        for di, nm in enumerate(["mu1", "nmu1", "rstd1", "sqm", "skm", "sqq", "skk",
                                 "sqk", "ab", "ssc", "nmu2", "rstd2", "zz", "dest"]):
            dma("scalar", dbg_st[di], st[nm][:])
        dma("scalar", dbg_st[14].rearrange("p c -> (p c)")[0:2048].rearrange("(o a) -> o a", o=1), t_macc[0:1, :])
        dma("scalar", dbg_st[15][:, 0:16], t_gden[:])
    P.emit("sync", lambda s: s.nop(), waits=[("dmsc", P.counts.get("dmsc", 0)),
                                             ("dmg", P.counts.get("dmg", 0)),
                                             ("dms", P.counts.get("dms", 0))])
    P.barrier()


def _replay(nc, P, VEC, env, es, sems):
    P.finalize()
    with nc.Block() as block:
        def section(eng_name):
            def run(eng):
                if eng_name == "vector":
                    VEC[0] = Vec(eng)
                for fn, waits, inc, k, v in P.ops[eng_name]:
                    if fn == "BARRIER":
                        eng.sem_inc(sems["bar"], 1)
                        eng.wait_ge(sems["bar"], inc)
                        if eng_name == "vector" and VEC[0] is not None:
                            VEC[0].dirty.clear()
                        continue
                    for (s, vv) in waits:
                        eng.wait_ge(sems[s], vv)
                    ins = fn(eng)
                    if inc is not None and ins is not None:
                        ins.then_inc(sems[inc], k)
                    elif inc is not None and ins is None:
                        eng.sem_inc(sems[inc], k)
            return run

        block.sync(section("sync"))
        block.scalar(section("scalar"))
        block.vector(section("vector"))
        block.tensor(section("tensor"))
        block.gpsimd(section("gpsimd"))

    es.close()
    return nc


def _phase_E_F(nc, P, VEC, env):
    g = env
    sems = g["sems"]
    t_l2sT, t_wffn, t_vffn, t_aoT, t_expS = g["t_l2sT"], g["t_wffn"], g["t_vffn"], g["t_aoT"], g["t_expS"]
    t_vffn4 = g["t_vffn4"]
    t_kh, t_qh, t_wo2, t_orow = g["t_kh"], g["t_qh"], g["t_wo2"], g["t_orow"]
    t_rows, t_facc, t_fx2, t_drow = g["t_rows"], g["t_facc"], g["t_fx2"], g["t_drow"]
    t_kbias, t_gden, t_gsel, t_wgbf = g["t_kbias"], g["t_gden"], g["t_gsel"], g["t_wgbf"]
    t_onesall, t_ident, t_desti = g["t_onesall"], g["t_ident"], g["t_desti"]
    t_qksc, t_dcol = g["t_qksc"], g["t_dcol"]
    st = g["st"]
    ps_a, ps_b = g["ps_a"], g["ps_b"]
    comp, qT_sp, kT_sp, ffn_sp, x2_sp, out_t = g["comp"], g["qT_sp"], g["kT_sp"], g["ffn_sp"], g["x2_sp"], g["out_t"]
    vT_sp = g["vT_sp"]
    wo_bf = g["wo_bf"]
    t_lnrow = g["t_lnrow"]
    wq_in, wk_in, wv_in, wo_in = g["wq_in"], g["wk_in"], g["wv_in"], g["wo_in"]
    dma, dmat, mm, act, vec, gps = g["dma"], g["dmat"], g["mm"], g["act"], g["vec"], g["gps"]
    e_breg = g["e_breg"]
    e_breg2 = g["e_breg2"]
    tokid_sp, t_stok, t_stokT = g["tokid_sp"], g["t_stok"], g["t_stokT"]
    t_tok64, t_dcol2 = g["t_tok64"], g["t_dcol2"]
    e_x2_last = g["e_x2_last"]
    e_sc_last = list(g["e_sc_prev"])
    e_tok_last = None
    for j in range(64):
        scc = j
        e_dc2 = gps(lambda gg, scc=scc: gg.tensor_copy(t_dcol2[:], t_desti[:, scc:scc + 1]),
                    waits=(g["e_ti"] + [g["e_tk"], g["e_di"]]) if j == 0 else [])
        e_tok_last = P.emit("gpsimd", lambda gg, scc=scc: gg.indirect_dma_start(
            tokid_sp, IndirectOffsetOnAxis(t_dcol2[:], 0), t_tok64[:, scc:scc + 1], None,
            bounds_check=g["e_breg"][0], oob_is_err=False),
            waits=[e_dc2], inc="dmg", k=16)

    g["dbg_evs"] = []
    e_prev_fdma = None
    for b in range(NB):
        if b > 0:
            P.barrier()
        e_tl = []
        for cc in range(8):
            e_tl.append(dmat("sync", t_l2sT[:, cc, :], comp[b * C:(b + 1) * C, 128 * cc:128 * (cc + 1)],
                             waits=list(g["dbg_evs"]) + list(g["e_sc_bybatch"][b])))
        e_on = [vec(lambda v: v.memset(t_vffn[:, :, :], 1.0))]
        e_zm = None
        for qt in range(4):
            for cc in range(8):
                e_zm = mm(lambda t, qt=qt, cc=cc: t.matmul(
                    ps_b[0:1, 512 * qt:512 * (qt + 1)], t_wgbf[:, cc, :],
                    t_l2sT[:, cc, 512 * qt:512 * (qt + 1)],
                    start=(cc == 0), stop=(cc == 7)), waits=e_tl)
        e_gs = act(lambda s: s.activation(t_gsel[0:1, :], ps_b[0:1, 0:2048], AF.Sigmoid), waits=[e_zm])
        if b == 0 and DBG:
            P.emit("gpsimd", lambda g: g.dma_start(vT_sp[0:1, 0:2048], t_gsel[0:1, :]), waits=[e_gs], inc="dmg", k=16)
            P.emit("gpsimd", lambda g: g.dma_start(vT_sp[1:3, 0:2048], t_l2sT[0:2, 0, :]), waits=list(e_tl), inc="dmg", k=16)
            P.emit("gpsimd", lambda g: g.dma_start(vT_sp[3:5, 0:2048], t_l2sT[0:2, 7, :]), waits=list(e_tl), inc="dmg", k=16)
        e_gt = []
        for kc in range(16):
            e_t = mm(lambda t, kc=kc: t.transpose(ps_a[0:128, 0:1],
                                                  t_gsel[0:1, 128 * kc:128 * (kc + 1)], t_ident[0:1, 0:1]),
                     waits=[e_gs] + ([e_gt[-1]] if e_gt else []))
            e_gt.append(vec(lambda v, kc=kc: v.copy(t_gden[:, kc:kc + 1], ps_a[0:128, 0:1]), waits=[e_t]))
        e_spill = []
        e_last_wmm = e_zm
        for w_i, (w_dram, dst) in enumerate([(wq_in, qT_sp), (wk_in, kT_sp)]):
            e_w = dma("gpsimd", t_wffn[:], w_dram.rearrange("(c p) o -> p c o", p=128),
                      waits=[e_last_wmm])
            e_par = [e_gt[-1], e_gt[-1]]
            for oc in range(8):
                par = oc % 2
                pp = ps_a[:, 0:512] if par == 0 else ps_a[:, 512:1024]
                for qt in range(4):
                    e_mm = None
                    for cc in range(8):
                        e_mm = mm(lambda t, oc=oc, qt=qt, cc=cc, pp=pp: t.matmul(
                            pp, t_wffn[:, cc, 128 * oc:128 * (oc + 1)],
                            t_l2sT[:, cc, 512 * qt:512 * (qt + 1)],
                            start=(cc == 0), stop=(cc == 7)),
                            waits=([e_w] + e_tl + [e_par[par]]) if cc == 0 else [])
                    e_ev = act(lambda s, pp=pp: s.activation(t_qksc[:, 0:512], pp, AF.Identity),
                               waits=[e_mm] + (e_spill[-1:] if e_spill else []))
                    e_par[par] = e_ev
                    e_last_wmm = e_mm
                    e_spill.append(dma("scalar", dst[b, 128 * oc:128 * (oc + 1), 512 * qt:512 * (qt + 1)],
                                       t_qksc[:, 0:512], waits=[e_ev]))
        e_wv = dma("gpsimd", t_wffn[:], wv_in.rearrange("(c p) o -> p c o", p=128),
                   waits=[e_last_wmm])
        e_vgrp = []
        for kc in range(16):
            for ot in range(2):
                gi = 2 * kc + ot
                par = gi % 2
                pp = ps_b[:, 0:512] if par == 0 else ps_b[:, 512:1024]
                w0 = [e_wv, e_gs] + e_on
                if gi >= 2:
                    w0 = list(e_vgrp[gi - 2])
                e_mm = None
                for cc in range(8):
                    e_mm = mm(lambda t, kc=kc, ot=ot, cc=cc, pp=pp: t.matmul(
                        pp, t_l2sT[:, cc, 128 * kc:128 * (kc + 1)],
                        t_wffn[:, cc, 512 * ot:512 * (ot + 1)],
                        start=(cc == 0), stop=(cc == 7)), waits=w0 if cc == 0 else [])
                grp = [vec(lambda v, kc=kc, ot=ot, pp=pp: v.copy(
                    t_vffn4[:, kc, 8 * ot:8 * (ot + 1), 0:64], pp[:, 0:512]), waits=[e_mm])]
                e_vgrp.append(grp)
        e_wo0 = dma("gpsimd", wo_bf, wo_in)
        e_wo = dma("sync", t_wo2[:], wo_bf.rearrange("(h p) o -> p h o", p=64), waits=[e_wo0])
        e_prev_av = e_vgrp[-1][-1]
        e_na = e_vgrp[-1][-1]
        e_prev_scores = None
        e_prev_avmm = None
        for h in range(H):
            e_kl = dma("sync", t_kh[:], kT_sp[b, DH * h:DH * (h + 1), :],
                       waits=[e_spill[-1], e_prev_scores])
            e_ql = dma("sync", t_qh[:], qT_sp[b, DH * h:DH * (h + 1), :],
                       waits=[e_spill[-1], e_prev_scores])
            e_qt_av = []
            for qt in range(4):
                e_ex = []
                for kc in range(16):
                    par = kc % 2
                    sps = ps_a[:, 0:512] if par == 0 else ps_a[:, 512:1024]
                    e_s = mm(lambda t, kc=kc, qt=qt, sps=sps: t.matmul(
                        sps, t_kh[:, 128 * kc:128 * (kc + 1)], t_qh[:, 512 * qt:512 * (qt + 1)],
                        start=True, stop=True),
                        waits=[e_kl, e_ql] + ([e_ex[kc - 2]] if kc >= 2 else []))
                    e_ex.append(act(lambda s, kc=kc, b=b, sps=sps: s.activation(
                        t_expS[:, kc, :], sps, AF.Exp,
                        bias=t_kbias[:, 16 * b + kc:16 * b + kc + 1],
                        scale=0.125), waits=[e_s, e_prev_avmm]))
                e_prev_scores = e_ex[-1]
                e_mm = None
                for kc in range(16):
                    e_mm = mm(lambda t, kc=kc, h=h: t.matmul(
                        ps_a[0:65, 1024:1536], t_vffn[:, kc, 65 * h:65 * (h + 1)],
                        t_expS[:, kc, :], start=(kc == 0), stop=(kc == 15)),
                        waits=[e_ex[kc]] + ([e_prev_av] if kc == 0 else []))
                e_prev_avmm = e_mm
                e_a1 = vec(lambda v, h=h, qt=qt: v.copy(t_aoT[:, h, 512 * qt:512 * (qt + 1)],
                                                        ps_a[0:64, 1024:1536]), waits=[e_mm])
                e_d1 = vec(lambda v, qt=qt: v.copy(t_drow[64:65, 512 * qt:512 * (qt + 1)],
                                                   ps_a[64:65, 1024:1536]), waits=[e_mm])
                e_prev_av = e_d1
                e_qt_av.append(e_a1)
            e_rc0 = act(lambda s: s.activation(t_drow[64:65, :], t_drow[64:65, :], AF.Ln),
                        waits=[e_prev_av])
            e_rc = act(lambda s: s.activation(t_drow[64:65, :], t_drow[64:65, :], AF.Exp,
                                              scale=-1.0), waits=[e_rc0])
            e_ntt = [None, None]
            for qt in range(4):
                par = qt % 2
                npp = ps_b[0:64, 1024 + 512 * par:1536 + 512 * par]
                e_nb = mm(lambda t, qt=qt, npp=npp: t.matmul(
                    npp, t_onesall[64:65, 0:64], t_drow[64:65, 512 * qt:512 * (qt + 1)],
                    start=True, stop=True), waits=[e_rc] + ([e_ntt[par]] if e_ntt[par] else []))
                e_ntt[par] = vec(lambda v, h=h, qt=qt, npp=npp: v.tt(
                    t_aoT[:, h, 512 * qt:512 * (qt + 1)],
                    t_aoT[:, h, 512 * qt:512 * (qt + 1)], npp, OP.mult),
                    waits=[e_nb, e_qt_av[qt]])
            e_na = e_ntt[1]
            if h == 0 and b == 0 and DBG:
                e_dbg1 = P.emit("gpsimd", lambda g: g.dma_start(vT_sp[8:72, 0:2048], t_aoT[:, 0, :]),
                                waits=[e_na], inc="dmg", k=16)
                e_dbg2 = P.emit("gpsimd", lambda g: g.dma_start(vT_sp[5:6, 0:2048], t_drow[64:65, :]),
                                waits=[e_prev_av], inc="dmg", k=16)
                g["dbg_evs"].extend([e_dbg1, e_dbg2])
        if b == 0 and g.get("dbg_big") is not None:
            dbg_big = g["dbg_big"]
            for hh2 in range(16):
                e_dmp = P.emit("gpsimd", lambda g2, hh2=hh2, dbg_big=dbg_big: g2.dma_start(
                    dbg_big[64 * hh2:64 * (hh2 + 1), 0:2048], t_aoT[:, hh2, :]),
                    waits=[e_na], inc="dmg", k=16)
                g["dbg_evs"].append(e_dmp)
        e_ix0 = dma("sync", t_stokT[:],
                    tokid_sp[b * C:(b + 1) * C, :].rearrange("(c p) o -> c (p o)", p=128),
                    waits=[e_tok_last])
        e_trx = mm(lambda t: t.transpose(ps_a[0:128, 1536:1552], t_stokT[:], t_ident[0:16, 0:16]),
                   waits=[e_ix0])
        e_ix = vec(lambda v, b=b: v.copy(t_stok[:, 16 * b:16 * (b + 1)], ps_a[0:128, 1536:1552]),
                   waits=[e_trx])
        e_prev_ev = [e_na, e_na]
        e_prev_eb = None
        for sl in range(16):
            par = sl % 2
            pp = ps_b[:, 0:1024] if par == 0 else ps_b[:, 1024:2048]
            e_mm = None
            for ot in range(2):
                for h in range(H):
                    e_mm = mm(lambda t, sl=sl, h=h, ot=ot, pp=pp: t.matmul(
                        pp[:, 512 * ot:512 * (ot + 1)],
                        t_aoT[:, h, 128 * sl:128 * (sl + 1)],
                        t_wo2[:, h, 512 * ot:512 * (ot + 1)],
                        start=(h == 0), stop=(h == H - 1)),
                        waits=[e_na, e_wo, e_prev_ev[par]] if (h == 0 and ot == 0) else [])
            e_g = vec(lambda v, sl=sl, pp=pp: v.ts(t_orow[:], pp, t_gden[:, sl:sl + 1], None, OP.mult),
                      waits=[e_mm, e_gt[-1], e_prev_eb])
            if b == 0 and sl == 2 and g.get("dbg_big") is not None:
                e_g = P.emit("gpsimd", lambda g2: g2.dma_start(g["dbg_big"][1024:1152, 0:1024], t_orow[:]),
                             waits=[e_g], inc="dmg", k=16)
                g["dbg_evs"].append(e_g)
            if b == 0 and sl == 0:
                e_gd = P.emit("gpsimd", lambda g2: g2.dma_start(vT_sp[80:208, 0:16], t_gden[:]),
                              waits=[e_gt[-1]], inc="dmg", k=16)
                g["dbg_evs"].append(e_gd)
            e_prev_ev[par] = e_g
            ea = act(lambda s: s.activation(t_qksc[:, 0:1024], t_orow[:], AF.Identity,
                                            accum_out=t_drow[0:128, 0:1]), waits=[e_g])
            eb = act(lambda s: s.activation(t_qksc[:, 0:1024], t_orow[:], AF.Square,
                                            accum_out=t_drow[0:128, 1:2]), waits=[e_g])
            e_prev_eb = eb
            vec(lambda v: v.ts(t_drow[0:128, 0:1], t_drow[0:128, 0:1], 1.0 / D, None, OP.mult), waits=[ea, eb])
            vec(lambda v: v.tt(t_drow[0:128, 2:3], t_drow[0:128, 0:1], t_drow[0:128, 0:1], OP.mult))
            vec(lambda v: v.ts(t_drow[0:128, 1:2], t_drow[0:128, 1:2], 1.0 / D, None, OP.mult))
            vec(lambda v: v.tt(t_drow[0:128, 1:2], t_drow[0:128, 1:2], t_drow[0:128, 2:3], OP.subtract))
            ev5 = vec(lambda v: v.ts(t_drow[0:128, 1:2], t_drow[0:128, 1:2], EPS, None, OP.add))
            ev6 = act(lambda s: s.activation(t_drow[0:128, 1:2], t_drow[0:128, 1:2], AF.Sqrt), waits=[ev5])
            ev7 = vec(lambda v: v.recip(t_drow[0:128, 1:2], t_drow[0:128, 1:2]), waits=[ev6])
            ev8 = vec(lambda v: v.ts(t_drow[0:128, 0:1], t_drow[0:128, 0:1], -1.0, None, OP.mult))
            e_ln = vec(lambda v: v.ts(t_lnrow[:], t_orow[:], t_drow[0:128, 0:1], t_drow[0:128, 1:2],
                                      OP.add, OP.mult), waits=[ev7, ev8, e_prev_fdma])
            e_prev_fdma = P.emit("gpsimd", lambda gg, b=b, sl=sl: gg.indirect_dma_start(
                out_t, IndirectOffsetOnAxis(t_stok[:, 16 * b + sl:16 * b + sl + 1], 0),
                t_lnrow[:], None, bounds_check=e_breg2[0], oob_is_err=False,
                compute_op=mybir.AluOpType.add),
                waits=[e_ln, e_ix, e_x2_last], inc="dmg", k=16)
    P.barrier()


_CACHE = {}


def kernel(**inputs):
    B = 16
    x = np.asarray(inputs["L0_attn_x"], dtype=np.float32)
    if "nc" not in _CACHE:
        _CACHE["nc"] = build_nc()
    nc = _CACHE["nc"]

    tri = (np.arange(128)[:, None] < np.arange(128)[None, :]).astype(np.float32)
    onescf = np.ones((128, 1), np.float32)
    onesrf = np.ones((1, 128), np.float32)
    onesall = np.ones((128, 128), np.float32)
    ident = np.eye(128, dtype=np.float32)
    slotid = (np.arange(128)[:, None] + 128.0 * np.arange(16)[None, :]).astype(np.float32)

    shared = dict(
        qkv_w=np.ascontiguousarray(inputs["L0_attn_qkv_w"], dtype=np.float32),
        proj_w=np.ascontiguousarray(inputs["L0_attn_proj_w"], dtype=np.float32),
        ch_w1=np.ascontiguousarray(inputs["L0_attn_ch_w1"], dtype=np.float32),
        ch_w2=np.ascontiguousarray(inputs["L0_attn_ch_w2"], dtype=np.float32),
        ch_b1=np.ascontiguousarray(inputs["L0_attn_ch_b1"], dtype=np.float32),
        ch_b2=np.ascontiguousarray(inputs["L0_attn_ch_b2"], dtype=np.float32),
        proj_b=np.ascontiguousarray(inputs["L0_attn_proj_b"], dtype=np.float32),
        wg_in=np.ascontiguousarray(inputs["L0_ffn_Wg"], dtype=np.float32),
        wq_in=np.ascontiguousarray(inputs["L0_ffn_Wq"], dtype=np.float32),
        wk_in=np.ascontiguousarray(inputs["L0_ffn_Wk"], dtype=np.float32),
        wv_in=np.ascontiguousarray(inputs["L0_ffn_Wv"], dtype=np.float32),
        wo_in=np.ascontiguousarray(inputs["L0_ffn_Wo"], dtype=np.float32),
        in_c1=np.asarray(inputs["L0_attn_c1"], dtype=np.float32).reshape(1),
        in_c2=np.asarray(inputs["L0_attn_c2"], dtype=np.float32).reshape(1),
        in_conv1=np.ascontiguousarray(inputs["L0_attn_conv1_w"], dtype=np.float32),
        in_gn_g=np.ascontiguousarray(inputs["L0_attn_gn_g"], dtype=np.float32),
        in_gn_b=np.ascontiguousarray(inputs["L0_attn_gn_b"], dtype=np.float32),
        in_conv2=np.ascontiguousarray(inputs["L0_attn_conv2_w"], dtype=np.float32),
        in_conv2b=np.asarray(inputs["L0_attn_conv2_b"], dtype=np.float32).reshape(1),
        tri_in=tri, onescf_in=onescf, onescr_in=onescf, onesrf_in=onesrf,
        onesrr_in=onesrf, onesall_in=onesall, ident_in=ident, slotid_in=slotid,
    )
    in_maps = []
    for c in range(8):
        xs = x[2 * c:2 * c + 2]
        x_tm = np.ascontiguousarray(xs.reshape(2 * N, D))
        xT = np.ascontiguousarray(np.moveaxis(xs, 2, 0).reshape(D, 2 * N))
        m = dict(shared)
        m["x_tm"] = x_tm
        m["xT"] = xT
        in_maps.append(m)

    res = run_bass_kernel_spmd(nc, in_maps, core_ids=list(range(8)))
    out = np.concatenate([r["out_t"].reshape(2, N, D) for r in res.results], axis=0)
    return out


# version 05
# speedup vs baseline: 1.0323x; 1.0323x over previous
import numpy as np
import concourse.bass as bass
import concourse.mybir as mybir
from concourse.bass_utils import run_bass_kernel_spmd
from concourse.bass import IndirectOffsetOnAxis
from contextlib import ExitStack

F32 = mybir.dt.float32
F32R = mybir.dt.float32r
BF16 = mybir.dt.bfloat16
I32 = mybir.dt.int32
U32 = mybir.dt.uint32
OP = mybir.AluOpType
AF = mybir.ActivationFunctionType
AX = mybir.AxisListType.X

NB, N, D, C, H, DH = 2, 4096, 1024, 2048, 16, 64
NT = 16
ROUNDS = 24
DBG = False
SKIP_EF = False
PHASE_LIM = 99
SETUP_LIM = 99
A_TILES = 16
A_STAGE = 99
DUMP_XC = False
BIGF = float(1 << 22)
EPS = 1e-5
INVCM1 = 1.0 / (D - 1)

ENGINES = ("sync", "scalar", "vector", "tensor", "gpsimd")


class Prog:
    def __init__(self):
        self.ops = {e: [] for e in ENGINES}
        self.counts = {}
        self.hw = {e: {} for e in ENGINES}

    def emit(self, eng, fn, waits=(), inc=None, k=1):
        w = []
        for ev in waits:
            if ev is None:
                continue
            s, v = ev
            if self.hw[eng].get(s, 0) < v:
                w.append((s, v))
                self.hw[eng][s] = v
        ev_out = None
        v = 0
        if inc is not None:
            self.counts[inc] = self.counts.get(inc, 0) + k
            v = self.counts[inc]
            ev_out = (inc, v)
        self.ops[eng].append((fn, tuple(w), inc, k, v))
        return ev_out

    def barrier(self):
        snap = {s: v for s, v in self.counts.items() if s in ("tn", "ve", "ac", "gp")}
        self.counts["bar"] = self.counts.get("bar", 0) + 5
        tgt = self.counts["bar"]
        for e in ENGINES:
            self.ops[e].append(("BARRIER", (), tgt, 1, tgt))
            hw = self.hw[e]
            for s, v in snap.items():
                hw[s] = max(hw.get(s, 0), v)
            hw["bar"] = tgt


    def finalize(self):
        waited = {}
        for e in ENGINES:
            for op in self.ops[e]:
                if op[0] == "BARRIER":
                    continue
                for (s, v) in op[1]:
                    waited.setdefault(s, set()).add(v)
        remap = {}
        for s in ("tn", "ve", "ac", "gp"):
            vals = sorted(waited.get(s, ()))
            remap[s] = {v: i + 1 for i, v in enumerate(vals)}
        for e in ENGINES:
            newops = []
            for fn, w, inc, k, v in self.ops[e]:
                if fn == "BARRIER":
                    newops.append((fn, w, inc, k, v))
                    continue
                w2 = tuple((s, remap[s][x]) if s in remap else (s, x) for (s, x) in w)
                if inc in remap:
                    if v in remap[inc]:
                        newops.append((fn, w2, inc, k, remap[inc][v]))
                    else:
                        newops.append((fn, w2, None, 0, 0))
                else:
                    newops.append((fn, w2, inc, k, v))
            self.ops[e] = newops


class Vec:

    def __init__(self, v):
        self.v = v
        self.dirty = set()

    def _nm(self, ap):
        t = getattr(ap, "tensor", None)
        return t.name if t is not None else None

    def _pre(self, aps):
        names = {self._nm(a) for a in aps if a is not None} - {None}
        if self.dirty & names:
            self.v.drain()
            self.dirty.clear()

    def _post(self, out, extra=None):
        self.dirty.add(self._nm(out))
        if extra is not None:
            self.dirty.add(self._nm(extra))

    def ts(self, out, in0, s1, s2, op0, op1=None, accum_out=None):
        ins = [in0, out] + [s for s in (s1, s2) if hasattr(s, "tensor")]
        self._pre(ins)
        if op1 is None:
            r = self.v.tensor_scalar(out, in0, s1, s2, op0, accum_out=accum_out)
        else:
            r = self.v.tensor_scalar(out, in0, s1, s2, op0, op1, accum_out=accum_out)
        self._post(out, accum_out)
        return r

    def tt(self, out, a, b, op):
        self._pre([a, b, out])
        r = self.v.tensor_tensor(out, a, b, op)
        self._post(out)
        return r

    def stt(self, out, in0, scalar, in1, op0, op1, accum_out=None):
        ins = [in0, in1, out] + ([scalar] if hasattr(scalar, "tensor") else [])
        self._pre(ins)
        r = self.v.scalar_tensor_tensor(out, in0, scalar, in1, op0, op1, accum_out=accum_out)
        self._post(out, accum_out)
        return r

    def red(self, out, in_, op):
        self._pre([in_, out])
        r = self.v.tensor_reduce(out, in_, AX, op)
        self._post(out)
        return r

    def copy(self, out, in_):
        self._pre([in_, out])
        r = self.v.tensor_copy(out, in_)
        self._post(out)
        return r

    def memset(self, out, c):
        self._pre([out])
        r = self.v.memset(out, c)
        self._post(out)
        return r

    def cpred(self, out, mask, data):
        self._pre([out, mask, data])
        r = self.v.copy_predicated(out, mask, data)
        self._post(out)
        return r

    def recip(self, out, in_):
        self._pre([in_, out])
        r = self.v.reciprocal(out, in_)
        self._post(out)
        return r


def build_nc():
    nc = bass.Bass("TRN2", target_bir_lowering=False)
    es = ExitStack()
    P = Prog()
    dt_ = nc.dram_tensor

    xT = dt_("xT", [D, NB * N], F32R, kind="ExternalInput").ap()
    x_tm = dt_("x_tm", [NB * N, D], F32, kind="ExternalInput").ap()
    qkv_w = dt_("qkv_w", [D, 3 * D], F32R, kind="ExternalInput").ap()
    proj_w = dt_("proj_w", [D, D], F32R, kind="ExternalInput").ap()
    ch_w1 = dt_("ch_w1", [D, 256], F32R, kind="ExternalInput").ap()
    ch_w2 = dt_("ch_w2", [256, D], F32R, kind="ExternalInput").ap()
    ch_b1 = dt_("ch_b1", [256], F32, kind="ExternalInput").ap()
    ch_b2 = dt_("ch_b2", [D], F32, kind="ExternalInput").ap()
    proj_b = dt_("proj_b", [D], F32, kind="ExternalInput").ap()
    wg_in = dt_("wg_in", [D, 1], F32, kind="ExternalInput").ap()
    wq_in = dt_("wq_in", [D, D], F32, kind="ExternalInput").ap()
    wk_in = dt_("wk_in", [D, D], F32, kind="ExternalInput").ap()
    wv_in = dt_("wv_in", [D, D], F32, kind="ExternalInput").ap()
    wo_in = dt_("wo_in", [D, D], F32, kind="ExternalInput").ap()
    sm_in = {}
    for nm, shp in [("c1", [1]), ("c2", [1]), ("conv1", [4, 1, 3]), ("gn_g", [4]),
                    ("gn_b", [4]), ("conv2", [1, 4, 3]), ("conv2b", [1])]:
        sm_in[nm] = dt_("in_" + nm, shp, F32, kind="ExternalInput").ap()
    tri_in = dt_("tri_in", [128, 128], F32, kind="ExternalInput").ap()
    onescf_in = dt_("onescf_in", [128, 1], F32, kind="ExternalInput").ap()
    onescr_in = dt_("onescr_in", [128, 1], F32R, kind="ExternalInput").ap()
    onesrf_in = dt_("onesrf_in", [1, 128], F32, kind="ExternalInput").ap()
    onesrr_in = dt_("onesrr_in", [1, 128], F32R, kind="ExternalInput").ap()
    onesall_in = dt_("onesall_in", [128, 128], F32, kind="ExternalInput").ap()
    ident_in = dt_("ident_in", [128, 128], F32, kind="ExternalInput").ap()
    slotid_in = dt_("slotid_in", [128, 16], F32, kind="ExternalInput").ap()

    out_t = dt_("out_t", [NB * N, D], F32, kind="ExternalOutput").ap()
    vT_sp = dt_("vT_sp", [D, NB * N], F32R, kind="ExternalOutput" if DBG else "Internal").ap()
    x2_sp = dt_("x2_sp", [NB * N, D], F32, kind="ExternalOutput" if DBG else "Internal").ap()
    ln2_sp = dt_("ln2_sp", [NB * N, D], BF16, kind="ExternalOutput" if DBG else "Internal").ap()
    ab_pad = dt_("ab_pad", [NB, N + 128], F32, kind="Internal").ap()
    hh_pad = dt_("hh_pad", [NB, 4, N + 128], F32, kind="Internal").ap()
    comp = dt_("comp", [NB * C, D], BF16, kind="ExternalOutput" if DBG else "Internal").ap()
    qT_sp = dt_("qT_sp", [NB, D, C], BF16, kind="ExternalOutput" if DBG else "Internal").ap()
    kT_sp = dt_("kT_sp", [NB, D, C], BF16, kind="Internal").ap()
    ffn_sp = dt_("ffn_sp", [NB * C, D], F32, kind="ExternalOutput" if DBG else "Internal").ap()
    tokid_sp = dt_("tokid_sp", [NB * C, 1], F32, kind="Internal").ap()
    wo_bf = dt_("wo_bf", [D, D], BF16, kind="Internal").ap()
    dbg_st = dt_("dbg_st", [16, 128, 64], F32, kind="ExternalOutput").ap() if DBG else None
    dbg_big = dt_("dbg_big", [1152, 2048], F32, kind="ExternalOutput").ap() if DBG else None

    sbuf = lambda *a: es.enter_context(nc.sbuf_tensor(*a))
    AR = 166 * 1024
    t_arena = es.enter_context(nc.sbuf_tensor("arena", [128, AR], mybir.dt.uint8))
    arena_base = nc.lookup_mloc(t_arena).addr

    def place(name, shape, dtype, off_kb):
        import functools, operator
        hnd = nc.virtual_sbuf_tensor(name, shape, dtype)
        nc._tensor(hnd.name, hnd.shape, hnd.dtype, type="SB")
        mloc = nc.lookup_mloc(hnd)
        mloc.allocated = True
        mloc.addr = arena_base + int(off_kb * 1024)
        pp = functools.reduce(operator.mul, shape[1:], 1) * mybir.dt.size(dtype)
        assert int(off_kb * 1024) + pp <= AR, (name, off_kb, pp)
        return hnd

    t_wqk = place("t_wqk", [128, 8, 2048], F32R, 0)
    t_wv = place("t_wv", [128, 8, 1024], F32R, 64)
    t_proj = place("t_proj", [128, 8, 1024], F32R, 0)
    t_chw1 = place("t_chw1", [128, 8, 256], F32R, 64)
    t_chw2 = place("t_chw2", [128, 2, 1024], F32R, 72)
    t_attnblk = place("t_attnblk", [128, 4, 1024], F32, 32)
    t_x2blk = place("t_x2blk", [128, 4, 1024], F32, 48)
    t_xt = [place(f"t_xt{j}", [128, 8, 512], F32R, 96 + 16 * j) for j in range(2)]
    t_xtm = [place(f"t_xtm{j}", [128, 4, 1024], F32, 128 + 16 * j) for j in range(2)]
    ode_names = ["am1", "a0", "ap1", "hh0", "hh1", "hh2", "hh3", "tmp", "yy", "fa"]
    t_ode = {nm: place("t_ode_" + nm, [32, 128], F32, 160 + 0.5 * k)
             for k, nm in enumerate(ode_names)}
    t_l2sT = place("t_l2sT", [128, 8, 2048], BF16, 0)
    t_wffn = place("t_wffn", [128, 8, 1024], BF16, 32)
    t_vffn = place("t_vffn", [128, 16, 1040], BF16, 64)
    t_vffn4 = place("t_vffn4", [128, 16, 16, 65], BF16, 64)
    t_attnblkB = place("t_attnblkB", [128, 4, 1024], F32, 64)
    t_aoT = place("t_aoT", [64, 16, 2048], BF16, 0)
    t_expS = place("t_expS", [128, 16, 512], BF16, 97)
    t_wo2 = place("t_wo2", [64, 16, 1024], BF16, 113)
    t_kh = place("t_kh", [64, 2048], BF16, 145)
    t_qh = place("t_qh", [64, 2048], BF16, 149)
    t_orow = place("t_orow", [128, 1024], F32, 153)
    t_drow = place("t_drow", [128, 2048], F32, 157)
    t_zeroE = place("t_zeroE", [128, 1024], BF16, 97)
    t_rows = place("t_rows", [128, 1024], BF16, 99)
    t_rows1 = place("t_rows1", [128, 1024], BF16, 101)
    t_facc = place("t_facc", [128, 1024], F32, 0)
    t_fx2 = place("t_fx2", [128, 1024], F32, 4)
    t_x2t = place("t_x2t", [128, 1024], F32, 80)
    t_biasbc = place("t_biasbc", [128, 1024], F32, 88)
    t_wgbc = place("t_wgbc", [128, 1024], F32, 92)
    t_bias_r = place("t_bias_r", [1, 1024], F32, 80)
    t_wgrow = place("t_wgrow", [1, 1024], F32, 84)
    t_ln2t = place("t_ln2t", [128, 1024], BF16, 84)
    t_gsel = place("t_gsel", [1, 2048], F32, 97)

    t_vout = sbuf("t_vout", [128, 2, 512], F32R)
    t_xm = sbuf("t_xm", [128, 1024], F32R)
    t_qksc = sbuf("t_qksc", [128, 1024], BF16)
    t_qksc2 = sbuf("t_qksc2", [128, 1024], BF16)
    t_zscr = sbuf("t_zscr", [128, 1024], BF16)
    t_lnrow = sbuf("t_lnrow", [128, 1024], F32)
    stat_names = ["mu1", "nmu1", "rstd1", "sqm", "skm", "sqq", "skk", "sqk",
                  "ab", "ssc", "nmu2", "rstd2", "zz", "dest", "scr1", "scr2"]
    st = {nm: sbuf("st_" + nm, [128, 64], F32) for nm in stat_names}
    t_desti = sbuf("t_desti", [128, 64], I32)
    t_smalls = sbuf("t_smalls", [1, 64], F32)
    t_smbc = sbuf("t_smbc", [128, 64], F32)
    t_nmrow = sbuf("t_nmrow", [1, 512], F32R)
    t_onescf = sbuf("t_onescf", [128, 1], F32)
    t_onescr = sbuf("t_onescr", [128, 1], F32R)
    t_onesrf = sbuf("t_onesrf", [1, 128], F32)
    t_onesrr = sbuf("t_onesrr", [1, 128], F32R)
    t_onesall = sbuf("t_onesall", [128, 128], F32)
    t_tri = sbuf("t_tri", [128, 128], F32)
    t_ident = sbuf("t_ident", [128, 128], F32)
    t_slotid = sbuf("t_slotid", [128, 16], F32)
    t_macc = sbuf("t_macc", [1, 2048], F32)
    t_mrow = sbuf("t_mrow", [128, 8, 2], F32R)
    t_wbar = sbuf("t_wbar", [128, 8, 2], F32R)
    t_hrel = sbuf("t_hrel", [128, 2, 2], F32R)
    t_chT = sbuf("t_chT", [128, 8, 2], F32)
    t_wgbf = sbuf("t_wgbf", [128, 8, 1], BF16)
    t_odes = sbuf("t_odes", [32, 16], F32)
    t_vred = sbuf("t_vred", [128, 32], F32)
    t_sc1 = sbuf("t_sc1", [1, 32], F32)
    t_sc2 = sbuf("t_sc2", [1, 32], F32)
    t_lo = sbuf("t_lo", [1, 2], F32)
    t_hi = sbuf("t_hi", [1, 2], F32)
    t_tm = sbuf("t_tm", [1, 2], F32)
    t_cn = sbuf("t_cn", [1, 2], F32)
    t_pr2 = sbuf("t_pr2", [1, 2], U32)
    t_tbc = sbuf("t_tbc", [128, 2], F32)
    t_mask = sbuf("t_mask", [128, 64], F32)
    t_umask = sbuf("t_umask", [128, 64], U32)
    t_pcnt = sbuf("t_pcnt", [128, 2], F32)
    t_sa = sbuf("t_sa", [128, 64], F32)
    t_sb2 = sbuf("t_sb2", [128, 64], F32)
    t_bigt = sbuf("t_bigt", [128, 64], F32)
    t_negt = sbuf("t_negt", [128, 1], F32)
    t_poff = sbuf("t_poff", [128, 2], F32)
    t_cntb = sbuf("t_cntb", [128, 2], F32)
    t_kbias = sbuf("t_kbias", [128, 32], F32)
    t_dcol = sbuf("t_dcol", [128, 1], I32)
    t_dcol1 = sbuf("t_dcol1", [128, 1], I32)
    t_dcol2 = sbuf("t_dcol2", [128, 1], I32)
    t_tok64 = sbuf("t_tok64", [128, 64], F32)
    t_stok = sbuf("t_stok", [128, 32], I32)
    t_stokT = sbuf("t_stokT", [16, 128], F32)
    t_big32 = sbuf("t_big32", [32, 128], F32)
    t_gden = sbuf("t_gden", [128, 16], F32)
    t_srowD = sbuf("t_srowD", [1, 512], F32R)

    ps_a = es.enter_context(nc.psum_tensor("ps_a", [128, 2048], F32))
    ps_b = es.enter_context(nc.psum_tensor("ps_b", [128, 2048], F32))

    sems = {nm: es.enter_context(nc.semaphore(nm)) for nm in
            ["bar", "dms", "dmsc", "dmg", "tn", "ve", "ac", "gp"]}

    VEC = [None]

    def dma(eng, out, in_, waits=(), **kw):
        stream = {"sync": "dms", "scalar": "dmsc", "gpsimd": "dmg"}[eng]
        return P.emit(eng, lambda e: e.dma_start(out, in_, **kw), waits, inc=stream, k=16)

    def dmat(eng, out, in_, waits=()):
        stream = {"sync": "dms", "scalar": "dmsc"}[eng]
        return P.emit(eng, lambda e: e.dma_start(out, in_, transpose=True), waits, inc=stream, k=16)

    def mm(lfn, waits=()):
        return P.emit("tensor", lfn, waits, inc="tn")

    def act(lfn, waits=()):
        return P.emit("scalar", lfn, waits, inc="ac")

    def vec(lfn, waits=()):
        return P.emit("vector", lambda e: lfn(VEC[0]), waits, inc="ve")

    def gps(lfn, waits=()):
        return P.emit("gpsimd", lfn, waits, inc="gp")

    def tile_bt(i):
        return i // 8, (i % 8) * 512

    for o, i_ in [(t_tri[:], tri_in), (t_onescf[:], onescf_in), (t_onescr[:], onescr_in),
                  (t_onesrf[:], onesrf_in), (t_onesrr[:], onesrr_in),
                  (t_onesall[:], onesall_in), (t_ident[:], ident_in),
                  (t_slotid[:], slotid_in)]:
        dma("sync", o, i_)
    if SETUP_LIM >= 2:
        dma("sync", t_wqk[:], qkv_w[:, 0:2048].rearrange("(c p) o -> p c o", p=128))
        dma("sync", t_wv[:], qkv_w[:, 2048:3072].rearrange("(c p) o -> p c o", p=128))
    if SETUP_LIM < 3:
        _finish(nc, P, VEC, locals());  return _replay(nc, P, VEC, locals(), es, sems)
    dma("sync", t_smalls[:, 0:1], sm_in["c1"].rearrange("(o a) -> o a", o=1))
    dma("sync", t_smalls[:, 1:2], sm_in["c2"].rearrange("(o a) -> o a", o=1))
    dma("sync", t_smalls[:, 2:14], sm_in["conv1"].rearrange("a b c -> b (a c)"))
    dma("sync", t_smalls[:, 14:18], sm_in["gn_g"].rearrange("(o a) -> o a", o=1))
    dma("sync", t_smalls[:, 18:22], sm_in["gn_b"].rearrange("(o a) -> o a", o=1))
    dma("sync", t_smalls[:, 22:34], sm_in["conv2"].rearrange("a b c -> a (b c)"))
    dma("sync", t_smalls[:, 34:35], sm_in["conv2b"].rearrange("(o a) -> o a", o=1))
    if SETUP_LIM < 4:
        _finish(nc, P, VEC, locals());  return _replay(nc, P, VEC, locals(), es, sems)
    if PHASE_LIM > 5:
        dma("gpsimd", t_wgbf[:], wg_in.rearrange("(c p) o -> p c o", p=128),
            allow_slow_non_contiguous=True)
    P.barrier()

    vec(lambda v: v.memset(t_macc[:], 0.0))
    vec(lambda v: v.memset(t_negt[:], -1e9))
    P.barrier()
    if PHASE_LIM <= 0:
        _finish(nc, P, VEC, locals());  return _replay(nc, P, VEC, locals(), es, sems)

    e_tile_done = [()] * NT
    e_prev_stats = []
    for i in range(min(NT, A_TILES)):
        b, t0 = tile_bt(i)
        col = b * N + t0
        sc0 = b * 32 + (t0 // 128)
        bufw = list(e_tile_done[i - 2]) if i >= 2 else []
        e_xt = dma("sync", t_xt[i % 2][:], xT[:, col:col + 512].rearrange("(c p) t -> p c t", p=128), waits=bufw)
        e_xtm = dma("sync", t_xtm[i % 2][:], x_tm[col:col + 512, :].rearrange("(c p) d -> p c d", p=128), waits=bufw)
        xc = t_xt[i % 2]
        xtm = t_xtm[i % 2]
        e_sums = []
        for cl in range(4):
            sc = sc0 + cl
            act(lambda s, cl=cl, sc=sc, xtm=xtm: s.activation(
                t_qksc[:, 0:1024], xtm[:, cl, :], AF.Identity,
                accum_out=st["scr1"][:, sc:sc + 1]), waits=[e_xtm])
            e_sums.append(act(lambda s, cl=cl, sc=sc, xtm=xtm: s.activation(
                t_qksc[:, 0:1024], xtm[:, cl, :], AF.Square,
                accum_out=st["scr2"][:, sc:sc + 1]), waits=[e_xtm]))
        s4 = slice(sc0, sc0 + 4)
        ev = vec(lambda v, s4=s4: v.ts(st["mu1"][:, s4], st["scr1"][:, s4], 1.0 / D, None, OP.mult), waits=e_sums)
        vec(lambda v, s4=s4: v.ts(st["nmu1"][:, s4], st["mu1"][:, s4], -1.0, None, OP.mult))
        vec(lambda v, s4=s4: v.tt(st["scr1"][:, s4], st["mu1"][:, s4], st["mu1"][:, s4], OP.mult))
        vec(lambda v, s4=s4: v.ts(st["scr2"][:, s4], st["scr2"][:, s4], 1.0 / D, None, OP.mult))
        vec(lambda v, s4=s4: v.tt(st["scr2"][:, s4], st["scr2"][:, s4], st["scr1"][:, s4], OP.subtract))
        e_var = vec(lambda v, s4=s4: v.ts(st["scr2"][:, s4], st["scr2"][:, s4], EPS, None, OP.add))
        e_sq = act(lambda s, s4=s4: s.activation(st["scr1"][:, s4], st["scr2"][:, s4], AF.Sqrt), waits=[e_var])
        e_rs = vec(lambda v, s4=s4: v.recip(st["rstd1"][:, s4], st["scr1"][:, s4]), waits=[e_sq])
        if A_STAGE < 2:
            e_tile_done[i] = (e_rs, e_rs, e_sums[-1]);  e_prev_stats = [e_rs];  continue
        e_tp = None
        for r in range(4):
            e_tp = mm(lambda t, r=r, sc0=sc0: t.transpose(ps_a[0:1, 128 * r:128 * (r + 1)],
                                                          st["nmu1"][:, sc0 + r:sc0 + r + 1], t_ident[:]),
                      waits=([ev, e_rs] + e_prev_stats) if r == 0 else [])
        e_nr = vec(lambda v: v.copy(t_nmrow[:], ps_a[0:1, 0:512]), waits=[e_tp])
        e_bc = None
        for r in range(4):
            e_bc = mm(lambda t, r=r: t.matmul(ps_a[:, 512 + 128 * r:640 + 128 * r],
                                              t_onesrr[0:1, :], t_nmrow[0:1, 128 * r:128 * (r + 1)],
                                              start=True, stop=True), waits=[e_nr])
        e_xc = []
        for cc in range(8):
            e_xc.append(vec(lambda v, cc=cc, xc=xc: v.tt(xc[:, cc, :], xc[:, cc, :],
                                                         ps_a[:, 512:1024], OP.add),
                            waits=[e_bc, e_xt]))
        if A_STAGE < 3:
            e_tile_done[i] = (e_xc[-1], e_xc[-1], e_sums[-1]);  e_prev_stats = [e_xc[-1]];  continue
        if DUMP_XC:
            dma("gpsimd", x2_sp[col:col + 512, :].rearrange("(c p) d -> p c d", p=128),
                xc[:, 0:8, :], waits=e_xc)
        e_qkstat = []
        for cl in range(4):
            sc = sc0 + cl
            par = cl % 2
            pso = ps_a if par == 0 else ps_b
            q_ps = pso[:, 0:1024]
            k_ps = pso[:, 1024:2048]
            w_pre = list(e_xc)
            if cl >= 2:
                w_pre = [e_qkstat[cl - 2][-1], e_qkstat[cl - 2][-2]]
            e_mm = None
            for cc in range(8):
                lhs = (lambda xc=xc, cc=cc, cl=cl: xc[:, cc, 128 * cl:128 * (cl + 1)])()
                mm(lambda t, lhs=lhs, cc=cc, q_ps=q_ps: t.matmul(
                    q_ps[:, 0:512], lhs, t_wqk[:, cc, 0:512],
                    start=(cc == 0), stop=(cc == 7)), waits=w_pre if cc == 0 else [])
                mm(lambda t, lhs=lhs, cc=cc, q_ps=q_ps: t.matmul(
                    q_ps[:, 512:1024], lhs, t_wqk[:, cc, 512:1024],
                    start=(cc == 0), stop=(cc == 7)))
                mm(lambda t, lhs=lhs, cc=cc, k_ps=k_ps: t.matmul(
                    k_ps[:, 0:512], lhs, t_wqk[:, cc, 1024:1536],
                    start=(cc == 0), stop=(cc == 7)))
                e_mm = mm(lambda t, lhs=lhs, cc=cc, k_ps=k_ps: t.matmul(
                    k_ps[:, 512:1024], lhs, t_wqk[:, cc, 1536:2048],
                    start=(cc == 0), stop=(cc == 7)))
            if A_STAGE >= 31:
                e1 = act(lambda s, q_ps=q_ps, sc=sc: s.activation(
                    t_qksc[:, 0:1024], q_ps, AF.Identity, accum_out=st["sqm"][:, sc:sc + 1]), waits=[e_mm])
                if DUMP_XC and i == 0 and cl == 0:
                    e_dq = dma("gpsimd", ffn_sp[0:128, :], t_qksc[:, 0:1024], waits=[e1])
                    e_dw = dma("gpsimd", ffn_sp[128:256, :], t_wqk[:, 0, 0:1024])
                    e_dw2 = dma("gpsimd", ffn_sp[256:384, :], t_wqk[:, 7, 0:1024])
                    e1 = e_dq
                e2 = act(lambda s, k_ps=k_ps, sc=sc: s.activation(
                    t_qksc[:, 0:1024], k_ps, AF.Identity, accum_out=st["skm"][:, sc:sc + 1]))
            else:
                e1 = e2 = e_mm
            if A_STAGE >= 32:
                e3 = act(lambda s, q_ps=q_ps, sc=sc: s.activation(
                    t_qksc[:, 0:1024], q_ps, AF.Square, accum_out=st["sqq"][:, sc:sc + 1]), waits=[e_mm])
                e4 = act(lambda s, k_ps=k_ps, sc=sc: s.activation(
                    t_qksc[:, 0:1024], k_ps, AF.Square, accum_out=st["skk"][:, sc:sc + 1]))
            else:
                e3 = e4 = e_mm
            if A_STAGE >= 33:
                e5a = vec(lambda v, q_ps=q_ps: v.copy(t_qksc2[:, 0:1024], q_ps), waits=[e_mm, e4])
                e5 = vec(lambda v, k_ps=k_ps, sc=sc: v.stt(
                    t_qksc2[:, 0:1024], t_qksc2[:, 0:1024], 1.0, k_ps, OP.mult, OP.mult,
                    accum_out=st["sqk"][:, sc:sc + 1]), waits=[e5a])
            else:
                e5 = e_mm
            e_qkstat.append([e1, e2, e3, e4, e5])
        if A_STAGE < 4 or (30 <= A_STAGE < 40):
            e_tile_done[i] = (e_qkstat[3][4], e_qkstat[3][4], e_qkstat[3][3])
            e_prev_stats = e_qkstat[2] + e_qkstat[3];  continue
        e_vev = [None, None]
        e_vdma = []
        for vc in range(8):
            vps = ps_b[:, 512 * (vc % 2):512 * (vc % 2) + 512]
            if vc < 2:
                w_pre = e_qkstat[3]
            else:
                w_pre = [e_vev[vc % 2]]
            e_mm = None
            for cc in range(8):
                e_mm = mm(lambda t, vc=vc, cc=cc, vps=vps, xc=xc: t.matmul(
                    vps, t_wv[:, cc, 128 * vc:128 * (vc + 1)], xc[:, cc, :],
                    start=(cc == 0), stop=(cc == 7)), waits=w_pre if cc == 0 else [])
            e_ev = vec(lambda v, vc=vc, vps=vps: v.copy(t_vout[:, vc % 2, :], vps), waits=[e_mm])
            if DUMP_XC and i == 0 and vc == 0:
                e_ev = dma("gpsimd", ffn_sp[384:512, 0:512], t_vout[:, 0, :], waits=[e_ev])
            e_vev[vc % 2] = e_ev
            e_vdma.append(dma("scalar", vT_sp[128 * vc:128 * (vc + 1), col:col + 512],
                              t_vout[:, vc % 2, :], waits=[e_ev]))
        if A_STAGE < 5:
            e_tile_done[i] = (e_vev[0], e_vdma[-1], e_qkstat[3][3])
            e_prev_stats = e_qkstat[2] + e_qkstat[3] + [e_vev[0], e_vev[1]];  continue
        e_mps = None
        for cl in range(4):
            sc = sc0 + cl
            exm = vec(lambda v, cl=cl, sc=sc, xtm=xtm: v.ts(
                t_xm[:, 0:1024], xtm[:, cl, :], st["nmu1"][:, sc:sc + 1],
                st["rstd1"][:, sc:sc + 1], OP.add, OP.mult),
                waits=[e_vev[0], e_vev[1], e_rs] if cl == 0 else [e_mps])
            for half in range(2):
                e_mps = mm(lambda t, half=half, cl=cl: t.matmul(
                    ps_b[0:1, 1024 + 512 * half:1024 + 512 * (half + 1)], t_onescr[:],
                    t_xm[:, 512 * half:512 * (half + 1)],
                    start=(cl == 0), stop=(cl == 3)), waits=[exm])
        e_mac = vec(lambda v, b=b: v.tt(t_macc[0:1, 1024 * b:1024 * (b + 1)],
                                        t_macc[0:1, 1024 * b:1024 * (b + 1)],
                                        ps_b[0:1, 1024:2048], OP.add), waits=[e_mps])
        e_tile_done[i] = (e_mac, e_vdma[-1], e_qkstat[3][3])
        e_prev_stats = e_qkstat[2] + e_qkstat[3] + [e_mac]
        P.barrier()
    if PHASE_LIM <= 1:
        _finish(nc, P, VEC, locals());  return _replay(nc, P, VEC, locals(), es, sems)

    dma("sync", t_proj[:], proj_w.rearrange("(c p) o -> p c o", p=128))
    dma("sync", t_chw1[:], ch_w1.rearrange("(c p) o -> p c o", p=128))
    dma("sync", t_chw2[:], ch_w2.rearrange("(c p) o -> p c o", p=128))
    e_wgr = dma("sync", t_wgrow[:], wg_in.rearrange("a b -> b a"))
    e_pbr = dma("sync", t_bias_r[:], proj_b.rearrange("(o a) -> o a", o=1))
    e_swg = vec(lambda v: v.red(t_sc1[0:1, 0:1], t_wgrow[0:1, :], OP.add), waits=[e_wgr])
    e_swg2 = vec(lambda v: v.copy(t_smalls[0:1, 35:36], t_sc1[0:1, 0:1]), waits=[e_swg])
    e1 = mm(lambda t: t.matmul(ps_b[:, 0:64], t_onesrf[0:1, :], t_smalls[0:1, :], start=True, stop=True), waits=[e_swg2])
    vec(lambda v: v.copy(t_smbc[:], ps_b[:, 0:64]), waits=[e1])
    e1 = mm(lambda t: t.matmul(ps_b[:, 512:1024], t_onesrf[0:1, :], t_bias_r[0:1, 0:512], start=True, stop=True), waits=[e_pbr])
    e2 = mm(lambda t: t.matmul(ps_b[:, 1024:1536], t_onesrf[0:1, :], t_bias_r[0:1, 512:1024], start=True, stop=True))
    vec(lambda v: v.copy(t_biasbc[:], ps_b[:, 512:1536]), waits=[e1, e2])
    e1 = mm(lambda t: t.matmul(ps_b[:, 1536:2048], t_onesrf[0:1, :], t_wgrow[0:1, 0:512], start=True, stop=True))
    vec(lambda v: v.copy(t_wgbc[:, 0:512], ps_b[:, 1536:2048]), waits=[e1])
    e2 = mm(lambda t: t.matmul(ps_a[:, 0:512], t_onesrf[0:1, :], t_wgrow[0:1, 512:1024], start=True, stop=True))
    vec(lambda v: v.copy(t_wgbc[:, 512:1024], ps_a[:, 0:512]), waits=[e2])
    P.barrier()

    A = st
    sm = lambda k_: t_smbc[:, k_:k_ + 1]
    vec(lambda v: v.tt(A["scr1"][:], A["rstd1"][:], A["rstd1"][:], OP.mult))
    vec(lambda v: v.ts(A["sqm"][:], A["sqm"][:], 1.0 / D, None, OP.mult))
    vec(lambda v: v.ts(A["skm"][:], A["skm"][:], 1.0 / D, None, OP.mult))
    vec(lambda v: v.tt(A["sqm"][:], A["sqm"][:], A["rstd1"][:], OP.mult))
    vec(lambda v: v.tt(A["skm"][:], A["skm"][:], A["rstd1"][:], OP.mult))
    vec(lambda v: v.tt(A["sqq"][:], A["sqq"][:], A["scr1"][:], OP.mult))
    vec(lambda v: v.tt(A["skk"][:], A["skk"][:], A["scr1"][:], OP.mult))
    vec(lambda v: v.tt(A["sqk"][:], A["sqk"][:], A["scr1"][:], OP.mult))
    vec(lambda v: v.tt(A["scr1"][:], A["sqm"][:], A["sqm"][:], OP.mult))
    vec(lambda v: v.tt(A["scr2"][:], A["skm"][:], A["skm"][:], OP.mult))
    vec(lambda v: v.stt(A["sqq"][:], A["scr1"][:], -float(D), A["sqq"][:], OP.mult, OP.add))
    vec(lambda v: v.stt(A["skk"][:], A["scr2"][:], -float(D), A["skk"][:], OP.mult, OP.add))
    vec(lambda v: v.tt(t_sa[:], A["sqm"][:], A["skm"][:], OP.mult))
    vec(lambda v: v.stt(A["sqk"][:], t_sa[:], -float(D), A["sqk"][:], OP.mult, OP.add))
    vec(lambda v: v.ts(t_sb2[:], t_sa[:], 2.0, None, OP.mult))
    vec(lambda v: v.ts(t_sb2[:], t_sb2[:], sm(0), None, OP.add))
    vec(lambda v: v.ts(A["sqk"][:], A["sqk"][:], 2.0 * INVCM1, None, OP.mult))
    vec(lambda v: v.ts(A["sqk"][:], A["sqk"][:], sm(1), None, OP.add))
    vec(lambda v: v.tt(t_sb2[:], t_sb2[:], A["sqk"][:], OP.mult))
    vec(lambda v: v.tt(A["scr1"][:], A["scr1"][:], A["scr2"][:], OP.add))
    vec(lambda v: v.ts(A["scr1"][:], A["scr1"][:], sm(0), None, OP.add))
    vec(lambda v: v.tt(A["sqq"][:], A["sqq"][:], A["skk"][:], OP.add))
    vec(lambda v: v.ts(A["sqq"][:], A["sqq"][:], INVCM1, None, OP.mult))
    vec(lambda v: v.ts(A["sqq"][:], A["sqq"][:], sm(1), None, OP.add))
    vec(lambda v: v.tt(A["scr1"][:], A["scr1"][:], A["sqq"][:], OP.mult))
    vec(lambda v: v.ts(A["scr1"][:], A["scr1"][:], 1e-7, None, OP.add))
    e_r = vec(lambda v: v.recip(A["scr2"][:], A["scr1"][:]))
    vec(lambda v: v.tt(t_sb2[:], t_sb2[:], A["scr2"][:], OP.mult))
    e_ab = vec(lambda v: v.tt(A["ab"][:], t_sb2[:], t_sb2[:], OP.mult))

    e_wsmT = []
    for b in range(NB):
        bs = slice(32 * b, 32 * b + 32)
        e_t = mm(lambda t, bs=bs: t.transpose(ps_a[0:32, 0:128], A["ab"][:, bs], t_ident[:]),
                 waits=[e_ab] + (e_wsmT[-1:] if e_wsmT else []))
        e_c = vec(lambda v: v.copy(t_ode["a0"][:], ps_a[0:32, 0:128]), waits=[e_t])
        vec(lambda v: v.memset(t_ode["tmp"][:, 0:1], 0.0))
        e_z1 = vec(lambda v: v.copy(t_ode["tmp"][0:1, 1:2], t_ode["tmp"][0:1, 0:1]))
        e_sp = dma("scalar", ab_pad[b, 64:64 + N].rearrange("(c p) -> c p", p=128), t_ode["a0"][:], waits=[e_c])
        e_zl = dma("scalar", ab_pad[b, 63:64].rearrange("(o a) -> o a", o=1), t_ode["tmp"][0:1, 0:1], waits=[e_z1])
        e_zr = dma("scalar", ab_pad[b, 64 + N:65 + N].rearrange("(o a) -> o a", o=1), t_ode["tmp"][0:1, 0:1], waits=[e_z1])
        e_m1 = dma("sync", t_ode["am1"][:], ab_pad[b, 63:63 + N].rearrange("(c p) -> c p", p=128), waits=[e_sp, e_zl, e_zr])
        e_p1 = dma("sync", t_ode["ap1"][:], ab_pad[b, 65:65 + N].rearrange("(c p) -> c p", p=128), waits=[e_sp, e_zl, e_zr])
        e_h = []
        for o in range(4):
            hh = t_ode[f"hh{o}"]
            vec(lambda v, hh=hh, o=o: v.ts(hh[:], t_ode["am1"][:], t_smbc[0:32, 2 + 3 * o:3 + 3 * o], None, OP.mult), waits=[e_m1, e_p1])
            vec(lambda v, hh=hh, o=o: v.stt(hh[:], t_ode["a0"][:], t_smbc[0:32, 3 + 3 * o:4 + 3 * o], hh[:], OP.mult, OP.add))
            e_h.append(vec(lambda v, hh=hh, o=o: v.stt(hh[:], t_ode["ap1"][:], t_smbc[0:32, 4 + 3 * o:5 + 3 * o], hh[:], OP.mult, OP.add)))
        e_gs = []
        for o in range(4):
            e1_ = vec(lambda v, o=o: v.red(t_vred[0:32, 18 + o:19 + o], t_ode[f"hh{o}"][:], OP.add), waits=[e_h[o]])
            e2_ = act(lambda s, o=o: s.activation(t_ode["tmp"][:], t_ode[f"hh{o}"][:], AF.Square,
                                                  accum_out=t_vred[0:32, 22 + o:23 + o]), waits=[e_h[o]])
            e_gs += [e1_, e2_]
        e_gm = mm(lambda t: t.matmul(ps_a[0:1, 0:8], t_onescf[0:32, :], t_vred[0:32, 18:26], start=True, stop=True), waits=e_gs)
        e_cp = vec(lambda v: v.copy(t_sc1[0:1, 8:16], ps_a[0:1, 0:8]), waits=[e_gm])
        vec(lambda v: v.tt(t_sc1[0:1, 16:17], t_sc1[0:1, 8:9], t_sc1[0:1, 9:10], OP.add))
        vec(lambda v: v.tt(t_sc1[0:1, 17:18], t_sc1[0:1, 10:11], t_sc1[0:1, 11:12], OP.add))
        vec(lambda v: v.tt(t_sc1[0:1, 18:19], t_sc1[0:1, 12:13], t_sc1[0:1, 13:14], OP.add))
        vec(lambda v: v.tt(t_sc1[0:1, 19:20], t_sc1[0:1, 14:15], t_sc1[0:1, 15:16], OP.add))
        vec(lambda v: v.ts(t_sc1[0:1, 16:20], t_sc1[0:1, 16:20], 1.0 / (2 * N), None, OP.mult))
        vec(lambda v: v.tt(t_sc2[0:1, 0:2], t_sc1[0:1, 16:18], t_sc1[0:1, 16:18], OP.mult))
        vec(lambda v: v.tt(t_sc2[0:1, 0:2], t_sc1[0:1, 18:20], t_sc2[0:1, 0:2], OP.subtract))
        e_v2 = vec(lambda v: v.ts(t_sc2[0:1, 0:2], t_sc2[0:1, 0:2], EPS, None, OP.add))
        e_sqg = act(lambda s: s.activation(t_sc2[0:1, 2:4], t_sc2[0:1, 0:2], AF.Sqrt), waits=[e_v2])
        e_rg = vec(lambda v: v.recip(t_sc2[0:1, 4:6], t_sc2[0:1, 2:4]), waits=[e_sqg])
        for ch_ in range(4):
            g_ = ch_ // 2
            vec(lambda v, ch_=ch_, g_=g_: v.tt(t_sc2[0:1, 8 + ch_:9 + ch_], t_sc2[0:1, 4 + g_:5 + g_], t_smalls[0:1, 14 + ch_:15 + ch_], OP.mult))
            vec(lambda v, ch_=ch_, g_=g_: v.tt(t_sc2[0:1, 12 + ch_:13 + ch_], t_sc2[0:1, 8 + ch_:9 + ch_], t_sc1[0:1, 16 + g_:17 + g_], OP.mult))
            vec(lambda v, ch_=ch_: v.tt(t_sc2[0:1, 12 + ch_:13 + ch_], t_smalls[0:1, 18 + ch_:19 + ch_], t_sc2[0:1, 12 + ch_:13 + ch_], OP.subtract))
        e_ab8 = vec(lambda v: v.copy(t_sc2[0:1, 16:24], t_sc2[0:1, 8:16]))
        e_bc8 = mm(lambda t: t.matmul(ps_a[0:32, 16:24], t_onesrf[0:1, 0:32], t_sc2[0:1, 16:24], start=True, stop=True), waits=[e_ab8])
        e_od = vec(lambda v: v.copy(t_odes[:, 0:8], ps_a[0:32, 16:24]), waits=[e_bc8])
        e_hr = []
        for o in range(4):
            e_hr.append(act(lambda s, o=o: s.activation(
                t_ode[f"hh{o}"][:], t_ode[f"hh{o}"][:], AF.Relu,
                bias=t_odes[:, 4 + o:5 + o], scale=t_odes[:, o:o + 1]), waits=[e_od]))
        e_yy = vec(lambda v: v.memset(t_ode["yy"][:], 0.0))
        for o in range(4):
            e_sp2 = dma("scalar", hh_pad[b, o, 64:64 + N].rearrange("(c p) -> c p", p=128), t_ode[f"hh{o}"][:], waits=[e_hr[o]])
            dma("scalar", hh_pad[b, o, 63:64].rearrange("(o a) -> o a", o=1), t_ode["tmp"][0:1, 0:1])
            e_zz2 = dma("scalar", hh_pad[b, o, 64 + N:65 + N].rearrange("(o a) -> o a", o=1), t_ode["tmp"][0:1, 0:1])
            for tap, off in [(0, 63), (1, 64), (2, 65)]:
                e_ld = dma("sync", t_ode["am1"][:] if tap == 0 else (t_ode["a0"][:] if tap == 1 else t_ode["ap1"][:]),
                           hh_pad[b, o, off:off + N].rearrange("(c p) -> c p", p=128), waits=[e_sp2, e_zz2, e_yy])
                src = {0: "am1", 1: "a0", 2: "ap1"}[tap]
                e_yy = vec(lambda v, src=src, o=o, tap=tap: v.stt(
                    t_ode["yy"][:], t_ode[src][:], t_smbc[0:32, 22 + 3 * o + tap:23 + 3 * o + tap],
                    t_ode["yy"][:], OP.mult, OP.add), waits=[e_ld, e_yy])
        vec(lambda v: v.ts(t_ode["yy"][:], t_ode["yy"][:], t_smbc[0:32, 34:35], None, OP.add), waits=[e_yy])
        e_t2 = mm(lambda t, bs=bs: t.transpose(ps_a[0:32, 32:160], A["ab"][:, bs], t_ident[:]), waits=[e_od])
        e_c2 = vec(lambda v: v.tt(t_ode["yy"][:], t_ode["yy"][:], ps_a[0:32, 32:160], OP.add), waits=[e_t2])
        if DUMP_XC and b == 0:
            e_c2 = dma("gpsimd", ffn_sp[512:544, 0:128], t_ode["yy"][:], waits=[e_c2])
            e_hh = dma("gpsimd", ffn_sp[544:576, 0:128], t_ode["hh0"][:], waits=[e_c2])
        e_fa = act(lambda s: s.activation(t_ode["fa"][:], t_ode["yy"][:], AF.Sigmoid), waits=[e_c2])
        e_ex = act(lambda s: s.activation(t_ode["tmp"][:], t_ode["fa"][:], AF.Exp), waits=[e_fa])
        e_rd = vec(lambda v: v.red(t_vred[0:32, 26:27], t_ode["tmp"][:], OP.add), waits=[e_ex])
        e_sm = mm(lambda t: t.matmul(ps_a[0:1, 24:25], t_onescf[0:32, :], t_vred[0:32, 26:27], start=True, stop=True), waits=[e_rd])
        e_rc = vec(lambda v: v.copy(t_sc1[0:1, 20:21], ps_a[0:1, 24:25]), waits=[e_sm])
        e_rc2 = vec(lambda v: v.recip(t_sc1[0:1, 21:22], t_sc1[0:1, 20:21]), waits=[e_rc])
        e_bc2 = mm(lambda t: t.matmul(ps_a[0:32, 25:26], t_onesrf[0:1, 0:32], t_sc1[0:1, 21:22], start=True, stop=True), waits=[e_rc2])
        e_cb2 = vec(lambda v: v.copy(t_vred[0:32, 27:28], ps_a[0:32, 25:26]), waits=[e_bc2])
        e_w = vec(lambda v: v.ts(t_ode["fa"][:], t_ode["tmp"][:], t_vred[0:32, 27:28], None, OP.mult), waits=[e_cb2])
        e_t3 = mm(lambda t: t.transpose(ps_b[0:128, 0:32], t_ode["fa"][:], t_ident[0:32, 0:32]), waits=[e_w])
        e_wT = vec(lambda v, bs=bs: v.copy(st["ssc"][:, bs], ps_b[0:128, 0:32]), waits=[e_t3])
        e_wsmT.append(e_wT)
    e_ssc = vec(lambda v: v.tt(st["ssc"][:], st["ssc"][:], st["rstd1"][:], OP.mult), waits=e_wsmT)
    P.barrier()
    if PHASE_LIM <= 2:
        _finish(nc, P, VEC, locals());  return _replay(nc, P, VEC, locals(), es, sems)

    vec(lambda v: v.ts(t_macc[:], t_macc[:], 1.0 / N, None, OP.mult))
    e_mT = []
    for b in range(NB):
        for cc in range(8):
            e_t = mm(lambda t, b=b, cc=cc: t.transpose(
                ps_a[0:128, 0:1], t_macc[0:1, 1024 * b + 128 * cc:1024 * b + 128 * (cc + 1)], t_ident[0:1, 0:1]),
                waits=e_mT[-1:])
            e_mT.append(vec(lambda v, b=b, cc=cc: v.copy(t_mrow[:, cc, b:b + 1], ps_a[0:128, 0:1]), waits=[e_t]))
    e_b1 = dma("sync", t_vred[:, 18:20], ch_b1.rearrange("(c p) -> p c", p=128), allow_slow_non_contiguous=True)
    e_b2 = dma("sync", t_vred[:, 20:28], ch_b2.rearrange("(c p) -> p c", p=128), allow_slow_non_contiguous=True)
    for hc in range(2):
        e_mm2 = None
        for cc in range(8):
            e_mm2 = mm(lambda t, hc=hc, cc=cc: t.matmul(
                ps_a[0:128, 4 + 2 * hc:6 + 2 * hc], t_chw1[:, cc, 128 * hc:128 * (hc + 1)],
                t_mrow[:, cc, :], start=(cc == 0), stop=(cc == 7)), waits=e_mT + [e_b1])
        act(lambda s, hc=hc: s.activation(t_hrel[:, hc, :], ps_a[0:128, 4 + 2 * hc:6 + 2 * hc],
                                          AF.Relu, bias=t_vred[:, 18 + hc:19 + hc]), waits=[e_mm2])
    e_hr2 = P.emit("scalar", lambda s: s.nop(), inc="ac")
    for oc in range(8):
        e_mm3 = None
        for hc in range(2):
            e_mm3 = mm(lambda t, oc=oc, hc=hc: t.matmul(
                ps_a[0:128, 8 + 2 * oc:10 + 2 * oc], t_chw2[:, hc, 128 * oc:128 * (oc + 1)],
                t_hrel[:, hc, :], start=(hc == 0), stop=(hc == 1)), waits=[e_hr2, e_b2])
        act(lambda s, oc=oc: s.activation(t_chT[:, oc, :], ps_a[0:128, 8 + 2 * oc:10 + 2 * oc],
                                          AF.Sigmoid, bias=t_vred[:, 20 + oc:21 + oc]), waits=[e_mm3])
    P.barrier()
    if PHASE_LIM <= 3:
        _finish(nc, P, VEC, locals());  return _replay(nc, P, VEC, locals(), es, sems)

    e_tile_done2 = [()] * NT
    prev_attn_rd = [[None] * 4, [None] * 4]
    prev_x2_rd = [None] * 4
    prev_ln2_dma = [None]
    prev_x2t = [None]
    dstate = [None] * NT
    x2_last = [None]
    rs2_last = [None]

    def d_stage0(i):
        b, t0 = tile_bt(i)
        col = b * N + t0
        sc0 = b * 32 + (t0 // 128)
        tp = i % 2
        wv = [dstate[i - 2]["e_mmlast"]] if i >= 2 else []
        wx = list(dstate[i - 2]["e_x2ev"]) if i >= 2 else []
        e_vt = dma("sync", t_xt[i % 2][:], vT_sp[:, col:col + 512].rearrange("(c p) t -> p c t", p=128), waits=wv)
        e_xtm = dma("sync", t_xtm[i % 2][:], x_tm[col:col + 512, :].rearrange("(c p) d -> p c d", p=128), waits=wx)
        vt = t_xt[i % 2]
        xtm = t_xtm[i % 2]
        e_ol = []
        for cc in range(8):
            e_ol.append(vec(lambda v, cc=cc, vt=vt, b=b: v.ts(
                vt[:, cc, :], vt[:, cc, :], t_chT[:, cc, b:b + 1], None, OP.mult),
                waits=[e_vt]))
        dstate[i] = dict(b=b, col=col, sc0=sc0, s4=slice(sc0, sc0 + 4), tp=tp,
                         ablk=(t_attnblk if tp == 0 else t_attnblkB),
                         vt=vt, xtm=xtm, e_xtm=e_xtm, e_ol=e_ol, e_x2ev=[])

    def d_stage1(i):
        ds = dstate[i]
        sc0, tp, ablk, vt = ds["sc0"], ds["tp"], ds["ablk"], ds["vt"]
        e_ol = ds["e_ol"]
        e_pr = []
        e_ev4 = []
        prev_ev4 = dstate[i - 1]["e_ev4"] if i >= 1 else []
        for cl in range(4):
            par = cl % 2
            pps = ps_a[:, 1024:2048] if par == 0 else ps_b[:, 0:1024]
            w_pre = list(e_ol) + prev_ev4
            if cl >= 2:
                w_pre += [e_pr[cl - 2]]
            e_mm4 = None
            for cc in range(8):
                for ot in range(2):
                    e_mm4 = mm(lambda t, cc=cc, cl=cl, ot=ot, pps=pps, vt=vt: t.matmul(
                        pps[:, 512 * ot:512 * (ot + 1)], vt[:, cc, 128 * cl:128 * (cl + 1)],
                        t_proj[:, cc, 512 * ot:512 * (ot + 1)],
                        start=(cc == 0), stop=(cc == 7)), waits=w_pre if (cc == 0 and ot == 0) else [])
            sc = sc0 + cl
            e_ev = vec(lambda v, cl=cl, sc=sc, pps=pps, ablk=ablk: v.stt(
                ablk[:, cl, :], pps, st["ssc"][:, sc:sc + 1], t_biasbc[:],
                OP.mult, OP.add, accum_out=st["scr1"][:, sc:sc + 1]),
                waits=[e_mm4, prev_attn_rd[tp][cl]])
            e_ev4.append(e_ev)
            e_pr.append(e_ev)
            eb = act(lambda s, cl=cl, sc=sc, ablk=ablk: s.activation(
                t_qksc[:, 0:1024], ablk[:, cl, :], AF.Square,
                accum_out=st["scr2"][:, sc:sc + 1]), waits=[e_ev])
            e_pr[-1] = eb
        ds["e_pr"] = e_pr
        ds["e_ev4"] = e_ev4[2:]
        ds["e_mmlast"] = e_mm4

    def d_stage2(i):
        ds = dstate[i]
        col, sc0, s4, tp, ablk = ds["col"], ds["sc0"], ds["s4"], ds["tp"], ds["ablk"]
        xtm, e_xtm, e_pr = ds["xtm"], ds["e_xtm"], ds["e_pr"]
        ev = vec(lambda v, s4=s4: v.ts(st["scr1"][:, s4], st["scr1"][:, s4], 1.0 / D, None, OP.mult), waits=e_pr)
        vec(lambda v, s4=s4: v.ts(st["scr2"][:, s4], st["scr2"][:, s4], 1.0 / D, None, OP.mult))
        vec(lambda v, s4=s4: v.tt(t_sa[:, s4], st["scr1"][:, s4], st["scr1"][:, s4], OP.mult))
        vec(lambda v, s4=s4: v.tt(st["scr2"][:, s4], st["scr2"][:, s4], t_sa[:, s4], OP.subtract))
        e_var = vec(lambda v, s4=s4: v.ts(st["scr2"][:, s4], st["scr2"][:, s4], EPS, None, OP.add))
        e_sq = act(lambda s, s4=s4: s.activation(t_sb2[:, s4], st["scr2"][:, s4], AF.Sqrt), waits=[e_var])
        e_rs = vec(lambda v, s4=s4: v.recip(t_sb2[:, s4], t_sb2[:, s4]), waits=[e_sq])
        e_nm = vec(lambda v, s4=s4: v.ts(st["scr1"][:, s4], st["scr1"][:, s4], -1.0, None, OP.mult), waits=[e_rs])
        e_nb2 = vec(lambda v, s4=s4: v.tt(st["scr1"][:, s4], st["scr1"][:, s4], t_sb2[:, s4], OP.mult))
        e_ch_done = []
        for cl in range(4):
            sc = sc0 + cl
            e_l = act(lambda s, cl=cl, sc=sc, ablk=ablk: s.activation(
                t_x2t[:, 0:1024], ablk[:, cl, :], AF.Identity,
                bias=st["scr1"][:, sc:sc + 1], scale=t_sb2[:, sc:sc + 1]),
                waits=[e_nb2, prev_x2t[0]])
            prev_attn_rd[tp][cl] = e_l
            e_x2 = vec(lambda v, cl=cl, sc=sc, xtm=xtm: v.stt(
                t_x2blk[:, cl, :], t_x2t[:, 0:1024], 1.0, xtm[:, cl, :],
                OP.mult, OP.add, accum_out=st["sqm"][:, sc:sc + 1]),
                waits=[e_xtm, prev_x2_rd[cl], e_l])
            ds["e_x2ev"].append(e_x2)
            prev_x2t[0] = e_x2
            e_st = dma("scalar", out_t[col + 128 * cl:col + 128 * (cl + 1), :], t_x2blk[:, cl, :], waits=[e_x2])
            x2_last[0] = e_st
            if DBG:
                dma("scalar", x2_sp[col + 128 * cl:col + 128 * (cl + 1), :], t_x2blk[:, cl, :], waits=[e_x2])
            eb = act(lambda s, cl=cl, sc=sc: s.activation(
                t_qksc[:, 0:1024], t_x2blk[:, cl, :], AF.Square,
                accum_out=st["skm"][:, sc:sc + 1]), waits=[e_x2])
            e_ch_done.append((e_st, eb))
        e_prs = [tup[1] for tup in e_ch_done]
        ev = vec(lambda v, s4=s4: v.ts(st["sqm"][:, s4], st["sqm"][:, s4], 1.0 / D, None, OP.mult), waits=e_prs)
        vec(lambda v, s4=s4: v.ts(st["nmu2"][:, s4], st["sqm"][:, s4], -1.0, None, OP.mult))
        vec(lambda v, s4=s4: v.tt(t_sa[:, s4], st["sqm"][:, s4], st["sqm"][:, s4], OP.mult))
        vec(lambda v, s4=s4: v.ts(st["skm"][:, s4], st["skm"][:, s4], 1.0 / D, None, OP.mult))
        vec(lambda v, s4=s4: v.tt(st["skm"][:, s4], st["skm"][:, s4], t_sa[:, s4], OP.subtract))
        e_var2 = vec(lambda v, s4=s4: v.ts(st["skm"][:, s4], st["skm"][:, s4], EPS, None, OP.add))
        e_sq2 = act(lambda s, s4=s4: s.activation(st["rstd2"][:, s4], st["skm"][:, s4], AF.Sqrt), waits=[e_var2])
        e_rs2 = vec(lambda v, s4=s4: v.recip(st["rstd2"][:, s4], st["rstd2"][:, s4]), waits=[e_sq2])
        rs2_last[0] = e_rs2
        e_zi = []
        for cl in range(4):
            sc = sc0 + cl
            e_l2 = vec(lambda v, cl=cl, sc=sc: v.ts(
                t_ln2t[:, 0:1024], t_x2blk[:, cl, :], st["nmu2"][:, sc:sc + 1],
                st["rstd2"][:, sc:sc + 1], OP.add, OP.mult), waits=[e_rs2, prev_ln2_dma[0]])
            e_sl = dma("scalar", ln2_sp[col + 128 * cl:col + 128 * (cl + 1), :], t_ln2t[:, 0:1024], waits=[e_l2])
            prev_ln2_dma[0] = e_sl
            e_zr = vec(lambda v, cl=cl, sc=sc: v.stt(
                t_zscr[:, 0:1024], t_x2blk[:, cl, :], 1.0, t_wgbc[:], OP.mult, OP.mult,
                accum_out=st["zz"][:, sc:sc + 1]))
            prev_x2_rd[cl] = e_zr
            e_zi.append((e_sl, e_zr))
        e_tile_done2[i] = (e_zi[3][0], e_zi[3][1], e_ch_done[3][0])

    for i in range(NT):
        d_stage0(i)
        if i >= 1:
            d_stage2(i - 1)
        d_stage1(i)
    d_stage2(NT - 1)
    e_rs2 = rs2_last[0]
    e_x2_last = x2_last[0]
    e_z1 = vec(lambda v: v.ts(t_sa[:], st["nmu2"][:], t_smbc[:, 35:36], None, OP.mult))
    e_z2 = vec(lambda v: v.tt(st["zz"][:], st["zz"][:], t_sa[:], OP.add), waits=[e_z1])
    e_z3 = vec(lambda v: v.tt(st["zz"][:], st["zz"][:], st["rstd2"][:], OP.mult), waits=[e_rs2])
    P.barrier()
    if PHASE_LIM <= 4:
        _finish(nc, P, VEC, locals());  return _replay(nc, P, VEC, locals(), es, sems)

    vec(lambda v: v.memset(t_lo[:], -64.0))
    vec(lambda v: v.memset(t_hi[:], 64.0))
    vec(lambda v: v.memset(t_tm[:], 0.0))
    vec(lambda v: v.memset(t_tbc[:], 0.0))
    e_big = vec(lambda v: v.memset(t_bigt[:], BIGF))
    e_zz = vec(lambda v: v.memset(t_zeroE[:], 0.0))
    for r in range(ROUNDS):
        e_pc = None
        for b in range(NB):
            bs = slice(32 * b, 32 * b + 32)
            vec(lambda v, b=b, bs=bs: v.ts(t_mask[:, bs], st["zz"][:, bs], t_tbc[:, b:b + 1], None, OP.is_gt))
            e_pc = vec(lambda v, b=b, bs=bs: v.red(t_pcnt[:, b:b + 1], t_mask[:, bs], OP.add))
        e_cm = mm(lambda t: t.matmul(ps_a[0:1, 0:2], t_onescf[:], t_pcnt[:], start=True, stop=True), waits=[e_pc])
        e_cc = vec(lambda v: v.copy(t_cn[:], ps_a[0:1, 0:2]), waits=[e_cm])
        vec(lambda v: v.ts(t_pr2[:], t_cn[:], float(C), None, OP.is_ge))
        vec(lambda v: v.cpred(t_lo[:], t_pr2[:], t_tm[:]))
        vec(lambda v: v.ts(t_pr2[:], t_cn[:], float(C), None, OP.is_lt))
        vec(lambda v: v.cpred(t_hi[:], t_pr2[:], t_tm[:]))
        vec(lambda v: v.tt(t_tm[:], t_lo[:], t_hi[:], OP.add))
        e_t = vec(lambda v: v.ts(t_tm[:], t_tm[:], 0.5, None, OP.mult))
        e_bc = mm(lambda t: t.matmul(ps_a[0:128, 4:6], t_onesrf[0:1, :], t_tm[0:1, :], start=True, stop=True), waits=[e_t])
        vec(lambda v: v.copy(t_tbc[:], ps_a[0:128, 4:6]), waits=[e_bc])
    e_lo = vec(lambda v: v.copy(t_tm[:], t_lo[:]))
    e_bc = mm(lambda t: t.matmul(ps_a[0:128, 4:6], t_onesrf[0:1, :], t_tm[0:1, :], start=True, stop=True), waits=[e_lo])
    vec(lambda v: v.copy(t_tbc[:], ps_a[0:128, 4:6]), waits=[e_bc])
    e_pc = None
    for b in range(NB):
        bs = slice(32 * b, 32 * b + 32)
        vec(lambda v, b=b, bs=bs: v.ts(t_mask[:, bs], st["zz"][:, bs], t_tbc[:, b:b + 1], None, OP.is_gt))
        e_pc = vec(lambda v, b=b, bs=bs: v.red(t_pcnt[:, b:b + 1], t_mask[:, bs], OP.add))
    e_off = mm(lambda t: t.matmul(ps_a[0:128, 8:10], t_tri[:], t_pcnt[:], start=True, stop=True), waits=[e_pc])
    e_tot = mm(lambda t: t.matmul(ps_a[0:1, 0:2], t_onescf[:], t_pcnt[:], start=True, stop=True))
    vec(lambda v: v.copy(t_poff[:], ps_a[0:128, 8:10]), waits=[e_off])
    e_cc = vec(lambda v: v.copy(t_cn[:], ps_a[0:1, 0:2]), waits=[e_tot])
    e_cmin = vec(lambda v: v.ts(t_cn[:], t_cn[:], float(C), None, OP.min), waits=[e_cc])
    e_cb = mm(lambda t: t.matmul(ps_a[0:128, 12:14], t_onesrf[0:1, :], t_cn[0:1, :], start=True, stop=True), waits=[e_cmin])
    e_cbv = vec(lambda v: v.copy(t_cntb[:], ps_a[0:128, 12:14]), waits=[e_cb])
    vec(lambda v: v.copy(t_sa[:], t_mask[:]))
    cur, nxt = t_sa, t_sb2
    for sh in [1, 2, 4, 8, 16]:
        for b in range(NB):
            o = 32 * b
            vec(lambda v, cur=cur, nxt=nxt, o=o, sh=sh: v.copy(nxt[:, o:o + sh], cur[:, o:o + sh]))
            vec(lambda v, cur=cur, nxt=nxt, o=o, sh=sh: v.tt(nxt[:, o + sh:o + 32], cur[:, o + sh:o + 32], cur[:, o:o + 32 - sh], OP.add))
        cur, nxt = nxt, cur
    vec(lambda v, cur=cur: v.tt(cur[:], cur[:], t_mask[:], OP.subtract))
    for b in range(NB):
        bs = slice(32 * b, 32 * b + 32)
        vec(lambda v, cur=cur, b=b, bs=bs: v.ts(st["dest"][:, bs], cur[:, bs], t_poff[:, b:b + 1], None, OP.add))
        if b == 1:
            vec(lambda v, bs=bs: v.ts(st["dest"][:, bs], st["dest"][:, bs], float(C), None, OP.add))
    vec(lambda v: v.ts(t_umask[:], t_mask[:], 0.5, None, OP.is_lt))
    vec(lambda v: v.cpred(st["dest"][:], t_umask[:], t_bigt[:]), waits=[e_big])
    for b in range(NB):
        bs = slice(32 * b, 32 * b + 32)
        vec(lambda v, b=b, bs=bs: v.ts(t_umask[:, bs], st["dest"][:, bs], float((b + 1) * C), None, OP.is_ge))
        vec(lambda v, bs=bs: v.cpred(st["dest"][:, bs], t_umask[:, bs], t_bigt[:, bs]))
    e_di = vec(lambda v: v.copy(t_desti[:], st["dest"][:]))
    e_tk = None
    for k4 in range(4):
        e_tk = vec(lambda v, k4=k4: v.ts(t_tok64[:, 16 * k4:16 * (k4 + 1)], t_slotid[:],
                                         2048.0 * k4, None, OP.add))
    e_b32 = vec(lambda v: v.memset(t_big32[:], BIGF))
    e_ti = [dma("gpsimd", tokid_sp.rearrange("(c p) o -> c (p o)", p=128), t_big32[:], waits=[e_b32])]
    e_kb = []
    for b in range(NB):
        for kc in range(16):
            cix = 16 * b + kc
            vec(lambda v, cix=cix, kc=kc, b=b: v.ts(t_umask[:, cix:cix + 1], t_slotid[:, kc:kc + 1], t_cntb[:, b:b + 1], None, OP.is_ge), waits=[e_cbv])
            vec(lambda v, cix=cix: v.memset(t_kbias[:, cix:cix + 1], 0.0))
            e_kb.append(vec(lambda v, cix=cix: v.cpred(t_kbias[:, cix:cix + 1], t_umask[:, cix:cix + 1], t_negt[:])))
    P.barrier()

    for j in range(32):
        dma("gpsimd", comp[128 * j:128 * (j + 1), :], t_zeroE[:])
    e_breg = [None]
    e_breg2 = [None]

    def setup_breg(g):
        r = g.alloc_register()
        g.reg_mov(r, NB * C - 1)
        e_breg[0] = r
        r2 = g.alloc_register()
        g.reg_mov(r2, NB * N - 1)
        e_breg2[0] = r2
        return None
    P.emit("gpsimd", lambda g: setup_breg(g))
    e_sc_chain = [None, None]
    rows_b = [t_rows, t_rows1]
    dcol_b = [t_dcol, t_dcol1]
    for j in range(64):
        b, cix = j // 32, j % 32
        col = b * N + 128 * cix
        scc = b * 32 + cix
        par = j % 2
        rows, dcol = rows_b[par], dcol_b[par]
        e_r = dma("gpsimd", rows[:], ln2_sp[col:col + 128, :], waits=[e_sc_chain[par]])
        e_dc = gps(lambda g, scc=scc, dcol=dcol: g.tensor_copy(dcol[:], t_desti[:, scc:scc + 1]))
        e_sc_chain[par] = P.emit("gpsimd", lambda g, rows=rows, dcol=dcol: g.indirect_dma_start(
            comp, IndirectOffsetOnAxis(dcol[:], 0), rows[:], None,
            bounds_check=e_breg[0], oob_is_err=False), waits=[e_r, e_dc], inc="dmg", k=16)
        if j == 31:
            e_sc_b0 = list(e_sc_chain)
    e_sc_prev = e_sc_chain
    e_sc_bybatch = [e_sc_b0, e_sc_chain]
    P.barrier()
    if PHASE_LIM <= 5:
        _finish(nc, P, VEC, locals());  return _replay(nc, P, VEC, locals(), es, sems)

    if not SKIP_EF:
        _phase_E_F(nc, P, VEC, locals())
    _finish(nc, P, VEC, locals())
    return _replay(nc, P, VEC, locals(), es, sems)


def _finish(nc, P, VEC, env):
    if DBG:
        dma, st, dbg_st, t_macc, t_gden = env["dma"], env["st"], env["dbg_st"], env["t_macc"], env["t_gden"]
        for di, nm in enumerate(["mu1", "nmu1", "rstd1", "sqm", "skm", "sqq", "skk",
                                 "sqk", "ab", "ssc", "nmu2", "rstd2", "zz", "dest"]):
            dma("scalar", dbg_st[di], st[nm][:])
        dma("scalar", dbg_st[14].rearrange("p c -> (p c)")[0:2048].rearrange("(o a) -> o a", o=1), t_macc[0:1, :])
        dma("scalar", dbg_st[15][:, 0:16], t_gden[:])
    P.emit("sync", lambda s: s.nop(), waits=[("dmsc", P.counts.get("dmsc", 0)),
                                             ("dmg", P.counts.get("dmg", 0)),
                                             ("dms", P.counts.get("dms", 0))])
    P.barrier()


def _replay(nc, P, VEC, env, es, sems):
    P.finalize()
    with nc.Block() as block:
        def section(eng_name):
            def run(eng):
                if eng_name == "vector":
                    VEC[0] = Vec(eng)
                for fn, waits, inc, k, v in P.ops[eng_name]:
                    if fn == "BARRIER":
                        eng.sem_inc(sems["bar"], 1)
                        eng.wait_ge(sems["bar"], inc)
                        if eng_name == "vector" and VEC[0] is not None:
                            VEC[0].dirty.clear()
                        continue
                    for (s, vv) in waits:
                        eng.wait_ge(sems[s], vv)
                    ins = fn(eng)
                    if inc is not None and ins is not None:
                        ins.then_inc(sems[inc], k)
                    elif inc is not None and ins is None:
                        eng.sem_inc(sems[inc], k)
            return run

        block.sync(section("sync"))
        block.scalar(section("scalar"))
        block.vector(section("vector"))
        block.tensor(section("tensor"))
        block.gpsimd(section("gpsimd"))

    es.close()
    return nc


def _phase_E_F(nc, P, VEC, env):
    g = env
    sems = g["sems"]
    t_l2sT, t_wffn, t_vffn, t_aoT, t_expS = g["t_l2sT"], g["t_wffn"], g["t_vffn"], g["t_aoT"], g["t_expS"]
    t_vffn4 = g["t_vffn4"]
    t_kh, t_qh, t_wo2, t_orow = g["t_kh"], g["t_qh"], g["t_wo2"], g["t_orow"]
    t_rows, t_facc, t_fx2, t_drow = g["t_rows"], g["t_facc"], g["t_fx2"], g["t_drow"]
    t_kbias, t_gden, t_gsel, t_wgbf = g["t_kbias"], g["t_gden"], g["t_gsel"], g["t_wgbf"]
    t_onesall, t_ident, t_desti = g["t_onesall"], g["t_ident"], g["t_desti"]
    t_qksc, t_dcol = g["t_qksc"], g["t_dcol"]
    st = g["st"]
    ps_a, ps_b = g["ps_a"], g["ps_b"]
    comp, qT_sp, kT_sp, ffn_sp, x2_sp, out_t = g["comp"], g["qT_sp"], g["kT_sp"], g["ffn_sp"], g["x2_sp"], g["out_t"]
    vT_sp = g["vT_sp"]
    wo_bf = g["wo_bf"]
    t_lnrow = g["t_lnrow"]
    wq_in, wk_in, wv_in, wo_in = g["wq_in"], g["wk_in"], g["wv_in"], g["wo_in"]
    dma, dmat, mm, act, vec, gps = g["dma"], g["dmat"], g["mm"], g["act"], g["vec"], g["gps"]
    e_breg = g["e_breg"]
    e_breg2 = g["e_breg2"]
    tokid_sp, t_stok, t_stokT = g["tokid_sp"], g["t_stok"], g["t_stokT"]
    t_tok64, t_dcol2 = g["t_tok64"], g["t_dcol2"]
    e_x2_last = g["e_x2_last"]
    e_sc_last = list(g["e_sc_prev"])
    e_tok_last = None
    for j in range(64):
        scc = j
        e_dc2 = gps(lambda gg, scc=scc: gg.tensor_copy(t_dcol2[:], t_desti[:, scc:scc + 1]),
                    waits=(g["e_ti"] + [g["e_tk"], g["e_di"]]) if j == 0 else [])
        e_tok_last = P.emit("gpsimd", lambda gg, scc=scc: gg.indirect_dma_start(
            tokid_sp, IndirectOffsetOnAxis(t_dcol2[:], 0), t_tok64[:, scc:scc + 1], None,
            bounds_check=g["e_breg"][0], oob_is_err=False),
            waits=[e_dc2], inc="dmg", k=16)

    g["dbg_evs"] = []
    e_prev_fdma = None
    for b in range(NB):
        if b > 0:
            P.barrier()
        e_tl = []
        for cc in range(8):
            e_tl.append(dmat("sync", t_l2sT[:, cc, :], comp[b * C:(b + 1) * C, 128 * cc:128 * (cc + 1)],
                             waits=list(g["dbg_evs"]) + list(g["e_sc_bybatch"][b])))
        e_on = [vec(lambda v: v.memset(t_vffn[:, :, :], 1.0))]
        e_zm = None
        for qt in range(4):
            for cc in range(8):
                e_zm = mm(lambda t, qt=qt, cc=cc: t.matmul(
                    ps_b[0:1, 512 * qt:512 * (qt + 1)], t_wgbf[:, cc, :],
                    t_l2sT[:, cc, 512 * qt:512 * (qt + 1)],
                    start=(cc == 0), stop=(cc == 7)), waits=e_tl)
        e_gs = act(lambda s: s.activation(t_gsel[0:1, :], ps_b[0:1, 0:2048], AF.Sigmoid), waits=[e_zm])
        if b == 0 and DBG:
            P.emit("gpsimd", lambda g: g.dma_start(vT_sp[0:1, 0:2048], t_gsel[0:1, :]), waits=[e_gs], inc="dmg", k=16)
            P.emit("gpsimd", lambda g: g.dma_start(vT_sp[1:3, 0:2048], t_l2sT[0:2, 0, :]), waits=list(e_tl), inc="dmg", k=16)
            P.emit("gpsimd", lambda g: g.dma_start(vT_sp[3:5, 0:2048], t_l2sT[0:2, 7, :]), waits=list(e_tl), inc="dmg", k=16)
        e_gt = []
        for kc in range(16):
            e_t = mm(lambda t, kc=kc: t.transpose(ps_a[0:128, 0:1],
                                                  t_gsel[0:1, 128 * kc:128 * (kc + 1)], t_ident[0:1, 0:1]),
                     waits=[e_gs] + ([e_gt[-1]] if e_gt else []))
            e_gt.append(vec(lambda v, kc=kc: v.copy(t_gden[:, kc:kc + 1], ps_a[0:128, 0:1]), waits=[e_t]))
        e_spill = []
        e_last_wmm = e_zm
        for w_i, (w_dram, dst) in enumerate([(wq_in, qT_sp), (wk_in, kT_sp)]):
            e_w = dma("gpsimd", t_wffn[:], w_dram.rearrange("(c p) o -> p c o", p=128),
                      waits=[e_last_wmm])
            e_par = [e_gt[-1], e_gt[-1]]
            for oc in range(8):
                par = oc % 2
                pp = ps_a[:, 0:512] if par == 0 else ps_a[:, 512:1024]
                for qt in range(4):
                    e_mm = None
                    for cc in range(8):
                        e_mm = mm(lambda t, oc=oc, qt=qt, cc=cc, pp=pp: t.matmul(
                            pp, t_wffn[:, cc, 128 * oc:128 * (oc + 1)],
                            t_l2sT[:, cc, 512 * qt:512 * (qt + 1)],
                            start=(cc == 0), stop=(cc == 7)),
                            waits=([e_w] + e_tl + [e_par[par]]) if cc == 0 else [])
                    e_ev = act(lambda s, pp=pp: s.activation(t_qksc[:, 0:512], pp, AF.Identity),
                               waits=[e_mm] + (e_spill[-1:] if e_spill else []))
                    e_par[par] = e_ev
                    e_last_wmm = e_mm
                    e_spill.append(dma("scalar", dst[b, 128 * oc:128 * (oc + 1), 512 * qt:512 * (qt + 1)],
                                       t_qksc[:, 0:512], waits=[e_ev]))
        e_wv = dma("gpsimd", t_wffn[:], wv_in.rearrange("(c p) o -> p c o", p=128),
                   waits=[e_last_wmm])
        e_vgrp = []
        for kc in range(16):
            for ot in range(2):
                gi = 2 * kc + ot
                par = gi % 2
                pp = ps_b[:, 0:512] if par == 0 else ps_b[:, 512:1024]
                w0 = [e_wv, e_gs] + e_on
                if gi >= 2:
                    w0 = list(e_vgrp[gi - 2])
                e_mm = None
                for cc in range(8):
                    e_mm = mm(lambda t, kc=kc, ot=ot, cc=cc, pp=pp: t.matmul(
                        pp, t_l2sT[:, cc, 128 * kc:128 * (kc + 1)],
                        t_wffn[:, cc, 512 * ot:512 * (ot + 1)],
                        start=(cc == 0), stop=(cc == 7)), waits=w0 if cc == 0 else [])
                grp = [vec(lambda v, kc=kc, ot=ot, pp=pp: v.copy(
                    t_vffn4[:, kc, 8 * ot:8 * (ot + 1), 0:64], pp[:, 0:512]), waits=[e_mm])]
                e_vgrp.append(grp)
        e_wo0 = dma("gpsimd", wo_bf, wo_in)
        e_wo = dma("sync", t_wo2[:], wo_bf.rearrange("(h p) o -> p h o", p=64), waits=[e_wo0])
        e_prev_av = e_vgrp[-1][-1]
        e_na = e_vgrp[-1][-1]
        e_prev_scores = None
        e_prev_avmm = None
        for h in range(H):
            e_kl = dma("sync", t_kh[:], kT_sp[b, DH * h:DH * (h + 1), :],
                       waits=[e_spill[-1], e_prev_scores])
            e_ql = dma("sync", t_qh[:], qT_sp[b, DH * h:DH * (h + 1), :],
                       waits=[e_spill[-1], e_prev_scores])
            e_qt_av = []
            for qt in range(4):
                e_ex = []
                for kc in range(16):
                    par = kc % 2
                    sps = ps_a[:, 0:512] if par == 0 else ps_a[:, 512:1024]
                    e_s = mm(lambda t, kc=kc, qt=qt, sps=sps: t.matmul(
                        sps, t_kh[:, 128 * kc:128 * (kc + 1)], t_qh[:, 512 * qt:512 * (qt + 1)],
                        start=True, stop=True),
                        waits=[e_kl, e_ql] + ([e_ex[kc - 2]] if kc >= 2 else []))
                    e_ex.append(act(lambda s, kc=kc, b=b, sps=sps: s.activation(
                        t_expS[:, kc, :], sps, AF.Exp,
                        bias=t_kbias[:, 16 * b + kc:16 * b + kc + 1],
                        scale=0.125), waits=[e_s, e_prev_avmm]))
                e_prev_scores = e_ex[-1]
                e_mm = None
                for kc in range(16):
                    e_mm = mm(lambda t, kc=kc, h=h: t.matmul(
                        ps_a[0:65, 1024:1536], t_vffn[:, kc, 65 * h:65 * (h + 1)],
                        t_expS[:, kc, :], start=(kc == 0), stop=(kc == 15)),
                        waits=[e_ex[kc]] + ([e_prev_av] if kc == 0 else []))
                e_prev_avmm = e_mm
                e_a1 = vec(lambda v, h=h, qt=qt: v.copy(t_aoT[:, h, 512 * qt:512 * (qt + 1)],
                                                        ps_a[0:64, 1024:1536]), waits=[e_mm])
                e_d1 = vec(lambda v, qt=qt: v.copy(t_drow[64:65, 512 * qt:512 * (qt + 1)],
                                                   ps_a[64:65, 1024:1536]), waits=[e_mm])
                e_prev_av = e_d1
                e_qt_av.append(e_a1)
            e_rc0 = act(lambda s: s.activation(t_drow[64:65, :], t_drow[64:65, :], AF.Ln),
                        waits=[e_prev_av])
            e_rc = act(lambda s: s.activation(t_drow[64:65, :], t_drow[64:65, :], AF.Exp,
                                              scale=-1.0), waits=[e_rc0])
            e_ntt = [None, None]
            for qt in range(4):
                par = qt % 2
                npp = ps_b[0:64, 1024 + 512 * par:1536 + 512 * par]
                e_nb = mm(lambda t, qt=qt, npp=npp: t.matmul(
                    npp, t_onesall[64:65, 0:64], t_drow[64:65, 512 * qt:512 * (qt + 1)],
                    start=True, stop=True), waits=[e_rc] + ([e_ntt[par]] if e_ntt[par] else []))
                e_ntt[par] = vec(lambda v, h=h, qt=qt, npp=npp: v.tt(
                    t_aoT[:, h, 512 * qt:512 * (qt + 1)],
                    t_aoT[:, h, 512 * qt:512 * (qt + 1)], npp, OP.mult),
                    waits=[e_nb, e_qt_av[qt]])
            e_na = e_ntt[1]
            if h == 0 and b == 0 and DBG:
                e_dbg1 = P.emit("gpsimd", lambda g: g.dma_start(vT_sp[8:72, 0:2048], t_aoT[:, 0, :]),
                                waits=[e_na], inc="dmg", k=16)
                e_dbg2 = P.emit("gpsimd", lambda g: g.dma_start(vT_sp[5:6, 0:2048], t_drow[64:65, :]),
                                waits=[e_prev_av], inc="dmg", k=16)
                g["dbg_evs"].extend([e_dbg1, e_dbg2])
        if b == 0 and g.get("dbg_big") is not None:
            dbg_big = g["dbg_big"]
            for hh2 in range(16):
                e_dmp = P.emit("gpsimd", lambda g2, hh2=hh2, dbg_big=dbg_big: g2.dma_start(
                    dbg_big[64 * hh2:64 * (hh2 + 1), 0:2048], t_aoT[:, hh2, :]),
                    waits=[e_na], inc="dmg", k=16)
                g["dbg_evs"].append(e_dmp)
        e_ix0 = dma("sync", t_stokT[:],
                    tokid_sp[b * C:(b + 1) * C, :].rearrange("(c p) o -> c (p o)", p=128),
                    waits=[e_tok_last])
        e_trx = mm(lambda t: t.transpose(ps_a[0:128, 1536:1552], t_stokT[:], t_ident[0:16, 0:16]),
                   waits=[e_ix0])
        e_ix = vec(lambda v, b=b: v.copy(t_stok[:, 16 * b:16 * (b + 1)], ps_a[0:128, 1536:1552]),
                   waits=[e_trx])
        e_prev_ev = [e_na, e_na]
        e_prev_eb = None
        for sl in range(16):
            par = sl % 2
            pp = ps_b[:, 0:1024] if par == 0 else ps_b[:, 1024:2048]
            e_mm = None
            for ot in range(2):
                for h in range(H):
                    e_mm = mm(lambda t, sl=sl, h=h, ot=ot, pp=pp: t.matmul(
                        pp[:, 512 * ot:512 * (ot + 1)],
                        t_aoT[:, h, 128 * sl:128 * (sl + 1)],
                        t_wo2[:, h, 512 * ot:512 * (ot + 1)],
                        start=(h == 0), stop=(h == H - 1)),
                        waits=[e_na, e_wo, e_prev_ev[par]] if (h == 0 and ot == 0) else [])
            e_g = vec(lambda v, sl=sl, pp=pp: v.ts(t_orow[:], pp, t_gden[:, sl:sl + 1], None, OP.mult),
                      waits=[e_mm, e_gt[-1], e_prev_eb])
            if b == 0 and sl == 2 and g.get("dbg_big") is not None:
                e_g = P.emit("gpsimd", lambda g2: g2.dma_start(g["dbg_big"][1024:1152, 0:1024], t_orow[:]),
                             waits=[e_g], inc="dmg", k=16)
                g["dbg_evs"].append(e_g)
            if b == 0 and sl == 0:
                e_gd = P.emit("gpsimd", lambda g2: g2.dma_start(vT_sp[80:208, 0:16], t_gden[:]),
                              waits=[e_gt[-1]], inc="dmg", k=16)
                g["dbg_evs"].append(e_gd)
            e_prev_ev[par] = e_g
            ea = act(lambda s: s.activation(t_qksc[:, 0:1024], t_orow[:], AF.Identity,
                                            accum_out=t_drow[0:128, 0:1]), waits=[e_g])
            eb = act(lambda s: s.activation(t_qksc[:, 0:1024], t_orow[:], AF.Square,
                                            accum_out=t_drow[0:128, 1:2]), waits=[e_g])
            e_prev_eb = eb
            vec(lambda v: v.ts(t_drow[0:128, 0:1], t_drow[0:128, 0:1], 1.0 / D, None, OP.mult), waits=[ea, eb])
            vec(lambda v: v.tt(t_drow[0:128, 2:3], t_drow[0:128, 0:1], t_drow[0:128, 0:1], OP.mult))
            vec(lambda v: v.ts(t_drow[0:128, 1:2], t_drow[0:128, 1:2], 1.0 / D, None, OP.mult))
            vec(lambda v: v.tt(t_drow[0:128, 1:2], t_drow[0:128, 1:2], t_drow[0:128, 2:3], OP.subtract))
            ev5 = vec(lambda v: v.ts(t_drow[0:128, 1:2], t_drow[0:128, 1:2], EPS, None, OP.add))
            ev6 = act(lambda s: s.activation(t_drow[0:128, 1:2], t_drow[0:128, 1:2], AF.Sqrt), waits=[ev5])
            ev7 = vec(lambda v: v.recip(t_drow[0:128, 1:2], t_drow[0:128, 1:2]), waits=[ev6])
            ev8 = vec(lambda v: v.ts(t_drow[0:128, 0:1], t_drow[0:128, 0:1], -1.0, None, OP.mult))
            e_ln = vec(lambda v: v.ts(t_lnrow[:], t_orow[:], t_drow[0:128, 0:1], t_drow[0:128, 1:2],
                                      OP.add, OP.mult), waits=[ev7, ev8, e_prev_fdma])
            e_prev_fdma = P.emit("gpsimd", lambda gg, b=b, sl=sl: gg.indirect_dma_start(
                out_t, IndirectOffsetOnAxis(t_stok[:, 16 * b + sl:16 * b + sl + 1], 0),
                t_lnrow[:], None, bounds_check=e_breg2[0], oob_is_err=False,
                compute_op=mybir.AluOpType.add),
                waits=[e_ln, e_ix, e_x2_last], inc="dmg", k=16)
    P.barrier()


_CACHE = {}


def kernel(**inputs):
    B = 16
    x = np.asarray(inputs["L0_attn_x"], dtype=np.float32)
    if "nc" not in _CACHE:
        _CACHE["nc"] = build_nc()
    nc = _CACHE["nc"]

    tri = (np.arange(128)[:, None] < np.arange(128)[None, :]).astype(np.float32)
    onescf = np.ones((128, 1), np.float32)
    onesrf = np.ones((1, 128), np.float32)
    onesall = np.ones((128, 128), np.float32)
    ident = np.eye(128, dtype=np.float32)
    slotid = (np.arange(128)[:, None] + 128.0 * np.arange(16)[None, :]).astype(np.float32)

    shared = dict(
        qkv_w=np.ascontiguousarray(inputs["L0_attn_qkv_w"], dtype=np.float32),
        proj_w=np.ascontiguousarray(inputs["L0_attn_proj_w"], dtype=np.float32),
        ch_w1=np.ascontiguousarray(inputs["L0_attn_ch_w1"], dtype=np.float32),
        ch_w2=np.ascontiguousarray(inputs["L0_attn_ch_w2"], dtype=np.float32),
        ch_b1=np.ascontiguousarray(inputs["L0_attn_ch_b1"], dtype=np.float32),
        ch_b2=np.ascontiguousarray(inputs["L0_attn_ch_b2"], dtype=np.float32),
        proj_b=np.ascontiguousarray(inputs["L0_attn_proj_b"], dtype=np.float32),
        wg_in=np.ascontiguousarray(inputs["L0_ffn_Wg"], dtype=np.float32),
        wq_in=np.ascontiguousarray(inputs["L0_ffn_Wq"], dtype=np.float32),
        wk_in=np.ascontiguousarray(inputs["L0_ffn_Wk"], dtype=np.float32),
        wv_in=np.ascontiguousarray(inputs["L0_ffn_Wv"], dtype=np.float32),
        wo_in=np.ascontiguousarray(inputs["L0_ffn_Wo"], dtype=np.float32),
        in_c1=np.asarray(inputs["L0_attn_c1"], dtype=np.float32).reshape(1),
        in_c2=np.asarray(inputs["L0_attn_c2"], dtype=np.float32).reshape(1),
        in_conv1=np.ascontiguousarray(inputs["L0_attn_conv1_w"], dtype=np.float32),
        in_gn_g=np.ascontiguousarray(inputs["L0_attn_gn_g"], dtype=np.float32),
        in_gn_b=np.ascontiguousarray(inputs["L0_attn_gn_b"], dtype=np.float32),
        in_conv2=np.ascontiguousarray(inputs["L0_attn_conv2_w"], dtype=np.float32),
        in_conv2b=np.asarray(inputs["L0_attn_conv2_b"], dtype=np.float32).reshape(1),
        tri_in=tri, onescf_in=onescf, onescr_in=onescf, onesrf_in=onesrf,
        onesrr_in=onesrf, onesall_in=onesall, ident_in=ident, slotid_in=slotid,
    )
    in_maps = []
    for c in range(8):
        xs = x[2 * c:2 * c + 2]
        x_tm = np.ascontiguousarray(xs.reshape(2 * N, D))
        xT = np.ascontiguousarray(np.moveaxis(xs, 2, 0).reshape(D, 2 * N))
        m = dict(shared)
        m["x_tm"] = x_tm
        m["xT"] = xT
        in_maps.append(m)

    res = run_bass_kernel_spmd(nc, in_maps, core_ids=list(range(8)))
    out = np.concatenate([r["out_t"].reshape(2, N, D) for r in res.results], axis=0)
    return out


# version 07
# speedup vs baseline: 1.0425x; 1.0098x over previous
import numpy as np
import concourse.bass as bass
import concourse.mybir as mybir
from concourse.bass_utils import run_bass_kernel_spmd
from concourse.bass import IndirectOffsetOnAxis
from contextlib import ExitStack

F32 = mybir.dt.float32
F32R = mybir.dt.float32r
BF16 = mybir.dt.bfloat16
I32 = mybir.dt.int32
U32 = mybir.dt.uint32
OP = mybir.AluOpType
AF = mybir.ActivationFunctionType
AX = mybir.AxisListType.X

NB, N, D, C, H, DH = 2, 4096, 1024, 2048, 16, 64
NT = 16
ROUNDS = 24
DBG = False
SKIP_EF = False
PHASE_LIM = 99
SETUP_LIM = 99
A_TILES = 16
A_STAGE = 99
DUMP_XC = False
BIGF = float(1 << 22)
EPS = 1e-5
INVCM1 = 1.0 / (D - 1)

ENGINES = ("sync", "scalar", "vector", "tensor", "gpsimd")


class Prog:
    def __init__(self):
        self.ops = {e: [] for e in ENGINES}
        self.counts = {}
        self.hw = {e: {} for e in ENGINES}

    def emit(self, eng, fn, waits=(), inc=None, k=1):
        w = []
        for ev in waits:
            if ev is None:
                continue
            s, v = ev
            if self.hw[eng].get(s, 0) < v:
                w.append((s, v))
                self.hw[eng][s] = v
        ev_out = None
        v = 0
        if inc is not None:
            self.counts[inc] = self.counts.get(inc, 0) + k
            v = self.counts[inc]
            ev_out = (inc, v)
        self.ops[eng].append((fn, tuple(w), inc, k, v))
        return ev_out

    def barrier(self):
        snap = {s: v for s, v in self.counts.items() if s in ("tn", "ve", "ac", "gp")}
        self.counts["bar"] = self.counts.get("bar", 0) + 5
        tgt = self.counts["bar"]
        for e in ENGINES:
            self.ops[e].append(("BARRIER", (), tgt, 1, tgt))
            hw = self.hw[e]
            for s, v in snap.items():
                hw[s] = max(hw.get(s, 0), v)
            hw["bar"] = tgt


    def finalize(self):
        waited = {}
        for e in ENGINES:
            for op in self.ops[e]:
                if op[0] == "BARRIER":
                    continue
                for (s, v) in op[1]:
                    waited.setdefault(s, set()).add(v)
        remap = {}
        for s in ("tn", "ve", "ac", "gp"):
            vals = sorted(waited.get(s, ()))
            remap[s] = {v: i + 1 for i, v in enumerate(vals)}
        for e in ENGINES:
            newops = []
            for fn, w, inc, k, v in self.ops[e]:
                if fn == "BARRIER":
                    newops.append((fn, w, inc, k, v))
                    continue
                w2 = tuple((s, remap[s][x]) if s in remap else (s, x) for (s, x) in w)
                if inc in remap:
                    if v in remap[inc]:
                        newops.append((fn, w2, inc, k, remap[inc][v]))
                    else:
                        newops.append((fn, w2, None, 0, 0))
                else:
                    newops.append((fn, w2, inc, k, v))
            self.ops[e] = newops


class Vec:

    def __init__(self, v):
        self.v = v
        self.dirty = set()

    def _nm(self, ap):
        t = getattr(ap, "tensor", None)
        return t.name if t is not None else None

    def _pre(self, aps):
        names = {self._nm(a) for a in aps if a is not None} - {None}
        if self.dirty & names:
            self.v.drain()
            self.dirty.clear()

    def _post(self, out, extra=None):
        self.dirty.add(self._nm(out))
        if extra is not None:
            self.dirty.add(self._nm(extra))

    def ts(self, out, in0, s1, s2, op0, op1=None, accum_out=None):
        ins = [in0, out] + [s for s in (s1, s2) if hasattr(s, "tensor")]
        self._pre(ins)
        if op1 is None:
            r = self.v.tensor_scalar(out, in0, s1, s2, op0, accum_out=accum_out)
        else:
            r = self.v.tensor_scalar(out, in0, s1, s2, op0, op1, accum_out=accum_out)
        self._post(out, accum_out)
        return r

    def tt(self, out, a, b, op):
        self._pre([a, b, out])
        r = self.v.tensor_tensor(out, a, b, op)
        self._post(out)
        return r

    def stt(self, out, in0, scalar, in1, op0, op1, accum_out=None):
        ins = [in0, in1, out] + ([scalar] if hasattr(scalar, "tensor") else [])
        self._pre(ins)
        r = self.v.scalar_tensor_tensor(out, in0, scalar, in1, op0, op1, accum_out=accum_out)
        self._post(out, accum_out)
        return r

    def red(self, out, in_, op):
        self._pre([in_, out])
        r = self.v.tensor_reduce(out, in_, AX, op)
        self._post(out)
        return r

    def copy(self, out, in_):
        self._pre([in_, out])
        r = self.v.tensor_copy(out, in_)
        self._post(out)
        return r

    def memset(self, out, c):
        self._pre([out])
        r = self.v.memset(out, c)
        self._post(out)
        return r

    def cpred(self, out, mask, data):
        self._pre([out, mask, data])
        r = self.v.copy_predicated(out, mask, data)
        self._post(out)
        return r

    def recip(self, out, in_):
        self._pre([in_, out])
        r = self.v.reciprocal(out, in_)
        self._post(out)
        return r


def build_nc():
    nc = bass.Bass("TRN2", target_bir_lowering=False)
    es = ExitStack()
    P = Prog()
    dt_ = nc.dram_tensor

    xT = dt_("xT", [D, NB * N], F32R, kind="ExternalInput").ap()
    x_tm = dt_("x_tm", [NB * N, D], F32, kind="ExternalInput").ap()
    qkv_w = dt_("qkv_w", [D, 3 * D], F32R, kind="ExternalInput").ap()
    proj_w = dt_("proj_w", [D, D], F32R, kind="ExternalInput").ap()
    ch_w1 = dt_("ch_w1", [D, 256], F32R, kind="ExternalInput").ap()
    ch_w2 = dt_("ch_w2", [256, D], F32R, kind="ExternalInput").ap()
    ch_b1 = dt_("ch_b1", [256], F32, kind="ExternalInput").ap()
    ch_b2 = dt_("ch_b2", [D], F32, kind="ExternalInput").ap()
    proj_b = dt_("proj_b", [D], F32, kind="ExternalInput").ap()
    wg_in = dt_("wg_in", [D, 1], F32, kind="ExternalInput").ap()
    wq_in = dt_("wq_in", [D, D], F32, kind="ExternalInput").ap()
    wk_in = dt_("wk_in", [D, D], F32, kind="ExternalInput").ap()
    wv_in = dt_("wv_in", [D, D], F32, kind="ExternalInput").ap()
    wo_in = dt_("wo_in", [D, D], F32, kind="ExternalInput").ap()
    sm_in = {}
    for nm, shp in [("c1", [1]), ("c2", [1]), ("conv1", [4, 1, 3]), ("gn_g", [4]),
                    ("gn_b", [4]), ("conv2", [1, 4, 3]), ("conv2b", [1])]:
        sm_in[nm] = dt_("in_" + nm, shp, F32, kind="ExternalInput").ap()
    tri_in = dt_("tri_in", [128, 128], F32, kind="ExternalInput").ap()
    onescf_in = dt_("onescf_in", [128, 1], F32, kind="ExternalInput").ap()
    onescr_in = dt_("onescr_in", [128, 1], F32R, kind="ExternalInput").ap()
    onesrf_in = dt_("onesrf_in", [1, 128], F32, kind="ExternalInput").ap()
    onesrr_in = dt_("onesrr_in", [1, 128], F32R, kind="ExternalInput").ap()
    onesall_in = dt_("onesall_in", [128, 128], F32, kind="ExternalInput").ap()
    ident_in = dt_("ident_in", [128, 128], F32, kind="ExternalInput").ap()
    slotid_in = dt_("slotid_in", [128, 16], F32, kind="ExternalInput").ap()

    out_t = dt_("out_t", [NB * N, D], F32, kind="ExternalOutput").ap()
    vT_sp = dt_("vT_sp", [D, NB * N], F32R, kind="ExternalOutput" if DBG else "Internal").ap()
    x2_sp = dt_("x2_sp", [NB * N, D], F32, kind="ExternalOutput" if DBG else "Internal").ap()
    ln2_sp = dt_("ln2_sp", [NB * N, D], BF16, kind="ExternalOutput" if DBG else "Internal").ap()
    ab_pad = dt_("ab_pad", [NB, N + 128], F32, kind="Internal").ap()
    hh_pad = dt_("hh_pad", [NB, 4, N + 128], F32, kind="Internal").ap()
    comp = dt_("comp", [NB * C, D], BF16, kind="ExternalOutput" if DBG else "Internal").ap()
    qT_sp = dt_("qT_sp", [NB, D, C], BF16, kind="ExternalOutput" if DBG else "Internal").ap()
    kT_sp = dt_("kT_sp", [NB, D, C], BF16, kind="Internal").ap()
    ffn_sp = dt_("ffn_sp", [NB * C, D], F32, kind="ExternalOutput" if DBG else "Internal").ap()
    tokid_sp = dt_("tokid_sp", [NB * C, 1], F32, kind="Internal").ap()
    wo_bf = dt_("wo_bf", [D, D], BF16, kind="Internal").ap()
    wq_bf = dt_("wq_bf", [D, D], BF16, kind="Internal").ap()
    wk_bf = dt_("wk_bf", [D, D], BF16, kind="Internal").ap()
    wv_bf = dt_("wv_bf", [D, D], BF16, kind="Internal").ap()
    dbg_st = dt_("dbg_st", [16, 128, 64], F32, kind="ExternalOutput").ap() if DBG else None
    dbg_big = dt_("dbg_big", [1152, 2048], F32, kind="ExternalOutput").ap() if DBG else None

    sbuf = lambda *a: es.enter_context(nc.sbuf_tensor(*a))
    AR = 166 * 1024
    t_arena = es.enter_context(nc.sbuf_tensor("arena", [128, AR], mybir.dt.uint8))
    arena_base = nc.lookup_mloc(t_arena).addr

    def place(name, shape, dtype, off_kb):
        import functools, operator
        hnd = nc.virtual_sbuf_tensor(name, shape, dtype)
        nc._tensor(hnd.name, hnd.shape, hnd.dtype, type="SB")
        mloc = nc.lookup_mloc(hnd)
        mloc.allocated = True
        mloc.addr = arena_base + int(off_kb * 1024)
        pp = functools.reduce(operator.mul, shape[1:], 1) * mybir.dt.size(dtype)
        assert int(off_kb * 1024) + pp <= AR, (name, off_kb, pp)
        return hnd

    t_wqk = place("t_wqk", [128, 8, 2048], F32R, 0)
    t_wv = place("t_wv", [128, 8, 1024], F32R, 64)
    t_proj = place("t_proj", [128, 8, 1024], F32R, 0)
    t_chw1 = place("t_chw1", [128, 8, 256], F32R, 64)
    t_chw2 = place("t_chw2", [128, 2, 1024], F32R, 72)
    t_attnblk = place("t_attnblk", [128, 4, 1024], F32, 32)
    t_x2blk = place("t_x2blk", [128, 4, 1024], F32, 48)
    t_xt = [place(f"t_xt{j}", [128, 8, 512], F32R, 96 + 16 * j) for j in range(2)]
    t_xtm = [place(f"t_xtm{j}", [128, 4, 1024], F32, 128 + 16 * j) for j in range(2)]
    ode_names = ["am1", "a0", "ap1", "hh0", "hh1", "hh2", "hh3", "tmp", "yy", "fa"]
    t_ode = {nm: place("t_ode_" + nm, [32, 128], F32, 160 + 0.5 * k)
             for k, nm in enumerate(ode_names)}
    t_l2sT = place("t_l2sT", [128, 8, 2048], BF16, 0)
    t_wffn = place("t_wffn", [128, 8, 1024], BF16, 32)
    t_vffn = place("t_vffn", [128, 16, 1040], BF16, 64)
    t_vffn4 = place("t_vffn4", [128, 16, 16, 65], BF16, 64)
    t_attnblkB = place("t_attnblkB", [128, 4, 1024], F32, 64)
    t_aoT = place("t_aoT", [64, 16, 2048], BF16, 0)
    t_expS = place("t_expS", [128, 16, 512], BF16, 97)
    t_wo2 = place("t_wo2", [64, 16, 1024], BF16, 113)
    t_kh = place("t_kh", [64, 2048], BF16, 145)
    t_qh = place("t_qh", [64, 2048], BF16, 149)
    t_orow = place("t_orow", [128, 1024], F32, 153)
    t_drow = place("t_drow", [128, 2048], F32, 157)
    t_zeroE = place("t_zeroE", [128, 1024], BF16, 105)
    t_rows = place("t_rows", [128, 1024], BF16, 107)
    t_rows1 = place("t_rows1", [128, 1024], BF16, 109)
    t_facc = place("t_facc", [128, 1024], F32, 0)
    t_fx2 = place("t_fx2", [128, 1024], F32, 4)
    t_x2t = place("t_x2t", [128, 1024], F32, 80)
    t_biasbc = place("t_biasbc", [128, 1024], F32, 88)
    t_wgbc = place("t_wgbc", [128, 1024], F32, 92)
    t_bias_r = place("t_bias_r", [1, 1024], F32, 80)
    t_wgrow = place("t_wgrow", [1, 1024], F32, 84)
    t_ln2t = place("t_ln2t", [128, 1024], BF16, 84)
    t_gsel = place("t_gsel", [1, 2048], F32, 97)

    t_vout = sbuf("t_vout", [128, 2, 512], F32R)
    t_xm = sbuf("t_xm", [128, 1024], F32R)
    t_qksc = sbuf("t_qksc", [128, 1024], BF16)
    t_qksc2 = sbuf("t_qksc2", [128, 1024], BF16)
    t_zscr = sbuf("t_zscr", [128, 1024], BF16)
    t_lnrow = sbuf("t_lnrow", [128, 1024], F32)
    stat_names = ["mu1", "nmu1", "rstd1", "sqm", "skm", "sqq", "skk", "sqk",
                  "ab", "ssc", "nmu2", "rstd2", "zz", "dest", "scr1", "scr2"]
    st = {nm: sbuf("st_" + nm, [128, 64], F32) for nm in stat_names}
    t_desti = sbuf("t_desti", [128, 64], I32)
    t_smalls = sbuf("t_smalls", [1, 64], F32)
    t_smbc = sbuf("t_smbc", [128, 64], F32)
    t_nmrow = sbuf("t_nmrow", [1, 512], F32R)
    t_onescf = sbuf("t_onescf", [128, 1], F32)
    t_onescr = sbuf("t_onescr", [128, 1], F32R)
    t_onesrf = sbuf("t_onesrf", [1, 128], F32)
    t_onesrr = sbuf("t_onesrr", [1, 128], F32R)
    t_onesall = sbuf("t_onesall", [128, 128], F32)
    t_tri = sbuf("t_tri", [128, 128], F32)
    t_ident = sbuf("t_ident", [128, 128], F32)
    t_slotid = sbuf("t_slotid", [128, 16], F32)
    t_macc = sbuf("t_macc", [1, 2048], F32)
    t_mrow = sbuf("t_mrow", [128, 8, 2], F32R)
    t_wbar = sbuf("t_wbar", [128, 8, 2], F32R)
    t_hrel = sbuf("t_hrel", [128, 2, 2], F32R)
    t_chT = sbuf("t_chT", [128, 8, 2], F32)
    t_wgbf = sbuf("t_wgbf", [128, 8, 1], BF16)
    t_odes = sbuf("t_odes", [32, 16], F32)
    t_vred = sbuf("t_vred", [128, 32], F32)
    t_sc1 = sbuf("t_sc1", [1, 32], F32)
    t_sc2 = sbuf("t_sc2", [1, 32], F32)
    t_lo = sbuf("t_lo", [1, 2], F32)
    t_hi = sbuf("t_hi", [1, 2], F32)
    t_tm = sbuf("t_tm", [1, 2], F32)
    t_cn = sbuf("t_cn", [1, 2], F32)
    t_pr2 = sbuf("t_pr2", [1, 2], U32)
    t_tbc = sbuf("t_tbc", [128, 2], F32)
    t_mask = sbuf("t_mask", [128, 64], F32)
    t_umask = sbuf("t_umask", [128, 64], U32)
    t_pcnt = sbuf("t_pcnt", [128, 2], F32)
    t_sa = sbuf("t_sa", [128, 64], F32)
    t_sb2 = sbuf("t_sb2", [128, 64], F32)
    t_bigt = sbuf("t_bigt", [128, 64], F32)
    t_negt = sbuf("t_negt", [128, 1], F32)
    t_poff = sbuf("t_poff", [128, 2], F32)
    t_cntb = sbuf("t_cntb", [128, 2], F32)
    t_kbias = sbuf("t_kbias", [128, 32], F32)
    t_dcol = sbuf("t_dcol", [128, 1], I32)
    t_dcol1 = sbuf("t_dcol1", [128, 1], I32)
    t_dcol2 = sbuf("t_dcol2", [128, 1], I32)
    t_tok64 = sbuf("t_tok64", [128, 64], F32)
    t_stok = sbuf("t_stok", [128, 32], I32)
    t_stokT = sbuf("t_stokT", [16, 128], F32)
    t_big32 = sbuf("t_big32", [32, 128], F32)
    t_gden = sbuf("t_gden", [128, 16], F32)
    t_srowD = sbuf("t_srowD", [1, 512], F32R)

    ps_a = es.enter_context(nc.psum_tensor("ps_a", [128, 2048], F32))
    ps_b = es.enter_context(nc.psum_tensor("ps_b", [128, 2048], F32))

    sems = {nm: es.enter_context(nc.semaphore(nm)) for nm in
            ["bar", "dms", "dmsc", "dmg", "tn", "ve", "ac", "gp"]}

    VEC = [None]

    def dma(eng, out, in_, waits=(), **kw):
        stream = {"sync": "dms", "scalar": "dmsc", "gpsimd": "dmg"}[eng]
        return P.emit(eng, lambda e: e.dma_start(out, in_, **kw), waits, inc=stream, k=16)

    def dmat(eng, out, in_, waits=()):
        stream = {"sync": "dms", "scalar": "dmsc"}[eng]
        return P.emit(eng, lambda e: e.dma_start(out, in_, transpose=True), waits, inc=stream, k=16)

    def mm(lfn, waits=()):
        return P.emit("tensor", lfn, waits, inc="tn")

    def act(lfn, waits=()):
        return P.emit("scalar", lfn, waits, inc="ac")

    def vec(lfn, waits=()):
        return P.emit("vector", lambda e: lfn(VEC[0]), waits, inc="ve")

    def gps(lfn, waits=()):
        return P.emit("gpsimd", lfn, waits, inc="gp")

    def tile_bt(i):
        return i // 8, (i % 8) * 512

    for o, i_ in [(t_tri[:], tri_in), (t_onescf[:], onescf_in), (t_onescr[:], onescr_in),
                  (t_onesrf[:], onesrf_in), (t_onesrr[:], onesrr_in),
                  (t_onesall[:], onesall_in), (t_ident[:], ident_in),
                  (t_slotid[:], slotid_in)]:
        dma("sync", o, i_)
    if SETUP_LIM >= 2:
        dma("sync", t_wqk[:], qkv_w[:, 0:2048].rearrange("(c p) o -> p c o", p=128))
        dma("sync", t_wv[:], qkv_w[:, 2048:3072].rearrange("(c p) o -> p c o", p=128))
    if SETUP_LIM < 3:
        _finish(nc, P, VEC, locals());  return _replay(nc, P, VEC, locals(), es, sems)
    dma("sync", t_smalls[:, 0:1], sm_in["c1"].rearrange("(o a) -> o a", o=1))
    dma("sync", t_smalls[:, 1:2], sm_in["c2"].rearrange("(o a) -> o a", o=1))
    dma("sync", t_smalls[:, 2:14], sm_in["conv1"].rearrange("a b c -> b (a c)"))
    dma("sync", t_smalls[:, 14:18], sm_in["gn_g"].rearrange("(o a) -> o a", o=1))
    dma("sync", t_smalls[:, 18:22], sm_in["gn_b"].rearrange("(o a) -> o a", o=1))
    dma("sync", t_smalls[:, 22:34], sm_in["conv2"].rearrange("a b c -> a (b c)"))
    dma("sync", t_smalls[:, 34:35], sm_in["conv2b"].rearrange("(o a) -> o a", o=1))
    if SETUP_LIM < 4:
        _finish(nc, P, VEC, locals());  return _replay(nc, P, VEC, locals(), es, sems)
    if PHASE_LIM > 5:
        dma("gpsimd", t_wgbf[:], wg_in.rearrange("(c p) o -> p c o", p=128),
            allow_slow_non_contiguous=True)
        e_wcast = [dma("gpsimd", dst, src) for dst, src in
                   [(wq_bf, wq_in), (wk_bf, wk_in), (wv_bf, wv_in), (wo_bf, wo_in)]]
    P.barrier()

    vec(lambda v: v.memset(t_macc[:], 0.0))
    vec(lambda v: v.memset(t_negt[:], -1e9))
    P.barrier()
    if PHASE_LIM <= 0:
        _finish(nc, P, VEC, locals());  return _replay(nc, P, VEC, locals(), es, sems)

    e_tile_done = [()] * NT
    e_prev_stats = []
    for i in range(min(NT, A_TILES)):
        b, t0 = tile_bt(i)
        col = b * N + t0
        sc0 = b * 32 + (t0 // 128)
        bufw = list(e_tile_done[i - 2]) if i >= 2 else []
        e_xt = dma("sync", t_xt[i % 2][:], xT[:, col:col + 512].rearrange("(c p) t -> p c t", p=128), waits=bufw)
        e_xtm = dma("sync", t_xtm[i % 2][:], x_tm[col:col + 512, :].rearrange("(c p) d -> p c d", p=128), waits=bufw)
        xc = t_xt[i % 2]
        xtm = t_xtm[i % 2]
        e_sums = []
        for cl in range(4):
            sc = sc0 + cl
            act(lambda s, cl=cl, sc=sc, xtm=xtm: s.activation(
                t_qksc[:, 0:1024], xtm[:, cl, :], AF.Identity,
                accum_out=st["scr1"][:, sc:sc + 1]), waits=[e_xtm])
            e_sums.append(act(lambda s, cl=cl, sc=sc, xtm=xtm: s.activation(
                t_qksc[:, 0:1024], xtm[:, cl, :], AF.Square,
                accum_out=st["scr2"][:, sc:sc + 1]), waits=[e_xtm]))
        s4 = slice(sc0, sc0 + 4)
        ev = vec(lambda v, s4=s4: v.ts(st["mu1"][:, s4], st["scr1"][:, s4], 1.0 / D, None, OP.mult), waits=e_sums)
        vec(lambda v, s4=s4: v.ts(st["nmu1"][:, s4], st["mu1"][:, s4], -1.0, None, OP.mult))
        vec(lambda v, s4=s4: v.tt(st["scr1"][:, s4], st["mu1"][:, s4], st["mu1"][:, s4], OP.mult))
        vec(lambda v, s4=s4: v.ts(st["scr2"][:, s4], st["scr2"][:, s4], 1.0 / D, None, OP.mult))
        vec(lambda v, s4=s4: v.tt(st["scr2"][:, s4], st["scr2"][:, s4], st["scr1"][:, s4], OP.subtract))
        e_var = vec(lambda v, s4=s4: v.ts(st["scr2"][:, s4], st["scr2"][:, s4], EPS, None, OP.add))
        e_sq = act(lambda s, s4=s4: s.activation(st["scr1"][:, s4], st["scr2"][:, s4], AF.Sqrt), waits=[e_var])
        e_rs = vec(lambda v, s4=s4: v.recip(st["rstd1"][:, s4], st["scr1"][:, s4]), waits=[e_sq])
        if A_STAGE < 2:
            e_tile_done[i] = (e_rs, e_rs, e_sums[-1]);  e_prev_stats = [e_rs];  continue
        e_tp = None
        for r in range(4):
            e_tp = mm(lambda t, r=r, sc0=sc0: t.transpose(ps_a[0:1, 128 * r:128 * (r + 1)],
                                                          st["nmu1"][:, sc0 + r:sc0 + r + 1], t_ident[:]),
                      waits=([ev, e_rs] + e_prev_stats) if r == 0 else [])
        e_nr = vec(lambda v: v.copy(t_nmrow[:], ps_a[0:1, 0:512]), waits=[e_tp])
        e_bc = None
        for r in range(4):
            e_bc = mm(lambda t, r=r: t.matmul(ps_a[:, 512 + 128 * r:640 + 128 * r],
                                              t_onesrr[0:1, :], t_nmrow[0:1, 128 * r:128 * (r + 1)],
                                              start=True, stop=True), waits=[e_nr])
        e_xc = []
        for cc in range(8):
            e_xc.append(vec(lambda v, cc=cc, xc=xc: v.tt(xc[:, cc, :], xc[:, cc, :],
                                                         ps_a[:, 512:1024], OP.add),
                            waits=[e_bc, e_xt]))
        if A_STAGE < 3:
            e_tile_done[i] = (e_xc[-1], e_xc[-1], e_sums[-1]);  e_prev_stats = [e_xc[-1]];  continue
        if DUMP_XC:
            dma("gpsimd", x2_sp[col:col + 512, :].rearrange("(c p) d -> p c d", p=128),
                xc[:, 0:8, :], waits=e_xc)
        e_qkstat = []
        for cl in range(4):
            sc = sc0 + cl
            par = cl % 2
            pso = ps_a if par == 0 else ps_b
            q_ps = pso[:, 0:1024]
            k_ps = pso[:, 1024:2048]
            w_pre = list(e_xc)
            if cl >= 2:
                w_pre = [e_qkstat[cl - 2][-1], e_qkstat[cl - 2][-2]]
            e_mm = None
            for cc in range(8):
                lhs = (lambda xc=xc, cc=cc, cl=cl: xc[:, cc, 128 * cl:128 * (cl + 1)])()
                mm(lambda t, lhs=lhs, cc=cc, q_ps=q_ps: t.matmul(
                    q_ps[:, 0:512], lhs, t_wqk[:, cc, 0:512],
                    start=(cc == 0), stop=(cc == 7)), waits=w_pre if cc == 0 else [])
                mm(lambda t, lhs=lhs, cc=cc, q_ps=q_ps: t.matmul(
                    q_ps[:, 512:1024], lhs, t_wqk[:, cc, 512:1024],
                    start=(cc == 0), stop=(cc == 7)))
                mm(lambda t, lhs=lhs, cc=cc, k_ps=k_ps: t.matmul(
                    k_ps[:, 0:512], lhs, t_wqk[:, cc, 1024:1536],
                    start=(cc == 0), stop=(cc == 7)))
                e_mm = mm(lambda t, lhs=lhs, cc=cc, k_ps=k_ps: t.matmul(
                    k_ps[:, 512:1024], lhs, t_wqk[:, cc, 1536:2048],
                    start=(cc == 0), stop=(cc == 7)))
            if A_STAGE >= 31:
                e1 = act(lambda s, q_ps=q_ps, sc=sc: s.activation(
                    t_qksc[:, 0:1024], q_ps, AF.Identity, accum_out=st["sqm"][:, sc:sc + 1]), waits=[e_mm])
                if DUMP_XC and i == 0 and cl == 0:
                    e_dq = dma("gpsimd", ffn_sp[0:128, :], t_qksc[:, 0:1024], waits=[e1])
                    e_dw = dma("gpsimd", ffn_sp[128:256, :], t_wqk[:, 0, 0:1024])
                    e_dw2 = dma("gpsimd", ffn_sp[256:384, :], t_wqk[:, 7, 0:1024])
                    e1 = e_dq
                e2 = act(lambda s, k_ps=k_ps, sc=sc: s.activation(
                    t_qksc[:, 0:1024], k_ps, AF.Identity, accum_out=st["skm"][:, sc:sc + 1]))
            else:
                e1 = e2 = e_mm
            if A_STAGE >= 32:
                e3 = act(lambda s, q_ps=q_ps, sc=sc: s.activation(
                    t_qksc[:, 0:1024], q_ps, AF.Square, accum_out=st["sqq"][:, sc:sc + 1]), waits=[e_mm])
                e4 = act(lambda s, k_ps=k_ps, sc=sc: s.activation(
                    t_qksc[:, 0:1024], k_ps, AF.Square, accum_out=st["skk"][:, sc:sc + 1]))
            else:
                e3 = e4 = e_mm
            if A_STAGE >= 33:
                e5a = vec(lambda v, q_ps=q_ps: v.copy(t_qksc2[:, 0:1024], q_ps), waits=[e_mm, e4])
                e5 = vec(lambda v, k_ps=k_ps, sc=sc: v.stt(
                    t_qksc2[:, 0:1024], t_qksc2[:, 0:1024], 1.0, k_ps, OP.mult, OP.mult,
                    accum_out=st["sqk"][:, sc:sc + 1]), waits=[e5a])
            else:
                e5 = e_mm
            e_qkstat.append([e1, e2, e3, e4, e5])
        if A_STAGE < 4 or (30 <= A_STAGE < 40):
            e_tile_done[i] = (e_qkstat[3][4], e_qkstat[3][4], e_qkstat[3][3])
            e_prev_stats = e_qkstat[2] + e_qkstat[3];  continue
        e_vev = [None, None]
        e_vdma = []
        for vc in range(8):
            vps = ps_b[:, 512 * (vc % 2):512 * (vc % 2) + 512]
            if vc < 2:
                w_pre = e_qkstat[3]
            else:
                w_pre = [e_vev[vc % 2]]
            e_mm = None
            for cc in range(8):
                e_mm = mm(lambda t, vc=vc, cc=cc, vps=vps, xc=xc: t.matmul(
                    vps, t_wv[:, cc, 128 * vc:128 * (vc + 1)], xc[:, cc, :],
                    start=(cc == 0), stop=(cc == 7)), waits=w_pre if cc == 0 else [])
            e_ev = vec(lambda v, vc=vc, vps=vps: v.copy(t_vout[:, vc % 2, :], vps), waits=[e_mm])
            if DUMP_XC and i == 0 and vc == 0:
                e_ev = dma("gpsimd", ffn_sp[384:512, 0:512], t_vout[:, 0, :], waits=[e_ev])
            e_vev[vc % 2] = e_ev
            e_vdma.append(dma("scalar", vT_sp[128 * vc:128 * (vc + 1), col:col + 512],
                              t_vout[:, vc % 2, :], waits=[e_ev]))
        if A_STAGE < 5:
            e_tile_done[i] = (e_vev[0], e_vdma[-1], e_qkstat[3][3])
            e_prev_stats = e_qkstat[2] + e_qkstat[3] + [e_vev[0], e_vev[1]];  continue
        e_mps = None
        for cl in range(4):
            sc = sc0 + cl
            exm = vec(lambda v, cl=cl, sc=sc, xtm=xtm: v.ts(
                t_xm[:, 0:1024], xtm[:, cl, :], st["nmu1"][:, sc:sc + 1],
                st["rstd1"][:, sc:sc + 1], OP.add, OP.mult),
                waits=[e_vev[0], e_vev[1], e_rs] if cl == 0 else [e_mps])
            for half in range(2):
                e_mps = mm(lambda t, half=half, cl=cl: t.matmul(
                    ps_b[0:1, 1024 + 512 * half:1024 + 512 * (half + 1)], t_onescr[:],
                    t_xm[:, 512 * half:512 * (half + 1)],
                    start=(cl == 0), stop=(cl == 3)), waits=[exm])
        e_mac = vec(lambda v, b=b: v.tt(t_macc[0:1, 1024 * b:1024 * (b + 1)],
                                        t_macc[0:1, 1024 * b:1024 * (b + 1)],
                                        ps_b[0:1, 1024:2048], OP.add), waits=[e_mps])
        e_tile_done[i] = (e_mac, e_vdma[-1], e_qkstat[3][3])
        e_prev_stats = e_qkstat[2] + e_qkstat[3] + [e_mac]
        P.barrier()
    if PHASE_LIM <= 1:
        _finish(nc, P, VEC, locals());  return _replay(nc, P, VEC, locals(), es, sems)

    dma("sync", t_proj[:], proj_w.rearrange("(c p) o -> p c o", p=128))
    dma("sync", t_chw1[:], ch_w1.rearrange("(c p) o -> p c o", p=128))
    dma("sync", t_chw2[:], ch_w2.rearrange("(c p) o -> p c o", p=128))
    e_wgr = dma("sync", t_wgrow[:], wg_in.rearrange("a b -> b a"))
    e_pbr = dma("sync", t_bias_r[:], proj_b.rearrange("(o a) -> o a", o=1))
    e_swg = vec(lambda v: v.red(t_sc1[0:1, 0:1], t_wgrow[0:1, :], OP.add), waits=[e_wgr])
    e_swg2 = vec(lambda v: v.copy(t_smalls[0:1, 35:36], t_sc1[0:1, 0:1]), waits=[e_swg])
    e1 = mm(lambda t: t.matmul(ps_b[:, 0:64], t_onesrf[0:1, :], t_smalls[0:1, :], start=True, stop=True), waits=[e_swg2])
    vec(lambda v: v.copy(t_smbc[:], ps_b[:, 0:64]), waits=[e1])
    e1 = mm(lambda t: t.matmul(ps_b[:, 512:1024], t_onesrf[0:1, :], t_bias_r[0:1, 0:512], start=True, stop=True), waits=[e_pbr])
    e2 = mm(lambda t: t.matmul(ps_b[:, 1024:1536], t_onesrf[0:1, :], t_bias_r[0:1, 512:1024], start=True, stop=True))
    vec(lambda v: v.copy(t_biasbc[:], ps_b[:, 512:1536]), waits=[e1, e2])
    e1 = mm(lambda t: t.matmul(ps_b[:, 1536:2048], t_onesrf[0:1, :], t_wgrow[0:1, 0:512], start=True, stop=True))
    vec(lambda v: v.copy(t_wgbc[:, 0:512], ps_b[:, 1536:2048]), waits=[e1])
    e2 = mm(lambda t: t.matmul(ps_a[:, 0:512], t_onesrf[0:1, :], t_wgrow[0:1, 512:1024], start=True, stop=True))
    vec(lambda v: v.copy(t_wgbc[:, 512:1024], ps_a[:, 0:512]), waits=[e2])
    P.barrier()

    A = st
    sm = lambda k_: t_smbc[:, k_:k_ + 1]
    vec(lambda v: v.tt(A["scr1"][:], A["rstd1"][:], A["rstd1"][:], OP.mult))
    vec(lambda v: v.ts(A["sqm"][:], A["sqm"][:], 1.0 / D, None, OP.mult))
    vec(lambda v: v.ts(A["skm"][:], A["skm"][:], 1.0 / D, None, OP.mult))
    vec(lambda v: v.tt(A["sqm"][:], A["sqm"][:], A["rstd1"][:], OP.mult))
    vec(lambda v: v.tt(A["skm"][:], A["skm"][:], A["rstd1"][:], OP.mult))
    vec(lambda v: v.tt(A["sqq"][:], A["sqq"][:], A["scr1"][:], OP.mult))
    vec(lambda v: v.tt(A["skk"][:], A["skk"][:], A["scr1"][:], OP.mult))
    vec(lambda v: v.tt(A["sqk"][:], A["sqk"][:], A["scr1"][:], OP.mult))
    vec(lambda v: v.tt(A["scr1"][:], A["sqm"][:], A["sqm"][:], OP.mult))
    vec(lambda v: v.tt(A["scr2"][:], A["skm"][:], A["skm"][:], OP.mult))
    vec(lambda v: v.stt(A["sqq"][:], A["scr1"][:], -float(D), A["sqq"][:], OP.mult, OP.add))
    vec(lambda v: v.stt(A["skk"][:], A["scr2"][:], -float(D), A["skk"][:], OP.mult, OP.add))
    vec(lambda v: v.tt(t_sa[:], A["sqm"][:], A["skm"][:], OP.mult))
    vec(lambda v: v.stt(A["sqk"][:], t_sa[:], -float(D), A["sqk"][:], OP.mult, OP.add))
    vec(lambda v: v.ts(t_sb2[:], t_sa[:], 2.0, None, OP.mult))
    vec(lambda v: v.ts(t_sb2[:], t_sb2[:], sm(0), None, OP.add))
    vec(lambda v: v.ts(A["sqk"][:], A["sqk"][:], 2.0 * INVCM1, None, OP.mult))
    vec(lambda v: v.ts(A["sqk"][:], A["sqk"][:], sm(1), None, OP.add))
    vec(lambda v: v.tt(t_sb2[:], t_sb2[:], A["sqk"][:], OP.mult))
    vec(lambda v: v.tt(A["scr1"][:], A["scr1"][:], A["scr2"][:], OP.add))
    vec(lambda v: v.ts(A["scr1"][:], A["scr1"][:], sm(0), None, OP.add))
    vec(lambda v: v.tt(A["sqq"][:], A["sqq"][:], A["skk"][:], OP.add))
    vec(lambda v: v.ts(A["sqq"][:], A["sqq"][:], INVCM1, None, OP.mult))
    vec(lambda v: v.ts(A["sqq"][:], A["sqq"][:], sm(1), None, OP.add))
    vec(lambda v: v.tt(A["scr1"][:], A["scr1"][:], A["sqq"][:], OP.mult))
    vec(lambda v: v.ts(A["scr1"][:], A["scr1"][:], 1e-7, None, OP.add))
    e_r = vec(lambda v: v.recip(A["scr2"][:], A["scr1"][:]))
    vec(lambda v: v.tt(t_sb2[:], t_sb2[:], A["scr2"][:], OP.mult))
    e_ab = vec(lambda v: v.tt(A["ab"][:], t_sb2[:], t_sb2[:], OP.mult))

    e_wsmT = []
    for b in range(NB):
        bs = slice(32 * b, 32 * b + 32)
        e_t = mm(lambda t, bs=bs: t.transpose(ps_a[0:32, 0:128], A["ab"][:, bs], t_ident[:]),
                 waits=[e_ab] + (e_wsmT[-1:] if e_wsmT else []))
        e_c = vec(lambda v: v.copy(t_ode["a0"][:], ps_a[0:32, 0:128]), waits=[e_t])
        vec(lambda v: v.memset(t_ode["tmp"][:, 0:1], 0.0))
        e_z1 = vec(lambda v: v.copy(t_ode["tmp"][0:1, 1:2], t_ode["tmp"][0:1, 0:1]))
        e_sp = dma("scalar", ab_pad[b, 64:64 + N].rearrange("(c p) -> c p", p=128), t_ode["a0"][:], waits=[e_c])
        e_zl = dma("scalar", ab_pad[b, 63:64].rearrange("(o a) -> o a", o=1), t_ode["tmp"][0:1, 0:1], waits=[e_z1])
        e_zr = dma("scalar", ab_pad[b, 64 + N:65 + N].rearrange("(o a) -> o a", o=1), t_ode["tmp"][0:1, 0:1], waits=[e_z1])
        e_m1 = dma("sync", t_ode["am1"][:], ab_pad[b, 63:63 + N].rearrange("(c p) -> c p", p=128), waits=[e_sp, e_zl, e_zr])
        e_p1 = dma("sync", t_ode["ap1"][:], ab_pad[b, 65:65 + N].rearrange("(c p) -> c p", p=128), waits=[e_sp, e_zl, e_zr])
        e_h = []
        for o in range(4):
            hh = t_ode[f"hh{o}"]
            vec(lambda v, hh=hh, o=o: v.ts(hh[:], t_ode["am1"][:], t_smbc[0:32, 2 + 3 * o:3 + 3 * o], None, OP.mult), waits=[e_m1, e_p1])
            vec(lambda v, hh=hh, o=o: v.stt(hh[:], t_ode["a0"][:], t_smbc[0:32, 3 + 3 * o:4 + 3 * o], hh[:], OP.mult, OP.add))
            e_h.append(vec(lambda v, hh=hh, o=o: v.stt(hh[:], t_ode["ap1"][:], t_smbc[0:32, 4 + 3 * o:5 + 3 * o], hh[:], OP.mult, OP.add)))
        e_gs = []
        for o in range(4):
            e1_ = vec(lambda v, o=o: v.red(t_vred[0:32, 18 + o:19 + o], t_ode[f"hh{o}"][:], OP.add), waits=[e_h[o]])
            e2_ = act(lambda s, o=o: s.activation(t_ode["tmp"][:], t_ode[f"hh{o}"][:], AF.Square,
                                                  accum_out=t_vred[0:32, 22 + o:23 + o]), waits=[e_h[o]])
            e_gs += [e1_, e2_]
        e_gm = mm(lambda t: t.matmul(ps_a[0:1, 0:8], t_onescf[0:32, :], t_vred[0:32, 18:26], start=True, stop=True), waits=e_gs)
        e_cp = vec(lambda v: v.copy(t_sc1[0:1, 8:16], ps_a[0:1, 0:8]), waits=[e_gm])
        vec(lambda v: v.tt(t_sc1[0:1, 16:17], t_sc1[0:1, 8:9], t_sc1[0:1, 9:10], OP.add))
        vec(lambda v: v.tt(t_sc1[0:1, 17:18], t_sc1[0:1, 10:11], t_sc1[0:1, 11:12], OP.add))
        vec(lambda v: v.tt(t_sc1[0:1, 18:19], t_sc1[0:1, 12:13], t_sc1[0:1, 13:14], OP.add))
        vec(lambda v: v.tt(t_sc1[0:1, 19:20], t_sc1[0:1, 14:15], t_sc1[0:1, 15:16], OP.add))
        vec(lambda v: v.ts(t_sc1[0:1, 16:20], t_sc1[0:1, 16:20], 1.0 / (2 * N), None, OP.mult))
        vec(lambda v: v.tt(t_sc2[0:1, 0:2], t_sc1[0:1, 16:18], t_sc1[0:1, 16:18], OP.mult))
        vec(lambda v: v.tt(t_sc2[0:1, 0:2], t_sc1[0:1, 18:20], t_sc2[0:1, 0:2], OP.subtract))
        e_v2 = vec(lambda v: v.ts(t_sc2[0:1, 0:2], t_sc2[0:1, 0:2], EPS, None, OP.add))
        e_sqg = act(lambda s: s.activation(t_sc2[0:1, 2:4], t_sc2[0:1, 0:2], AF.Sqrt), waits=[e_v2])
        e_rg = vec(lambda v: v.recip(t_sc2[0:1, 4:6], t_sc2[0:1, 2:4]), waits=[e_sqg])
        for ch_ in range(4):
            g_ = ch_ // 2
            vec(lambda v, ch_=ch_, g_=g_: v.tt(t_sc2[0:1, 8 + ch_:9 + ch_], t_sc2[0:1, 4 + g_:5 + g_], t_smalls[0:1, 14 + ch_:15 + ch_], OP.mult))
            vec(lambda v, ch_=ch_, g_=g_: v.tt(t_sc2[0:1, 12 + ch_:13 + ch_], t_sc2[0:1, 8 + ch_:9 + ch_], t_sc1[0:1, 16 + g_:17 + g_], OP.mult))
            vec(lambda v, ch_=ch_: v.tt(t_sc2[0:1, 12 + ch_:13 + ch_], t_smalls[0:1, 18 + ch_:19 + ch_], t_sc2[0:1, 12 + ch_:13 + ch_], OP.subtract))
        e_ab8 = vec(lambda v: v.copy(t_sc2[0:1, 16:24], t_sc2[0:1, 8:16]))
        e_bc8 = mm(lambda t: t.matmul(ps_a[0:32, 16:24], t_onesrf[0:1, 0:32], t_sc2[0:1, 16:24], start=True, stop=True), waits=[e_ab8])
        e_od = vec(lambda v: v.copy(t_odes[:, 0:8], ps_a[0:32, 16:24]), waits=[e_bc8])
        e_hr = []
        for o in range(4):
            e_hr.append(act(lambda s, o=o: s.activation(
                t_ode[f"hh{o}"][:], t_ode[f"hh{o}"][:], AF.Relu,
                bias=t_odes[:, 4 + o:5 + o], scale=t_odes[:, o:o + 1]), waits=[e_od]))
        e_yy = vec(lambda v: v.memset(t_ode["yy"][:], 0.0))
        for o in range(4):
            e_sp2 = dma("scalar", hh_pad[b, o, 64:64 + N].rearrange("(c p) -> c p", p=128), t_ode[f"hh{o}"][:], waits=[e_hr[o]])
            dma("scalar", hh_pad[b, o, 63:64].rearrange("(o a) -> o a", o=1), t_ode["tmp"][0:1, 0:1])
            e_zz2 = dma("scalar", hh_pad[b, o, 64 + N:65 + N].rearrange("(o a) -> o a", o=1), t_ode["tmp"][0:1, 0:1])
            for tap, off in [(0, 63), (1, 64), (2, 65)]:
                e_ld = dma("sync", t_ode["am1"][:] if tap == 0 else (t_ode["a0"][:] if tap == 1 else t_ode["ap1"][:]),
                           hh_pad[b, o, off:off + N].rearrange("(c p) -> c p", p=128), waits=[e_sp2, e_zz2, e_yy])
                src = {0: "am1", 1: "a0", 2: "ap1"}[tap]
                e_yy = vec(lambda v, src=src, o=o, tap=tap: v.stt(
                    t_ode["yy"][:], t_ode[src][:], t_smbc[0:32, 22 + 3 * o + tap:23 + 3 * o + tap],
                    t_ode["yy"][:], OP.mult, OP.add), waits=[e_ld, e_yy])
        vec(lambda v: v.ts(t_ode["yy"][:], t_ode["yy"][:], t_smbc[0:32, 34:35], None, OP.add), waits=[e_yy])
        e_t2 = mm(lambda t, bs=bs: t.transpose(ps_a[0:32, 32:160], A["ab"][:, bs], t_ident[:]), waits=[e_od])
        e_c2 = vec(lambda v: v.tt(t_ode["yy"][:], t_ode["yy"][:], ps_a[0:32, 32:160], OP.add), waits=[e_t2])
        if DUMP_XC and b == 0:
            e_c2 = dma("gpsimd", ffn_sp[512:544, 0:128], t_ode["yy"][:], waits=[e_c2])
            e_hh = dma("gpsimd", ffn_sp[544:576, 0:128], t_ode["hh0"][:], waits=[e_c2])
        e_fa = act(lambda s: s.activation(t_ode["fa"][:], t_ode["yy"][:], AF.Sigmoid), waits=[e_c2])
        e_ex = act(lambda s: s.activation(t_ode["tmp"][:], t_ode["fa"][:], AF.Exp), waits=[e_fa])
        e_rd = vec(lambda v: v.red(t_vred[0:32, 26:27], t_ode["tmp"][:], OP.add), waits=[e_ex])
        e_sm = mm(lambda t: t.matmul(ps_a[0:1, 24:25], t_onescf[0:32, :], t_vred[0:32, 26:27], start=True, stop=True), waits=[e_rd])
        e_rc = vec(lambda v: v.copy(t_sc1[0:1, 20:21], ps_a[0:1, 24:25]), waits=[e_sm])
        e_rc2 = vec(lambda v: v.recip(t_sc1[0:1, 21:22], t_sc1[0:1, 20:21]), waits=[e_rc])
        e_bc2 = mm(lambda t: t.matmul(ps_a[0:32, 25:26], t_onesrf[0:1, 0:32], t_sc1[0:1, 21:22], start=True, stop=True), waits=[e_rc2])
        e_cb2 = vec(lambda v: v.copy(t_vred[0:32, 27:28], ps_a[0:32, 25:26]), waits=[e_bc2])
        e_w = vec(lambda v: v.ts(t_ode["fa"][:], t_ode["tmp"][:], t_vred[0:32, 27:28], None, OP.mult), waits=[e_cb2])
        e_t3 = mm(lambda t: t.transpose(ps_b[0:128, 0:32], t_ode["fa"][:], t_ident[0:32, 0:32]), waits=[e_w])
        e_wT = vec(lambda v, bs=bs: v.copy(st["ssc"][:, bs], ps_b[0:128, 0:32]), waits=[e_t3])
        e_wsmT.append(e_wT)
    e_ssc = vec(lambda v: v.tt(st["ssc"][:], st["ssc"][:], st["rstd1"][:], OP.mult), waits=e_wsmT)
    P.barrier()
    if PHASE_LIM <= 2:
        _finish(nc, P, VEC, locals());  return _replay(nc, P, VEC, locals(), es, sems)

    vec(lambda v: v.ts(t_macc[:], t_macc[:], 1.0 / N, None, OP.mult))
    e_mT = []
    for b in range(NB):
        for cc in range(8):
            e_t = mm(lambda t, b=b, cc=cc: t.transpose(
                ps_a[0:128, 0:1], t_macc[0:1, 1024 * b + 128 * cc:1024 * b + 128 * (cc + 1)], t_ident[0:1, 0:1]),
                waits=e_mT[-1:])
            e_mT.append(vec(lambda v, b=b, cc=cc: v.copy(t_mrow[:, cc, b:b + 1], ps_a[0:128, 0:1]), waits=[e_t]))
    e_b1 = dma("sync", t_vred[:, 18:20], ch_b1.rearrange("(c p) -> p c", p=128), allow_slow_non_contiguous=True)
    e_b2 = dma("sync", t_vred[:, 20:28], ch_b2.rearrange("(c p) -> p c", p=128), allow_slow_non_contiguous=True)
    for hc in range(2):
        e_mm2 = None
        for cc in range(8):
            e_mm2 = mm(lambda t, hc=hc, cc=cc: t.matmul(
                ps_a[0:128, 4 + 2 * hc:6 + 2 * hc], t_chw1[:, cc, 128 * hc:128 * (hc + 1)],
                t_mrow[:, cc, :], start=(cc == 0), stop=(cc == 7)), waits=e_mT + [e_b1])
        act(lambda s, hc=hc: s.activation(t_hrel[:, hc, :], ps_a[0:128, 4 + 2 * hc:6 + 2 * hc],
                                          AF.Relu, bias=t_vred[:, 18 + hc:19 + hc]), waits=[e_mm2])
    e_hr2 = P.emit("scalar", lambda s: s.nop(), inc="ac")
    for oc in range(8):
        e_mm3 = None
        for hc in range(2):
            e_mm3 = mm(lambda t, oc=oc, hc=hc: t.matmul(
                ps_a[0:128, 8 + 2 * oc:10 + 2 * oc], t_chw2[:, hc, 128 * oc:128 * (oc + 1)],
                t_hrel[:, hc, :], start=(hc == 0), stop=(hc == 1)), waits=[e_hr2, e_b2])
        act(lambda s, oc=oc: s.activation(t_chT[:, oc, :], ps_a[0:128, 8 + 2 * oc:10 + 2 * oc],
                                          AF.Sigmoid, bias=t_vred[:, 20 + oc:21 + oc]), waits=[e_mm3])
    P.barrier()
    if PHASE_LIM <= 3:
        _finish(nc, P, VEC, locals());  return _replay(nc, P, VEC, locals(), es, sems)

    e_tile_done2 = [()] * NT
    prev_attn_rd = [[None] * 4, [None] * 4]
    prev_x2_rd = [None] * 4
    prev_ln2_dma = [None]
    prev_x2t = [None]
    dstate = [None] * NT
    x2_last = [None]
    rs2_last = [None]

    def d_stage0(i):
        b, t0 = tile_bt(i)
        col = b * N + t0
        sc0 = b * 32 + (t0 // 128)
        tp = i % 2
        wv = [dstate[i - 2]["e_mmlast"]] if i >= 2 else []
        wx = list(dstate[i - 2]["e_x2ev"]) if i >= 2 else []
        e_vt = dma("sync", t_xt[i % 2][:], vT_sp[:, col:col + 512].rearrange("(c p) t -> p c t", p=128), waits=wv)
        e_xtm = dma("sync", t_xtm[i % 2][:], x_tm[col:col + 512, :].rearrange("(c p) d -> p c d", p=128), waits=wx)
        vt = t_xt[i % 2]
        xtm = t_xtm[i % 2]
        e_ol = []
        for cc in range(8):
            e_ol.append(vec(lambda v, cc=cc, vt=vt, b=b: v.ts(
                vt[:, cc, :], vt[:, cc, :], t_chT[:, cc, b:b + 1], None, OP.mult),
                waits=[e_vt]))
        dstate[i] = dict(b=b, col=col, sc0=sc0, s4=slice(sc0, sc0 + 4), tp=tp,
                         ablk=(t_attnblk if tp == 0 else t_attnblkB),
                         vt=vt, xtm=xtm, e_xtm=e_xtm, e_ol=e_ol, e_x2ev=[])

    def d_stage1(i):
        ds = dstate[i]
        sc0, tp, ablk, vt = ds["sc0"], ds["tp"], ds["ablk"], ds["vt"]
        e_ol = ds["e_ol"]
        e_pr = []
        e_ev4 = []
        prev_ev4 = dstate[i - 1]["e_ev4"] if i >= 1 else []
        for cl in range(4):
            par = cl % 2
            pps = ps_a[:, 1024:2048] if par == 0 else ps_b[:, 0:1024]
            w_pre = list(e_ol) + prev_ev4
            if cl >= 2:
                w_pre += [e_pr[cl - 2]]
            e_mm4 = None
            for cc in range(8):
                for ot in range(2):
                    e_mm4 = mm(lambda t, cc=cc, cl=cl, ot=ot, pps=pps, vt=vt: t.matmul(
                        pps[:, 512 * ot:512 * (ot + 1)], vt[:, cc, 128 * cl:128 * (cl + 1)],
                        t_proj[:, cc, 512 * ot:512 * (ot + 1)],
                        start=(cc == 0), stop=(cc == 7)), waits=w_pre if (cc == 0 and ot == 0) else [])
            sc = sc0 + cl
            e_ev = vec(lambda v, cl=cl, sc=sc, pps=pps, ablk=ablk: v.stt(
                ablk[:, cl, :], pps, st["ssc"][:, sc:sc + 1], t_biasbc[:],
                OP.mult, OP.add, accum_out=st["scr1"][:, sc:sc + 1]),
                waits=[e_mm4, prev_attn_rd[tp][cl]])
            e_ev4.append(e_ev)
            e_pr.append(e_ev)
            eb = act(lambda s, cl=cl, sc=sc, ablk=ablk: s.activation(
                t_qksc[:, 0:1024], ablk[:, cl, :], AF.Square,
                accum_out=st["scr2"][:, sc:sc + 1]), waits=[e_ev])
            e_pr[-1] = eb
        ds["e_pr"] = e_pr
        ds["e_ev4"] = e_ev4[2:]
        ds["e_mmlast"] = e_mm4

    def d_stage2(i):
        ds = dstate[i]
        col, sc0, s4, tp, ablk = ds["col"], ds["sc0"], ds["s4"], ds["tp"], ds["ablk"]
        xtm, e_xtm, e_pr = ds["xtm"], ds["e_xtm"], ds["e_pr"]
        ev = vec(lambda v, s4=s4: v.ts(st["scr1"][:, s4], st["scr1"][:, s4], 1.0 / D, None, OP.mult), waits=e_pr)
        vec(lambda v, s4=s4: v.ts(st["scr2"][:, s4], st["scr2"][:, s4], 1.0 / D, None, OP.mult))
        vec(lambda v, s4=s4: v.tt(t_sa[:, s4], st["scr1"][:, s4], st["scr1"][:, s4], OP.mult))
        vec(lambda v, s4=s4: v.tt(st["scr2"][:, s4], st["scr2"][:, s4], t_sa[:, s4], OP.subtract))
        e_var = vec(lambda v, s4=s4: v.ts(st["scr2"][:, s4], st["scr2"][:, s4], EPS, None, OP.add))
        e_sq = act(lambda s, s4=s4: s.activation(t_sb2[:, s4], st["scr2"][:, s4], AF.Sqrt), waits=[e_var])
        e_rs = vec(lambda v, s4=s4: v.recip(t_sb2[:, s4], t_sb2[:, s4]), waits=[e_sq])
        e_nm = vec(lambda v, s4=s4: v.ts(st["scr1"][:, s4], st["scr1"][:, s4], -1.0, None, OP.mult), waits=[e_rs])
        e_nb2 = vec(lambda v, s4=s4: v.tt(st["scr1"][:, s4], st["scr1"][:, s4], t_sb2[:, s4], OP.mult))
        e_ch_done = []
        for cl in range(4):
            sc = sc0 + cl
            e_l = act(lambda s, cl=cl, sc=sc, ablk=ablk: s.activation(
                t_x2t[:, 0:1024], ablk[:, cl, :], AF.Identity,
                bias=st["scr1"][:, sc:sc + 1], scale=t_sb2[:, sc:sc + 1]),
                waits=[e_nb2, prev_x2t[0]])
            prev_attn_rd[tp][cl] = e_l
            e_x2 = vec(lambda v, cl=cl, sc=sc, xtm=xtm: v.stt(
                t_x2blk[:, cl, :], t_x2t[:, 0:1024], 1.0, xtm[:, cl, :],
                OP.mult, OP.add, accum_out=st["sqm"][:, sc:sc + 1]),
                waits=[e_xtm, prev_x2_rd[cl], e_l])
            ds["e_x2ev"].append(e_x2)
            prev_x2t[0] = e_x2
            e_st = dma("scalar", out_t[col + 128 * cl:col + 128 * (cl + 1), :], t_x2blk[:, cl, :], waits=[e_x2])
            x2_last[0] = e_st
            if DBG:
                dma("scalar", x2_sp[col + 128 * cl:col + 128 * (cl + 1), :], t_x2blk[:, cl, :], waits=[e_x2])
            eb = act(lambda s, cl=cl, sc=sc: s.activation(
                t_qksc[:, 0:1024], t_x2blk[:, cl, :], AF.Square,
                accum_out=st["skm"][:, sc:sc + 1]), waits=[e_x2])
            e_ch_done.append((e_st, eb))
        e_prs = [tup[1] for tup in e_ch_done]
        ev = vec(lambda v, s4=s4: v.ts(st["sqm"][:, s4], st["sqm"][:, s4], 1.0 / D, None, OP.mult), waits=e_prs)
        vec(lambda v, s4=s4: v.ts(st["nmu2"][:, s4], st["sqm"][:, s4], -1.0, None, OP.mult))
        vec(lambda v, s4=s4: v.tt(t_sa[:, s4], st["sqm"][:, s4], st["sqm"][:, s4], OP.mult))
        vec(lambda v, s4=s4: v.ts(st["skm"][:, s4], st["skm"][:, s4], 1.0 / D, None, OP.mult))
        vec(lambda v, s4=s4: v.tt(st["skm"][:, s4], st["skm"][:, s4], t_sa[:, s4], OP.subtract))
        e_var2 = vec(lambda v, s4=s4: v.ts(st["skm"][:, s4], st["skm"][:, s4], EPS, None, OP.add))
        e_sq2 = act(lambda s, s4=s4: s.activation(st["rstd2"][:, s4], st["skm"][:, s4], AF.Sqrt), waits=[e_var2])
        e_rs2 = vec(lambda v, s4=s4: v.recip(st["rstd2"][:, s4], st["rstd2"][:, s4]), waits=[e_sq2])
        rs2_last[0] = e_rs2
        e_zi = []
        for cl in range(4):
            sc = sc0 + cl
            e_l2 = vec(lambda v, cl=cl, sc=sc: v.ts(
                t_ln2t[:, 0:1024], t_x2blk[:, cl, :], st["nmu2"][:, sc:sc + 1],
                st["rstd2"][:, sc:sc + 1], OP.add, OP.mult), waits=[e_rs2, prev_ln2_dma[0]])
            e_sl = dma("scalar", ln2_sp[col + 128 * cl:col + 128 * (cl + 1), :], t_ln2t[:, 0:1024], waits=[e_l2])
            prev_ln2_dma[0] = e_sl
            e_zr = vec(lambda v, cl=cl, sc=sc: v.stt(
                t_zscr[:, 0:1024], t_x2blk[:, cl, :], 1.0, t_wgbc[:], OP.mult, OP.mult,
                accum_out=st["zz"][:, sc:sc + 1]))
            prev_x2_rd[cl] = e_zr
            e_zi.append((e_sl, e_zr))
        e_tile_done2[i] = (e_zi[3][0], e_zi[3][1], e_ch_done[3][0])

    for i in range(NT):
        d_stage0(i)
        if i >= 1:
            d_stage2(i - 1)
        d_stage1(i)
    d_stage2(NT - 1)
    e_rs2 = rs2_last[0]
    e_x2_last = x2_last[0]
    e_z1 = vec(lambda v: v.ts(t_sa[:], st["nmu2"][:], t_smbc[:, 35:36], None, OP.mult))
    e_z2 = vec(lambda v: v.tt(st["zz"][:], st["zz"][:], t_sa[:], OP.add), waits=[e_z1])
    e_z3 = vec(lambda v: v.tt(st["zz"][:], st["zz"][:], st["rstd2"][:], OP.mult), waits=[e_rs2])
    P.barrier()
    if PHASE_LIM <= 4:
        _finish(nc, P, VEC, locals());  return _replay(nc, P, VEC, locals(), es, sems)

    vec(lambda v: v.memset(t_lo[:], -64.0))
    vec(lambda v: v.memset(t_hi[:], 64.0))
    vec(lambda v: v.memset(t_tm[:], 0.0))
    vec(lambda v: v.memset(t_tbc[:], 0.0))
    e_big = vec(lambda v: v.memset(t_bigt[:], BIGF))
    e_zz = vec(lambda v: v.memset(t_zeroE[:], 0.0))
    for r in range(ROUNDS):
        e_pc = None
        for b in range(NB):
            bs = slice(32 * b, 32 * b + 32)
            vec(lambda v, b=b, bs=bs: v.ts(t_mask[:, bs], st["zz"][:, bs], t_tbc[:, b:b + 1], None, OP.is_gt))
            e_pc = vec(lambda v, b=b, bs=bs: v.red(t_pcnt[:, b:b + 1], t_mask[:, bs], OP.add))
        e_cm = mm(lambda t: t.matmul(ps_a[0:1, 0:2], t_onescf[:], t_pcnt[:], start=True, stop=True), waits=[e_pc])
        e_cc = vec(lambda v: v.copy(t_cn[:], ps_a[0:1, 0:2]), waits=[e_cm])
        vec(lambda v: v.ts(t_pr2[:], t_cn[:], float(C), None, OP.is_ge))
        vec(lambda v: v.cpred(t_lo[:], t_pr2[:], t_tm[:]))
        vec(lambda v: v.ts(t_pr2[:], t_cn[:], float(C), None, OP.is_lt))
        vec(lambda v: v.cpred(t_hi[:], t_pr2[:], t_tm[:]))
        vec(lambda v: v.tt(t_tm[:], t_lo[:], t_hi[:], OP.add))
        e_t = vec(lambda v: v.ts(t_tm[:], t_tm[:], 0.5, None, OP.mult))
        e_bc = mm(lambda t: t.matmul(ps_a[0:128, 4:6], t_onesrf[0:1, :], t_tm[0:1, :], start=True, stop=True), waits=[e_t])
        vec(lambda v: v.copy(t_tbc[:], ps_a[0:128, 4:6]), waits=[e_bc])
    e_lo = vec(lambda v: v.copy(t_tm[:], t_lo[:]))
    e_bc = mm(lambda t: t.matmul(ps_a[0:128, 4:6], t_onesrf[0:1, :], t_tm[0:1, :], start=True, stop=True), waits=[e_lo])
    vec(lambda v: v.copy(t_tbc[:], ps_a[0:128, 4:6]), waits=[e_bc])
    e_pc = None
    for b in range(NB):
        bs = slice(32 * b, 32 * b + 32)
        vec(lambda v, b=b, bs=bs: v.ts(t_mask[:, bs], st["zz"][:, bs], t_tbc[:, b:b + 1], None, OP.is_gt))
        e_pc = vec(lambda v, b=b, bs=bs: v.red(t_pcnt[:, b:b + 1], t_mask[:, bs], OP.add))
    e_off = mm(lambda t: t.matmul(ps_a[0:128, 8:10], t_tri[:], t_pcnt[:], start=True, stop=True), waits=[e_pc])
    e_tot = mm(lambda t: t.matmul(ps_a[0:1, 0:2], t_onescf[:], t_pcnt[:], start=True, stop=True))
    vec(lambda v: v.copy(t_poff[:], ps_a[0:128, 8:10]), waits=[e_off])
    e_cc = vec(lambda v: v.copy(t_cn[:], ps_a[0:1, 0:2]), waits=[e_tot])
    e_cmin = vec(lambda v: v.ts(t_cn[:], t_cn[:], float(C), None, OP.min), waits=[e_cc])
    e_cb = mm(lambda t: t.matmul(ps_a[0:128, 12:14], t_onesrf[0:1, :], t_cn[0:1, :], start=True, stop=True), waits=[e_cmin])
    e_cbv = vec(lambda v: v.copy(t_cntb[:], ps_a[0:128, 12:14]), waits=[e_cb])
    vec(lambda v: v.copy(t_sa[:], t_mask[:]))
    cur, nxt = t_sa, t_sb2
    for sh in [1, 2, 4, 8, 16]:
        for b in range(NB):
            o = 32 * b
            vec(lambda v, cur=cur, nxt=nxt, o=o, sh=sh: v.copy(nxt[:, o:o + sh], cur[:, o:o + sh]))
            vec(lambda v, cur=cur, nxt=nxt, o=o, sh=sh: v.tt(nxt[:, o + sh:o + 32], cur[:, o + sh:o + 32], cur[:, o:o + 32 - sh], OP.add))
        cur, nxt = nxt, cur
    vec(lambda v, cur=cur: v.tt(cur[:], cur[:], t_mask[:], OP.subtract))
    for b in range(NB):
        bs = slice(32 * b, 32 * b + 32)
        vec(lambda v, cur=cur, b=b, bs=bs: v.ts(st["dest"][:, bs], cur[:, bs], t_poff[:, b:b + 1], None, OP.add))
        if b == 1:
            vec(lambda v, bs=bs: v.ts(st["dest"][:, bs], st["dest"][:, bs], float(C), None, OP.add))
    vec(lambda v: v.ts(t_umask[:], t_mask[:], 0.5, None, OP.is_lt))
    vec(lambda v: v.cpred(st["dest"][:], t_umask[:], t_bigt[:]), waits=[e_big])
    for b in range(NB):
        bs = slice(32 * b, 32 * b + 32)
        vec(lambda v, b=b, bs=bs: v.ts(t_umask[:, bs], st["dest"][:, bs], float((b + 1) * C), None, OP.is_ge))
        vec(lambda v, bs=bs: v.cpred(st["dest"][:, bs], t_umask[:, bs], t_bigt[:, bs]))
    e_di = vec(lambda v: v.copy(t_desti[:], st["dest"][:]))
    e_tk = None
    for k4 in range(4):
        e_tk = vec(lambda v, k4=k4: v.ts(t_tok64[:, 16 * k4:16 * (k4 + 1)], t_slotid[:],
                                         2048.0 * k4, None, OP.add))
    e_b32 = vec(lambda v: v.memset(t_big32[:], BIGF))
    e_ti = [dma("gpsimd", tokid_sp.rearrange("(c p) o -> c (p o)", p=128), t_big32[:], waits=[e_b32])]
    e_kb = []
    for b in range(NB):
        for kc in range(16):
            cix = 16 * b + kc
            vec(lambda v, cix=cix, kc=kc, b=b: v.ts(t_umask[:, cix:cix + 1], t_slotid[:, kc:kc + 1], t_cntb[:, b:b + 1], None, OP.is_ge), waits=[e_cbv])
            vec(lambda v, cix=cix: v.memset(t_kbias[:, cix:cix + 1], 0.0))
            e_kb.append(vec(lambda v, cix=cix: v.cpred(t_kbias[:, cix:cix + 1], t_umask[:, cix:cix + 1], t_negt[:])))
    P.barrier()

    e_zc_last = None
    for j in range(32):
        e_zc_last = dma("gpsimd", comp[128 * j:128 * (j + 1), :], t_zeroE[:], waits=[e_zz])
    e_breg = [None]
    e_breg2 = [None]

    def setup_breg(g):
        r = g.alloc_register()
        g.reg_mov(r, NB * C - 1)
        e_breg[0] = r
        r2 = g.alloc_register()
        g.reg_mov(r2, NB * N - 1)
        e_breg2[0] = r2
        return None
    P.emit("gpsimd", lambda g: setup_breg(g))
    e_sc_chain = [None, None]
    rows_b = [t_rows, t_rows1]
    dcol_b = [t_dcol, t_dcol1]
    for j in range(64):
        b, cix = j // 32, j % 32
        col = b * N + 128 * cix
        scc = b * 32 + cix
        par = j % 2
        rows, dcol = rows_b[par], dcol_b[par]
        e_r = dma("gpsimd", rows[:], ln2_sp[col:col + 128, :], waits=[e_sc_chain[par]])
        e_dc = gps(lambda g, scc=scc, dcol=dcol: g.tensor_copy(dcol[:], t_desti[:, scc:scc + 1]),
                   waits=[e_di] if j == 0 else [])
        e_sc_chain[par] = P.emit("gpsimd", lambda g, rows=rows, dcol=dcol: g.indirect_dma_start(
            comp, IndirectOffsetOnAxis(dcol[:], 0), rows[:], None,
            bounds_check=e_breg[0], oob_is_err=False), waits=[e_r, e_dc], inc="dmg", k=16)
        if j == 31:
            e_sc_b0 = list(e_sc_chain)
    e_sc_prev = e_sc_chain
    e_sc_bybatch = [e_sc_b0, e_sc_chain]
    P.barrier()
    if PHASE_LIM <= 5:
        _finish(nc, P, VEC, locals());  return _replay(nc, P, VEC, locals(), es, sems)

    if not SKIP_EF:
        _phase_E_F(nc, P, VEC, locals())
    _finish(nc, P, VEC, locals())
    return _replay(nc, P, VEC, locals(), es, sems)


def _finish(nc, P, VEC, env):
    if DBG:
        dma, st, dbg_st, t_macc, t_gden = env["dma"], env["st"], env["dbg_st"], env["t_macc"], env["t_gden"]
        for di, nm in enumerate(["mu1", "nmu1", "rstd1", "sqm", "skm", "sqq", "skk",
                                 "sqk", "ab", "ssc", "nmu2", "rstd2", "zz", "dest"]):
            dma("scalar", dbg_st[di], st[nm][:])
        dma("scalar", dbg_st[14].rearrange("p c -> (p c)")[0:2048].rearrange("(o a) -> o a", o=1), t_macc[0:1, :])
        dma("scalar", dbg_st[15][:, 0:16], t_gden[:])
    P.emit("sync", lambda s: s.nop(), waits=[("dmsc", P.counts.get("dmsc", 0)),
                                             ("dmg", P.counts.get("dmg", 0)),
                                             ("dms", P.counts.get("dms", 0))])
    P.barrier()


def _replay(nc, P, VEC, env, es, sems):
    P.finalize()
    with nc.Block() as block:
        def section(eng_name):
            def run(eng):
                if eng_name == "vector":
                    VEC[0] = Vec(eng)
                for fn, waits, inc, k, v in P.ops[eng_name]:
                    if fn == "BARRIER":
                        eng.sem_inc(sems["bar"], 1)
                        eng.wait_ge(sems["bar"], inc)
                        if eng_name == "vector" and VEC[0] is not None:
                            VEC[0].dirty.clear()
                        continue
                    for (s, vv) in waits:
                        eng.wait_ge(sems[s], vv)
                    ins = fn(eng)
                    if inc is not None and ins is not None:
                        ins.then_inc(sems[inc], k)
                    elif inc is not None and ins is None:
                        eng.sem_inc(sems[inc], k)
            return run

        block.sync(section("sync"))
        block.scalar(section("scalar"))
        block.vector(section("vector"))
        block.tensor(section("tensor"))
        block.gpsimd(section("gpsimd"))

    es.close()
    return nc


def _phase_E_F(nc, P, VEC, env):
    g = env
    sems = g["sems"]
    t_l2sT, t_wffn, t_vffn, t_aoT, t_expS = g["t_l2sT"], g["t_wffn"], g["t_vffn"], g["t_aoT"], g["t_expS"]
    t_vffn4 = g["t_vffn4"]
    t_kh, t_qh, t_wo2, t_orow = g["t_kh"], g["t_qh"], g["t_wo2"], g["t_orow"]
    t_rows, t_facc, t_fx2, t_drow = g["t_rows"], g["t_facc"], g["t_fx2"], g["t_drow"]
    t_kbias, t_gden, t_gsel, t_wgbf = g["t_kbias"], g["t_gden"], g["t_gsel"], g["t_wgbf"]
    t_onesall, t_ident, t_desti = g["t_onesall"], g["t_ident"], g["t_desti"]
    t_qksc, t_dcol = g["t_qksc"], g["t_dcol"]
    st = g["st"]
    ps_a, ps_b = g["ps_a"], g["ps_b"]
    comp, qT_sp, kT_sp, ffn_sp, x2_sp, out_t = g["comp"], g["qT_sp"], g["kT_sp"], g["ffn_sp"], g["x2_sp"], g["out_t"]
    vT_sp = g["vT_sp"]
    wo_bf = g["wo_bf"]
    t_lnrow = g["t_lnrow"]
    wq_in, wk_in, wv_in, wo_in = g["wq_in"], g["wk_in"], g["wv_in"], g["wo_in"]
    dma, dmat, mm, act, vec, gps = g["dma"], g["dmat"], g["mm"], g["act"], g["vec"], g["gps"]
    e_breg = g["e_breg"]
    e_breg2 = g["e_breg2"]
    tokid_sp, t_stok, t_stokT = g["tokid_sp"], g["t_stok"], g["t_stokT"]
    t_tok64, t_dcol2 = g["t_tok64"], g["t_dcol2"]
    e_x2_last = g["e_x2_last"]
    e_sc_last = list(g["e_sc_prev"])
    e_tok_last = None
    for j in range(64):
        scc = j
        e_dc2 = gps(lambda gg, scc=scc: gg.tensor_copy(t_dcol2[:], t_desti[:, scc:scc + 1]),
                    waits=(g["e_ti"] + [g["e_tk"], g["e_di"]]) if j == 0 else [])
        e_tok_last = P.emit("gpsimd", lambda gg, scc=scc: gg.indirect_dma_start(
            tokid_sp, IndirectOffsetOnAxis(t_dcol2[:], 0), t_tok64[:, scc:scc + 1], None,
            bounds_check=g["e_breg"][0], oob_is_err=False),
            waits=[e_dc2], inc="dmg", k=16)

    g["dbg_evs"] = []
    e_prev_fdma = None
    for b in range(NB):
        if b > 0:
            P.barrier()
        e_tl = []
        for cc in range(8):
            e_tl.append(dmat("sync", t_l2sT[:, cc, :], comp[b * C:(b + 1) * C, 128 * cc:128 * (cc + 1)],
                             waits=list(g["dbg_evs"]) + list(g["e_sc_bybatch"][b])))
        e_on = [vec(lambda v: v.memset(t_vffn[:, :, :], 1.0), waits=[g["e_zc_last"]])]
        e_zm = None
        for qt in range(4):
            for cc in range(8):
                e_zm = mm(lambda t, qt=qt, cc=cc: t.matmul(
                    ps_b[0:1, 512 * qt:512 * (qt + 1)], t_wgbf[:, cc, :],
                    t_l2sT[:, cc, 512 * qt:512 * (qt + 1)],
                    start=(cc == 0), stop=(cc == 7)), waits=e_tl)
        e_gs = act(lambda s: s.activation(t_gsel[0:1, :], ps_b[0:1, 0:2048], AF.Sigmoid), waits=[e_zm])
        if b == 0 and DBG:
            P.emit("gpsimd", lambda g: g.dma_start(vT_sp[0:1, 0:2048], t_gsel[0:1, :]), waits=[e_gs], inc="dmg", k=16)
            P.emit("gpsimd", lambda g: g.dma_start(vT_sp[1:3, 0:2048], t_l2sT[0:2, 0, :]), waits=list(e_tl), inc="dmg", k=16)
            P.emit("gpsimd", lambda g: g.dma_start(vT_sp[3:5, 0:2048], t_l2sT[0:2, 7, :]), waits=list(e_tl), inc="dmg", k=16)
        e_gt = []
        for kc in range(16):
            e_t = mm(lambda t, kc=kc: t.transpose(ps_a[0:128, 0:1],
                                                  t_gsel[0:1, 128 * kc:128 * (kc + 1)], t_ident[0:1, 0:1]),
                     waits=[e_gs] + ([e_gt[-1]] if e_gt else []))
            e_gt.append(vec(lambda v, kc=kc: v.copy(t_gden[:, kc:kc + 1], ps_a[0:128, 0:1]), waits=[e_t]))
        e_spill = []
        e_last_wmm = e_zm
        for w_i, (w_dram, dst) in enumerate([(g["wq_bf"], qT_sp), (g["wk_bf"], kT_sp)]):
            e_w = dma("scalar", t_wffn[:], w_dram.rearrange("(c p) o -> p c o", p=128),
                      waits=[e_last_wmm] + g["e_wcast"])
            e_par = [e_gt[-1], e_gt[-1]]
            for oc in range(8):
                par = oc % 2
                pp = ps_a[:, 0:512] if par == 0 else ps_a[:, 512:1024]
                for qt in range(4):
                    e_mm = None
                    for cc in range(8):
                        e_mm = mm(lambda t, oc=oc, qt=qt, cc=cc, pp=pp: t.matmul(
                            pp, t_wffn[:, cc, 128 * oc:128 * (oc + 1)],
                            t_l2sT[:, cc, 512 * qt:512 * (qt + 1)],
                            start=(cc == 0), stop=(cc == 7)),
                            waits=([e_w] + e_tl + [e_par[par]]) if cc == 0 else [])
                    e_ev = act(lambda s, pp=pp: s.activation(t_qksc[:, 0:512], pp, AF.Identity),
                               waits=[e_mm] + (e_spill[-1:] if e_spill else []))
                    e_par[par] = e_ev
                    e_last_wmm = e_mm
                    e_spill.append(dma("scalar", dst[b, 128 * oc:128 * (oc + 1), 512 * qt:512 * (qt + 1)],
                                       t_qksc[:, 0:512], waits=[e_ev]))
        e_wv = dma("scalar", t_wffn[:], g["wv_bf"].rearrange("(c p) o -> p c o", p=128),
                   waits=[e_last_wmm] + g["e_wcast"])
        e_vgrp = []
        for kc in range(16):
            for ot in range(2):
                gi = 2 * kc + ot
                par = gi % 2
                pp = ps_b[:, 0:512] if par == 0 else ps_b[:, 512:1024]
                w0 = [e_wv, e_gs] + e_on
                if gi >= 2:
                    w0 = list(e_vgrp[gi - 2])
                e_mm = None
                for cc in range(8):
                    e_mm = mm(lambda t, kc=kc, ot=ot, cc=cc, pp=pp: t.matmul(
                        pp, t_l2sT[:, cc, 128 * kc:128 * (kc + 1)],
                        t_wffn[:, cc, 512 * ot:512 * (ot + 1)],
                        start=(cc == 0), stop=(cc == 7)), waits=w0 if cc == 0 else [])
                grp = [vec(lambda v, kc=kc, ot=ot, pp=pp: v.copy(
                    t_vffn4[:, kc, 8 * ot:8 * (ot + 1), 0:64], pp[:, 0:512]), waits=[e_mm])]
                e_vgrp.append(grp)
        e_wo = dma("sync", t_wo2[:], wo_bf.rearrange("(h p) o -> p h o", p=64), waits=g["e_wcast"])
        e_prev_av = e_vgrp[-1][-1]
        e_na = e_vgrp[-1][-1]
        e_prev_scores = None
        e_prev_avmm = None
        for h in range(H):
            e_kl = dma("sync", t_kh[:], kT_sp[b, DH * h:DH * (h + 1), :],
                       waits=[e_spill[-1], e_prev_scores])
            e_ql = dma("sync", t_qh[:], qT_sp[b, DH * h:DH * (h + 1), :],
                       waits=[e_spill[-1], e_prev_scores])
            e_qt_av = []
            for qt in range(4):
                e_ex = []
                for kc in range(16):
                    par = kc % 2
                    sps = ps_a[:, 0:512] if par == 0 else ps_a[:, 512:1024]
                    e_s = mm(lambda t, kc=kc, qt=qt, sps=sps: t.matmul(
                        sps, t_kh[:, 128 * kc:128 * (kc + 1)], t_qh[:, 512 * qt:512 * (qt + 1)],
                        start=True, stop=True),
                        waits=[e_kl, e_ql] + ([e_ex[kc - 2]] if kc >= 2 else []))
                    w_ex = [e_s, e_prev_avmm]
                    if h == 0 and qt == 0 and kc == 0:
                        w_ex += e_sc_last + [g["e_kb"][-1]]
                    e_ex.append(act(lambda s, kc=kc, b=b, sps=sps: s.activation(
                        t_expS[:, kc, :], sps, AF.Exp,
                        bias=t_kbias[:, 16 * b + kc:16 * b + kc + 1],
                        scale=0.125), waits=w_ex))
                e_prev_scores = e_ex[-1]
                e_mm = None
                for kc in range(16):
                    e_mm = mm(lambda t, kc=kc, h=h: t.matmul(
                        ps_a[0:65, 1024:1536], t_vffn[:, kc, 65 * h:65 * (h + 1)],
                        t_expS[:, kc, :], start=(kc == 0), stop=(kc == 15)),
                        waits=[e_ex[kc]] + ([e_prev_av] if kc == 0 else []))
                e_prev_avmm = e_mm
                e_a1 = vec(lambda v, h=h, qt=qt: v.copy(t_aoT[:, h, 512 * qt:512 * (qt + 1)],
                                                        ps_a[0:64, 1024:1536]), waits=[e_mm])
                e_d1 = vec(lambda v, qt=qt: v.copy(t_drow[64:65, 512 * qt:512 * (qt + 1)],
                                                   ps_a[64:65, 1024:1536]), waits=[e_mm])
                e_prev_av = e_d1
                e_qt_av.append(e_a1)
            e_rc0 = act(lambda s: s.activation(t_drow[64:65, :], t_drow[64:65, :], AF.Ln),
                        waits=[e_prev_av])
            e_rc = act(lambda s: s.activation(t_drow[64:65, :], t_drow[64:65, :], AF.Exp,
                                              scale=-1.0), waits=[e_rc0])
            e_ntt = [None, None]
            for qt in range(4):
                par = qt % 2
                npp = ps_b[0:64, 1024 + 512 * par:1536 + 512 * par]
                e_nb = mm(lambda t, qt=qt, npp=npp: t.matmul(
                    npp, t_onesall[64:65, 0:64], t_drow[64:65, 512 * qt:512 * (qt + 1)],
                    start=True, stop=True), waits=[e_rc] + ([e_ntt[par]] if e_ntt[par] else []))
                e_ntt[par] = vec(lambda v, h=h, qt=qt, npp=npp: v.tt(
                    t_aoT[:, h, 512 * qt:512 * (qt + 1)],
                    t_aoT[:, h, 512 * qt:512 * (qt + 1)], npp, OP.mult),
                    waits=[e_nb, e_qt_av[qt]])
            e_na = e_ntt[1]
            if h == 0 and b == 0 and DBG:
                e_dbg1 = P.emit("gpsimd", lambda g: g.dma_start(vT_sp[8:72, 0:2048], t_aoT[:, 0, :]),
                                waits=[e_na], inc="dmg", k=16)
                e_dbg2 = P.emit("gpsimd", lambda g: g.dma_start(vT_sp[5:6, 0:2048], t_drow[64:65, :]),
                                waits=[e_prev_av], inc="dmg", k=16)
                g["dbg_evs"].extend([e_dbg1, e_dbg2])
        if b == 0 and g.get("dbg_big") is not None:
            dbg_big = g["dbg_big"]
            for hh2 in range(16):
                e_dmp = P.emit("gpsimd", lambda g2, hh2=hh2, dbg_big=dbg_big: g2.dma_start(
                    dbg_big[64 * hh2:64 * (hh2 + 1), 0:2048], t_aoT[:, hh2, :]),
                    waits=[e_na], inc="dmg", k=16)
                g["dbg_evs"].append(e_dmp)
        e_ix0 = dma("sync", t_stokT[:],
                    tokid_sp[b * C:(b + 1) * C, :].rearrange("(c p) o -> c (p o)", p=128),
                    waits=[e_tok_last])
        e_trx = mm(lambda t: t.transpose(ps_a[0:128, 1536:1552], t_stokT[:], t_ident[0:16, 0:16]),
                   waits=[e_ix0])
        e_ix = vec(lambda v, b=b: v.copy(t_stok[:, 16 * b:16 * (b + 1)], ps_a[0:128, 1536:1552]),
                   waits=[e_trx])
        e_prev_ev = [e_na, e_na]
        e_prev_eb = None
        for sl in range(16):
            par = sl % 2
            pp = ps_b[:, 0:1024] if par == 0 else ps_b[:, 1024:2048]
            e_mm = None
            for ot in range(2):
                for h in range(H):
                    e_mm = mm(lambda t, sl=sl, h=h, ot=ot, pp=pp: t.matmul(
                        pp[:, 512 * ot:512 * (ot + 1)],
                        t_aoT[:, h, 128 * sl:128 * (sl + 1)],
                        t_wo2[:, h, 512 * ot:512 * (ot + 1)],
                        start=(h == 0), stop=(h == H - 1)),
                        waits=[e_na, e_wo, e_prev_ev[par]] if (h == 0 and ot == 0) else [])
            e_g = vec(lambda v, sl=sl, pp=pp: v.ts(t_orow[:], pp, t_gden[:, sl:sl + 1], None, OP.mult),
                      waits=[e_mm, e_gt[-1], e_prev_eb])
            if b == 0 and sl == 2 and g.get("dbg_big") is not None:
                e_g = P.emit("gpsimd", lambda g2: g2.dma_start(g["dbg_big"][1024:1152, 0:1024], t_orow[:]),
                             waits=[e_g], inc="dmg", k=16)
                g["dbg_evs"].append(e_g)
            if b == 0 and sl == 0:
                e_gd = P.emit("gpsimd", lambda g2: g2.dma_start(vT_sp[80:208, 0:16], t_gden[:]),
                              waits=[e_gt[-1]], inc="dmg", k=16)
                g["dbg_evs"].append(e_gd)
            e_prev_ev[par] = e_g
            ea = act(lambda s: s.activation(t_qksc[:, 0:1024], t_orow[:], AF.Identity,
                                            accum_out=t_drow[0:128, 0:1]), waits=[e_g])
            eb = act(lambda s: s.activation(t_qksc[:, 0:1024], t_orow[:], AF.Square,
                                            accum_out=t_drow[0:128, 1:2]), waits=[e_g])
            e_prev_eb = eb
            vec(lambda v: v.ts(t_drow[0:128, 0:1], t_drow[0:128, 0:1], 1.0 / D, None, OP.mult), waits=[ea, eb])
            vec(lambda v: v.tt(t_drow[0:128, 2:3], t_drow[0:128, 0:1], t_drow[0:128, 0:1], OP.mult))
            vec(lambda v: v.ts(t_drow[0:128, 1:2], t_drow[0:128, 1:2], 1.0 / D, None, OP.mult))
            vec(lambda v: v.tt(t_drow[0:128, 1:2], t_drow[0:128, 1:2], t_drow[0:128, 2:3], OP.subtract))
            ev5 = vec(lambda v: v.ts(t_drow[0:128, 1:2], t_drow[0:128, 1:2], EPS, None, OP.add))
            ev6 = act(lambda s: s.activation(t_drow[0:128, 1:2], t_drow[0:128, 1:2], AF.Sqrt), waits=[ev5])
            ev7 = vec(lambda v: v.recip(t_drow[0:128, 1:2], t_drow[0:128, 1:2]), waits=[ev6])
            ev8 = vec(lambda v: v.ts(t_drow[0:128, 0:1], t_drow[0:128, 0:1], -1.0, None, OP.mult))
            e_ln = vec(lambda v: v.ts(t_lnrow[:], t_orow[:], t_drow[0:128, 0:1], t_drow[0:128, 1:2],
                                      OP.add, OP.mult), waits=[ev7, ev8, e_prev_fdma])
            e_prev_fdma = P.emit("gpsimd", lambda gg, b=b, sl=sl: gg.indirect_dma_start(
                out_t, IndirectOffsetOnAxis(t_stok[:, 16 * b + sl:16 * b + sl + 1], 0),
                t_lnrow[:], None, bounds_check=e_breg2[0], oob_is_err=False,
                compute_op=mybir.AluOpType.add),
                waits=[e_ln, e_ix, e_x2_last], inc="dmg", k=16)
    P.barrier()


_CACHE = {}


def kernel(**inputs):
    B = 16
    x = np.asarray(inputs["L0_attn_x"], dtype=np.float32)
    if "nc" not in _CACHE:
        _CACHE["nc"] = build_nc()
    nc = _CACHE["nc"]

    tri = (np.arange(128)[:, None] < np.arange(128)[None, :]).astype(np.float32)
    onescf = np.ones((128, 1), np.float32)
    onesrf = np.ones((1, 128), np.float32)
    onesall = np.ones((128, 128), np.float32)
    ident = np.eye(128, dtype=np.float32)
    slotid = (np.arange(128)[:, None] + 128.0 * np.arange(16)[None, :]).astype(np.float32)

    shared = dict(
        qkv_w=np.ascontiguousarray(inputs["L0_attn_qkv_w"], dtype=np.float32),
        proj_w=np.ascontiguousarray(inputs["L0_attn_proj_w"], dtype=np.float32),
        ch_w1=np.ascontiguousarray(inputs["L0_attn_ch_w1"], dtype=np.float32),
        ch_w2=np.ascontiguousarray(inputs["L0_attn_ch_w2"], dtype=np.float32),
        ch_b1=np.ascontiguousarray(inputs["L0_attn_ch_b1"], dtype=np.float32),
        ch_b2=np.ascontiguousarray(inputs["L0_attn_ch_b2"], dtype=np.float32),
        proj_b=np.ascontiguousarray(inputs["L0_attn_proj_b"], dtype=np.float32),
        wg_in=np.ascontiguousarray(inputs["L0_ffn_Wg"], dtype=np.float32),
        wq_in=np.ascontiguousarray(inputs["L0_ffn_Wq"], dtype=np.float32),
        wk_in=np.ascontiguousarray(inputs["L0_ffn_Wk"], dtype=np.float32),
        wv_in=np.ascontiguousarray(inputs["L0_ffn_Wv"], dtype=np.float32),
        wo_in=np.ascontiguousarray(inputs["L0_ffn_Wo"], dtype=np.float32),
        in_c1=np.asarray(inputs["L0_attn_c1"], dtype=np.float32).reshape(1),
        in_c2=np.asarray(inputs["L0_attn_c2"], dtype=np.float32).reshape(1),
        in_conv1=np.ascontiguousarray(inputs["L0_attn_conv1_w"], dtype=np.float32),
        in_gn_g=np.ascontiguousarray(inputs["L0_attn_gn_g"], dtype=np.float32),
        in_gn_b=np.ascontiguousarray(inputs["L0_attn_gn_b"], dtype=np.float32),
        in_conv2=np.ascontiguousarray(inputs["L0_attn_conv2_w"], dtype=np.float32),
        in_conv2b=np.asarray(inputs["L0_attn_conv2_b"], dtype=np.float32).reshape(1),
        tri_in=tri, onescf_in=onescf, onescr_in=onescf, onesrf_in=onesrf,
        onesrr_in=onesrf, onesall_in=onesall, ident_in=ident, slotid_in=slotid,
    )
    in_maps = []
    for c in range(8):
        xs = x[2 * c:2 * c + 2]
        x_tm = np.ascontiguousarray(xs.reshape(2 * N, D))
        xT = np.ascontiguousarray(np.moveaxis(xs, 2, 0).reshape(D, 2 * N))
        m = dict(shared)
        m["x_tm"] = x_tm
        m["xT"] = xT
        in_maps.append(m)

    res = run_bass_kernel_spmd(nc, in_maps, core_ids=list(range(8)))
    out = np.concatenate([r["out_t"].reshape(2, N, D) for r in res.results], axis=0)
    return out


# version 10 (final)
# speedup vs baseline: 1.0899x; 1.0455x over previous
import numpy as np
import concourse.bass as bass
import concourse.mybir as mybir
from concourse.bass_utils import run_bass_kernel_spmd
from concourse.bass import IndirectOffsetOnAxis
from contextlib import ExitStack

F32 = mybir.dt.float32
F32R = mybir.dt.float32r
BF16 = mybir.dt.bfloat16
I32 = mybir.dt.int32
U32 = mybir.dt.uint32
OP = mybir.AluOpType
AF = mybir.ActivationFunctionType
AX = mybir.AxisListType.X

NB, N, D, C, H, DH = 2, 4096, 1024, 2048, 16, 64
NT = 16
ROUNDS = 24
DBG = False
SKIP_EF = False
PHASE_LIM = 99
SETUP_LIM = 99
A_TILES = 16
A_STAGE = 99
DUMP_XC = False
BIGF = float(1 << 22)
EPS = 1e-5
INVCM1 = 1.0 / (D - 1)

ENGINES = ("sync", "scalar", "vector", "tensor", "gpsimd")


class Prog:
    def __init__(self):
        self.ops = {e: [] for e in ENGINES}
        self.counts = {}
        self.hw = {e: {} for e in ENGINES}

    def emit(self, eng, fn, waits=(), inc=None, k=1):
        w = []
        for ev in waits:
            if ev is None:
                continue
            s, v = ev
            if self.hw[eng].get(s, 0) < v:
                w.append((s, v))
                self.hw[eng][s] = v
        ev_out = None
        v = 0
        if inc is not None:
            self.counts[inc] = self.counts.get(inc, 0) + k
            v = self.counts[inc]
            ev_out = (inc, v)
        self.ops[eng].append((fn, tuple(w), inc, k, v))
        return ev_out

    def barrier(self):
        snap = {s: v for s, v in self.counts.items() if s in ("tn", "ve", "ac", "gp")}
        self.counts["bar"] = self.counts.get("bar", 0) + 5
        tgt = self.counts["bar"]
        for e in ENGINES:
            self.ops[e].append(("BARRIER", (), tgt, 1, tgt))
            hw = self.hw[e]
            for s, v in snap.items():
                hw[s] = max(hw.get(s, 0), v)
            hw["bar"] = tgt


    def finalize(self):
        waited = {}
        for e in ENGINES:
            for op in self.ops[e]:
                if op[0] == "BARRIER":
                    continue
                for (s, v) in op[1]:
                    waited.setdefault(s, set()).add(v)
        remap = {}
        for s in ("tn", "ve", "ac", "gp"):
            vals = sorted(waited.get(s, ()))
            remap[s] = {v: i + 1 for i, v in enumerate(vals)}
        for e in ENGINES:
            newops = []
            for fn, w, inc, k, v in self.ops[e]:
                if fn == "BARRIER":
                    newops.append((fn, w, inc, k, v))
                    continue
                w2 = tuple((s, remap[s][x]) if s in remap else (s, x) for (s, x) in w)
                if inc in remap:
                    if v in remap[inc]:
                        newops.append((fn, w2, inc, k, remap[inc][v]))
                    else:
                        newops.append((fn, w2, None, 0, 0))
                else:
                    newops.append((fn, w2, inc, k, v))
            self.ops[e] = newops


class Vec:

    def __init__(self, v):
        self.v = v
        self.dirty = set()

    def _nm(self, ap):
        t = getattr(ap, "tensor", None)
        return t.name if t is not None else None

    def _pre(self, aps):
        names = {self._nm(a) for a in aps if a is not None} - {None}
        if self.dirty & names:
            self.v.drain()
            self.dirty.clear()

    def _post(self, out, extra=None):
        self.dirty.add(self._nm(out))
        if extra is not None:
            self.dirty.add(self._nm(extra))

    def ts(self, out, in0, s1, s2, op0, op1=None, accum_out=None):
        ins = [in0, out] + [s for s in (s1, s2) if hasattr(s, "tensor")]
        self._pre(ins)
        if op1 is None:
            r = self.v.tensor_scalar(out, in0, s1, s2, op0, accum_out=accum_out)
        else:
            r = self.v.tensor_scalar(out, in0, s1, s2, op0, op1, accum_out=accum_out)
        self._post(out, accum_out)
        return r

    def tt(self, out, a, b, op):
        self._pre([a, b, out])
        r = self.v.tensor_tensor(out, a, b, op)
        self._post(out)
        return r

    def stt(self, out, in0, scalar, in1, op0, op1, accum_out=None):
        ins = [in0, in1, out] + ([scalar] if hasattr(scalar, "tensor") else [])
        self._pre(ins)
        r = self.v.scalar_tensor_tensor(out, in0, scalar, in1, op0, op1, accum_out=accum_out)
        self._post(out, accum_out)
        return r

    def red(self, out, in_, op):
        self._pre([in_, out])
        r = self.v.tensor_reduce(out, in_, AX, op)
        self._post(out)
        return r

    def copy(self, out, in_):
        self._pre([in_, out])
        r = self.v.tensor_copy(out, in_)
        self._post(out)
        return r

    def memset(self, out, c):
        self._pre([out])
        r = self.v.memset(out, c)
        self._post(out)
        return r

    def cpred(self, out, mask, data):
        self._pre([out, mask, data])
        r = self.v.copy_predicated(out, mask, data)
        self._post(out)
        return r

    def recip(self, out, in_):
        self._pre([in_, out])
        r = self.v.reciprocal(out, in_)
        self._post(out)
        return r


def build_nc():
    nc = bass.Bass("TRN2", target_bir_lowering=False)
    es = ExitStack()
    P = Prog()
    dt_ = nc.dram_tensor

    xT = dt_("xT", [D, NB * N], F32R, kind="ExternalInput").ap()
    x_tm = dt_("x_tm", [NB * N, D], F32, kind="ExternalInput").ap()
    qkv_w = dt_("qkv_w", [D, 3 * D], F32R, kind="ExternalInput").ap()
    proj_w = dt_("proj_w", [D, D], F32R, kind="ExternalInput").ap()
    ch_w1 = dt_("ch_w1", [D, 256], F32R, kind="ExternalInput").ap()
    ch_w2 = dt_("ch_w2", [256, D], F32R, kind="ExternalInput").ap()
    ch_b1 = dt_("ch_b1", [256], F32, kind="ExternalInput").ap()
    ch_b2 = dt_("ch_b2", [D], F32, kind="ExternalInput").ap()
    proj_b = dt_("proj_b", [D], F32, kind="ExternalInput").ap()
    wg_in = dt_("wg_in", [D, 1], F32, kind="ExternalInput").ap()
    wq_in = dt_("wq_in", [D, D], F32, kind="ExternalInput").ap()
    wk_in = dt_("wk_in", [D, D], F32, kind="ExternalInput").ap()
    wv_in = dt_("wv_in", [D, D], F32, kind="ExternalInput").ap()
    wo_in = dt_("wo_in", [D, D], F32, kind="ExternalInput").ap()
    sm_in = {}
    for nm, shp in [("c1", [1]), ("c2", [1]), ("conv1", [4, 1, 3]), ("gn_g", [4]),
                    ("gn_b", [4]), ("conv2", [1, 4, 3]), ("conv2b", [1])]:
        sm_in[nm] = dt_("in_" + nm, shp, F32, kind="ExternalInput").ap()
    tri_in = dt_("tri_in", [128, 128], F32, kind="ExternalInput").ap()
    onescf_in = dt_("onescf_in", [128, 1], F32, kind="ExternalInput").ap()
    onescr_in = dt_("onescr_in", [128, 1], F32R, kind="ExternalInput").ap()
    onesrf_in = dt_("onesrf_in", [1, 128], F32, kind="ExternalInput").ap()
    onesrr_in = dt_("onesrr_in", [1, 128], F32R, kind="ExternalInput").ap()
    onesall_in = dt_("onesall_in", [128, 128], F32, kind="ExternalInput").ap()
    ident_in = dt_("ident_in", [128, 128], F32, kind="ExternalInput").ap()
    slotid_in = dt_("slotid_in", [128, 16], F32, kind="ExternalInput").ap()

    out_t = dt_("out_t", [NB * N, D], F32, kind="ExternalOutput").ap()
    vT_sp = dt_("vT_sp", [D, NB * N], F32R, kind="ExternalOutput" if DBG else "Internal").ap()
    x2_sp = dt_("x2_sp", [NB * N, D], F32, kind="ExternalOutput" if DBG else "Internal").ap()
    ln2_sp = dt_("ln2_sp", [NB * N, D], BF16, kind="ExternalOutput" if DBG else "Internal").ap()
    ab_pad = dt_("ab_pad", [NB, N + 128], F32, kind="Internal").ap()
    hh_pad = dt_("hh_pad", [NB, 4, N + 128], F32, kind="Internal").ap()
    comp = dt_("comp", [NB * C, D], BF16, kind="ExternalOutput" if DBG else "Internal").ap()
    qT_sp = dt_("qT_sp", [NB, D, C], BF16, kind="ExternalOutput" if DBG else "Internal").ap()
    kT_sp = dt_("kT_sp", [NB, D, C], BF16, kind="Internal").ap()
    ffn_sp = dt_("ffn_sp", [NB * C, D], F32, kind="ExternalOutput" if DBG else "Internal").ap()
    tokid_sp = dt_("tokid_sp", [NB * C, 1], F32, kind="Internal").ap()
    wo_bf = dt_("wo_bf", [D, D], BF16, kind="Internal").ap()
    dbg_st = dt_("dbg_st", [16, 128, 64], F32, kind="ExternalOutput").ap() if DBG else None
    dbg_big = dt_("dbg_big", [1152, 2048], F32, kind="ExternalOutput").ap() if DBG else None

    sbuf = lambda *a: es.enter_context(nc.sbuf_tensor(*a))
    AR = 166 * 1024
    t_arena = es.enter_context(nc.sbuf_tensor("arena", [128, AR], mybir.dt.uint8))
    arena_base = nc.lookup_mloc(t_arena).addr

    def place(name, shape, dtype, off_kb):
        import functools, operator
        hnd = nc.virtual_sbuf_tensor(name, shape, dtype)
        nc._tensor(hnd.name, hnd.shape, hnd.dtype, type="SB")
        mloc = nc.lookup_mloc(hnd)
        mloc.allocated = True
        mloc.addr = arena_base + int(off_kb * 1024)
        pp = functools.reduce(operator.mul, shape[1:], 1) * mybir.dt.size(dtype)
        assert int(off_kb * 1024) + pp <= AR, (name, off_kb, pp)
        return hnd

    t_wqk = place("t_wqk", [128, 8, 2048], F32R, 0)
    t_wv = place("t_wv", [128, 8, 1024], F32R, 64)
    t_proj = place("t_proj", [128, 8, 1024], F32R, 0)
    t_chw1 = place("t_chw1", [128, 8, 256], F32R, 64)
    t_chw2 = place("t_chw2", [128, 2, 1024], F32R, 72)
    t_attnblk = place("t_attnblk", [128, 4, 1024], F32, 32)
    t_x2blk = place("t_x2blk", [128, 4, 1024], F32, 48)
    t_xt = [place(f"t_xt{j}", [128, 8, 512], F32R, 96 + 16 * j) for j in range(2)]
    t_xtm = [place(f"t_xtm{j}", [128, 4, 1024], F32, 128 + 16 * j) for j in range(2)]
    ode_names = ["am1", "a0", "ap1", "hh0", "hh1", "hh2", "hh3", "tmp", "yy", "fa"]
    t_ode = {nm: place("t_ode_" + nm, [32, 128], F32, 160 + 0.5 * k)
             for k, nm in enumerate(ode_names)}
    t_l2sT = place("t_l2sT", [128, 8, 2048], BF16, 0)
    t_wffn = place("t_wffn", [128, 8, 1024], BF16, 32)
    t_vffn = place("t_vffn", [128, 16, 1040], BF16, 64)
    t_vffn4 = place("t_vffn4", [128, 16, 16, 65], BF16, 64)
    t_attnblkB = place("t_attnblkB", [128, 4, 1024], F32, 64)
    t_aoT = place("t_aoT", [64, 16, 2048], BF16, 0)
    t_expS = place("t_expS", [128, 16, 512], BF16, 97)
    t_wo2 = place("t_wo2", [64, 16, 1024], BF16, 113)
    t_kh = place("t_kh", [64, 2048], BF16, 145)
    t_qh = place("t_qh", [64, 2048], BF16, 149)
    t_orow = place("t_orow", [128, 1024], F32, 153)
    t_drow = place("t_drow", [128, 2048], F32, 157)
    t_zeroE = place("t_zeroE", [128, 1024], BF16, 97)
    t_rows = place("t_rows", [128, 1024], BF16, 99)
    t_rows1 = place("t_rows1", [128, 1024], BF16, 101)
    t_facc = place("t_facc", [128, 1024], F32, 0)
    t_fx2 = place("t_fx2", [128, 1024], F32, 4)
    t_x2t = place("t_x2t", [128, 1024], F32, 80)
    t_biasbc = place("t_biasbc", [128, 1024], F32, 88)
    t_wgbc = place("t_wgbc", [128, 1024], F32, 92)
    t_bias_r = place("t_bias_r", [1, 1024], F32, 80)
    t_wgrow = place("t_wgrow", [1, 1024], F32, 84)
    t_ln2t = place("t_ln2t", [128, 1024], BF16, 84)
    t_gsel = place("t_gsel", [1, 2048], F32, 97)

    t_vout = sbuf("t_vout", [128, 2, 512], F32R)
    t_xm = sbuf("t_xm", [128, 1024], F32R)
    t_qksc = sbuf("t_qksc", [128, 1024], BF16)
    t_qksc2 = sbuf("t_qksc2", [128, 1024], BF16)
    t_zscr = sbuf("t_zscr", [128, 1024], BF16)
    t_lnrow = sbuf("t_lnrow", [128, 1024], F32)
    stat_names = ["mu1", "nmu1", "rstd1", "sqm", "skm", "sqq", "skk", "sqk",
                  "ab", "ssc", "nmu2", "rstd2", "zz", "dest", "scr1", "scr2"]
    st = {nm: sbuf("st_" + nm, [128, 64], F32) for nm in stat_names}
    t_desti = sbuf("t_desti", [128, 64], I32)
    t_smalls = sbuf("t_smalls", [1, 64], F32)
    t_smbc = sbuf("t_smbc", [128, 64], F32)
    t_nmrow = sbuf("t_nmrow", [1, 512], F32R)
    t_onescf = sbuf("t_onescf", [128, 1], F32)
    t_onescr = sbuf("t_onescr", [128, 1], F32R)
    t_onesrf = sbuf("t_onesrf", [1, 128], F32)
    t_onesrr = sbuf("t_onesrr", [1, 128], F32R)
    t_onesall = sbuf("t_onesall", [128, 128], F32)
    t_tri = sbuf("t_tri", [128, 128], F32)
    t_ident = sbuf("t_ident", [128, 128], F32)
    t_slotid = sbuf("t_slotid", [128, 16], F32)
    t_macc = sbuf("t_macc", [1, 2048], F32)
    t_mrow = sbuf("t_mrow", [128, 8, 2], F32R)
    t_wbar = sbuf("t_wbar", [128, 8, 2], F32R)
    t_hrel = sbuf("t_hrel", [128, 2, 2], F32R)
    t_chT = sbuf("t_chT", [128, 8, 2], F32)
    t_wgbf = sbuf("t_wgbf", [128, 8, 1], BF16)
    t_odes = sbuf("t_odes", [32, 16], F32)
    t_vred = sbuf("t_vred", [128, 32], F32)
    t_sc1 = sbuf("t_sc1", [1, 32], F32)
    t_sc2 = sbuf("t_sc2", [1, 32], F32)
    t_lo = sbuf("t_lo", [1, 2], F32)
    t_hi = sbuf("t_hi", [1, 2], F32)
    t_tm = sbuf("t_tm", [1, 2], F32)
    t_cn = sbuf("t_cn", [1, 2], F32)
    t_pr2 = sbuf("t_pr2", [1, 2], U32)
    t_tbc = sbuf("t_tbc", [128, 2], F32)
    t_mask = sbuf("t_mask", [128, 64], F32)
    t_umask = sbuf("t_umask", [128, 64], U32)
    t_pcnt = sbuf("t_pcnt", [128, 2], F32)
    t_sa = sbuf("t_sa", [128, 64], F32)
    t_sb2 = sbuf("t_sb2", [128, 64], F32)
    t_bigt = sbuf("t_bigt", [128, 64], F32)
    t_negt = sbuf("t_negt", [128, 1], F32)
    t_poff = sbuf("t_poff", [128, 2], F32)
    t_cntb = sbuf("t_cntb", [128, 2], F32)
    t_kbias = sbuf("t_kbias", [128, 32], F32)
    t_dcol = sbuf("t_dcol", [128, 1], I32)
    t_dcol1 = sbuf("t_dcol1", [128, 1], I32)
    t_dcol2 = sbuf("t_dcol2", [128, 1], I32)
    t_tok64 = sbuf("t_tok64", [128, 64], F32)
    t_stok = sbuf("t_stok", [128, 32], I32)
    t_stokT = sbuf("t_stokT", [16, 128], F32)
    t_big32 = sbuf("t_big32", [32, 128], F32)
    t_gden = sbuf("t_gden", [128, 16], F32)
    t_srowD = sbuf("t_srowD", [1, 512], F32R)

    ps_a = es.enter_context(nc.psum_tensor("ps_a", [128, 2048], F32))
    ps_b = es.enter_context(nc.psum_tensor("ps_b", [128, 2048], F32))

    sems = {nm: es.enter_context(nc.semaphore(nm)) for nm in
            ["bar", "dms", "dmsc", "dmg", "tn", "ve", "ac", "gp"]}

    VEC = [None]

    def dma(eng, out, in_, waits=(), **kw):
        stream = {"sync": "dms", "scalar": "dmsc", "gpsimd": "dmg"}[eng]
        return P.emit(eng, lambda e: e.dma_start(out, in_, **kw), waits, inc=stream, k=16)

    def dmat(eng, out, in_, waits=()):
        stream = {"sync": "dms", "scalar": "dmsc"}[eng]
        return P.emit(eng, lambda e: e.dma_start(out, in_, transpose=True), waits, inc=stream, k=16)

    def mm(lfn, waits=()):
        return P.emit("tensor", lfn, waits, inc="tn")

    def act(lfn, waits=()):
        return P.emit("scalar", lfn, waits, inc="ac")

    def vec(lfn, waits=()):
        return P.emit("vector", lambda e: lfn(VEC[0]), waits, inc="ve")

    def gps(lfn, waits=()):
        return P.emit("gpsimd", lfn, waits, inc="gp")

    def tile_bt(i):
        return i // 8, (i % 8) * 512

    for o, i_ in [(t_tri[:], tri_in), (t_onescf[:], onescf_in), (t_onescr[:], onescr_in),
                  (t_onesrf[:], onesrf_in), (t_onesrr[:], onesrr_in),
                  (t_onesall[:], onesall_in), (t_ident[:], ident_in),
                  (t_slotid[:], slotid_in)]:
        dma("sync", o, i_)
    if SETUP_LIM >= 2:
        dma("sync", t_wqk[:], qkv_w[:, 0:2048].rearrange("(c p) o -> p c o", p=128))
        dma("sync", t_wv[:], qkv_w[:, 2048:3072].rearrange("(c p) o -> p c o", p=128))
    if SETUP_LIM < 3:
        _finish(nc, P, VEC, locals());  return _replay(nc, P, VEC, locals(), es, sems)
    dma("sync", t_smalls[:, 0:1], sm_in["c1"].rearrange("(o a) -> o a", o=1))
    dma("sync", t_smalls[:, 1:2], sm_in["c2"].rearrange("(o a) -> o a", o=1))
    dma("sync", t_smalls[:, 2:14], sm_in["conv1"].rearrange("a b c -> b (a c)"))
    dma("sync", t_smalls[:, 14:18], sm_in["gn_g"].rearrange("(o a) -> o a", o=1))
    dma("sync", t_smalls[:, 18:22], sm_in["gn_b"].rearrange("(o a) -> o a", o=1))
    dma("sync", t_smalls[:, 22:34], sm_in["conv2"].rearrange("a b c -> a (b c)"))
    dma("sync", t_smalls[:, 34:35], sm_in["conv2b"].rearrange("(o a) -> o a", o=1))
    if SETUP_LIM < 4:
        _finish(nc, P, VEC, locals());  return _replay(nc, P, VEC, locals(), es, sems)
    if PHASE_LIM > 5:
        dma("gpsimd", t_wgbf[:], wg_in.rearrange("(c p) o -> p c o", p=128),
            allow_slow_non_contiguous=True)
    P.barrier()

    vec(lambda v: v.memset(t_macc[:], 0.0))
    vec(lambda v: v.memset(t_negt[:], -1e9))
    P.barrier()
    if PHASE_LIM <= 0:
        _finish(nc, P, VEC, locals());  return _replay(nc, P, VEC, locals(), es, sems)

    e_tile_done = [()] * NT
    e_prev_stats = []
    for i in range(min(NT, A_TILES)):
        b, t0 = tile_bt(i)
        col = b * N + t0
        sc0 = b * 32 + (t0 // 128)
        bufw = list(e_tile_done[i - 2]) if i >= 2 else []
        e_xt = dma("sync", t_xt[i % 2][:], xT[:, col:col + 512].rearrange("(c p) t -> p c t", p=128), waits=bufw)
        e_xtm = dma("sync", t_xtm[i % 2][:], x_tm[col:col + 512, :].rearrange("(c p) d -> p c d", p=128), waits=bufw)
        xc = t_xt[i % 2]
        xtm = t_xtm[i % 2]
        e_sums = []
        for cl in range(4):
            sc = sc0 + cl
            act(lambda s, cl=cl, sc=sc, xtm=xtm: s.activation(
                t_qksc[:, 0:1024], xtm[:, cl, :], AF.Identity,
                accum_out=st["scr1"][:, sc:sc + 1]), waits=[e_xtm])
            e_sums.append(act(lambda s, cl=cl, sc=sc, xtm=xtm: s.activation(
                t_qksc[:, 0:1024], xtm[:, cl, :], AF.Square,
                accum_out=st["scr2"][:, sc:sc + 1]), waits=[e_xtm]))
        s4 = slice(sc0, sc0 + 4)
        ev = vec(lambda v, s4=s4: v.ts(st["mu1"][:, s4], st["scr1"][:, s4], 1.0 / D, None, OP.mult), waits=e_sums)
        vec(lambda v, s4=s4: v.ts(st["nmu1"][:, s4], st["mu1"][:, s4], -1.0, None, OP.mult))
        vec(lambda v, s4=s4: v.tt(st["scr1"][:, s4], st["mu1"][:, s4], st["mu1"][:, s4], OP.mult))
        vec(lambda v, s4=s4: v.ts(st["scr2"][:, s4], st["scr2"][:, s4], 1.0 / D, None, OP.mult))
        vec(lambda v, s4=s4: v.tt(st["scr2"][:, s4], st["scr2"][:, s4], st["scr1"][:, s4], OP.subtract))
        e_var = vec(lambda v, s4=s4: v.ts(st["scr2"][:, s4], st["scr2"][:, s4], EPS, None, OP.add))
        e_sq = act(lambda s, s4=s4: s.activation(st["scr1"][:, s4], st["scr2"][:, s4], AF.Sqrt), waits=[e_var])
        e_rs = vec(lambda v, s4=s4: v.recip(st["rstd1"][:, s4], st["scr1"][:, s4]), waits=[e_sq])
        if A_STAGE < 2:
            e_tile_done[i] = (e_rs, e_rs, e_sums[-1]);  e_prev_stats = [e_rs];  continue
        e_tp = None
        for r in range(4):
            e_tp = mm(lambda t, r=r, sc0=sc0: t.transpose(ps_a[0:1, 128 * r:128 * (r + 1)],
                                                          st["nmu1"][:, sc0 + r:sc0 + r + 1], t_ident[:]),
                      waits=([ev, e_rs] + e_prev_stats) if r == 0 else [])
        e_nr = vec(lambda v: v.copy(t_nmrow[:], ps_a[0:1, 0:512]), waits=[e_tp])
        e_bc = None
        for r in range(4):
            e_bc = mm(lambda t, r=r: t.matmul(ps_a[:, 512 + 128 * r:640 + 128 * r],
                                              t_onesrr[0:1, :], t_nmrow[0:1, 128 * r:128 * (r + 1)],
                                              start=True, stop=True), waits=[e_nr])
        e_xc = []
        for cc in range(8):
            e_xc.append(vec(lambda v, cc=cc, xc=xc: v.tt(xc[:, cc, :], xc[:, cc, :],
                                                         ps_a[:, 512:1024], OP.add),
                            waits=[e_bc, e_xt]))
        if A_STAGE < 3:
            e_tile_done[i] = (e_xc[-1], e_xc[-1], e_sums[-1]);  e_prev_stats = [e_xc[-1]];  continue
        if DUMP_XC:
            dma("gpsimd", x2_sp[col:col + 512, :].rearrange("(c p) d -> p c d", p=128),
                xc[:, 0:8, :], waits=e_xc)
        e_qkstat = []
        for cl in range(4):
            sc = sc0 + cl
            par = cl % 2
            pso = ps_a if par == 0 else ps_b
            q_ps = pso[:, 0:1024]
            k_ps = pso[:, 1024:2048]
            w_pre = list(e_xc)
            if cl >= 2:
                w_pre = [e_qkstat[cl - 2][-1], e_qkstat[cl - 2][-2]]
            e_mm = None
            for cc in range(8):
                lhs = (lambda xc=xc, cc=cc, cl=cl: xc[:, cc, 128 * cl:128 * (cl + 1)])()
                mm(lambda t, lhs=lhs, cc=cc, q_ps=q_ps: t.matmul(
                    q_ps[:, 0:512], lhs, t_wqk[:, cc, 0:512],
                    start=(cc == 0), stop=(cc == 7)), waits=w_pre if cc == 0 else [])
                mm(lambda t, lhs=lhs, cc=cc, q_ps=q_ps: t.matmul(
                    q_ps[:, 512:1024], lhs, t_wqk[:, cc, 512:1024],
                    start=(cc == 0), stop=(cc == 7)))
                mm(lambda t, lhs=lhs, cc=cc, k_ps=k_ps: t.matmul(
                    k_ps[:, 0:512], lhs, t_wqk[:, cc, 1024:1536],
                    start=(cc == 0), stop=(cc == 7)))
                e_mm = mm(lambda t, lhs=lhs, cc=cc, k_ps=k_ps: t.matmul(
                    k_ps[:, 512:1024], lhs, t_wqk[:, cc, 1536:2048],
                    start=(cc == 0), stop=(cc == 7)))
            if A_STAGE >= 31:
                e1 = act(lambda s, q_ps=q_ps, sc=sc: s.activation(
                    t_qksc[:, 0:1024], q_ps, AF.Identity, accum_out=st["sqm"][:, sc:sc + 1]), waits=[e_mm])
                if DUMP_XC and i == 0 and cl == 0:
                    e_dq = dma("gpsimd", ffn_sp[0:128, :], t_qksc[:, 0:1024], waits=[e1])
                    e_dw = dma("gpsimd", ffn_sp[128:256, :], t_wqk[:, 0, 0:1024])
                    e_dw2 = dma("gpsimd", ffn_sp[256:384, :], t_wqk[:, 7, 0:1024])
                    e1 = e_dq
                e2 = act(lambda s, k_ps=k_ps, sc=sc: s.activation(
                    t_qksc[:, 0:1024], k_ps, AF.Identity, accum_out=st["skm"][:, sc:sc + 1]))
            else:
                e1 = e2 = e_mm
            if A_STAGE >= 32:
                e3 = act(lambda s, q_ps=q_ps, sc=sc: s.activation(
                    t_qksc[:, 0:1024], q_ps, AF.Square, accum_out=st["sqq"][:, sc:sc + 1]), waits=[e_mm])
                e4 = act(lambda s, k_ps=k_ps, sc=sc: s.activation(
                    t_qksc[:, 0:1024], k_ps, AF.Square, accum_out=st["skk"][:, sc:sc + 1]))
            else:
                e3 = e4 = e_mm
            if A_STAGE >= 33:
                e5a = vec(lambda v, q_ps=q_ps: v.copy(t_qksc2[:, 0:1024], q_ps), waits=[e_mm, e4])
                e5 = vec(lambda v, k_ps=k_ps, sc=sc: v.stt(
                    t_qksc2[:, 0:1024], t_qksc2[:, 0:1024], 1.0, k_ps, OP.mult, OP.mult,
                    accum_out=st["sqk"][:, sc:sc + 1]), waits=[e5a])
            else:
                e5 = e_mm
            e_qkstat.append([e1, e2, e3, e4, e5])
        if A_STAGE < 4 or (30 <= A_STAGE < 40):
            e_tile_done[i] = (e_qkstat[3][4], e_qkstat[3][4], e_qkstat[3][3])
            e_prev_stats = e_qkstat[2] + e_qkstat[3];  continue
        e_vev = [None, None]
        e_vdma = []
        for vc in range(8):
            vps = ps_b[:, 512 * (vc % 2):512 * (vc % 2) + 512]
            if vc < 2:
                w_pre = e_qkstat[3]
            else:
                w_pre = [e_vev[vc % 2]]
            e_mm = None
            for cc in range(8):
                e_mm = mm(lambda t, vc=vc, cc=cc, vps=vps, xc=xc: t.matmul(
                    vps, t_wv[:, cc, 128 * vc:128 * (vc + 1)], xc[:, cc, :],
                    start=(cc == 0), stop=(cc == 7)), waits=w_pre if cc == 0 else [])
            e_ev = vec(lambda v, vc=vc, vps=vps: v.copy(t_vout[:, vc % 2, :], vps), waits=[e_mm])
            if DUMP_XC and i == 0 and vc == 0:
                e_ev = dma("gpsimd", ffn_sp[384:512, 0:512], t_vout[:, 0, :], waits=[e_ev])
            e_vev[vc % 2] = e_ev
            e_vdma.append(dma("scalar", vT_sp[128 * vc:128 * (vc + 1), col:col + 512],
                              t_vout[:, vc % 2, :], waits=[e_ev]))
        if A_STAGE < 5:
            e_tile_done[i] = (e_vev[0], e_vdma[-1], e_qkstat[3][3])
            e_prev_stats = e_qkstat[2] + e_qkstat[3] + [e_vev[0], e_vev[1]];  continue
        e_mps = None
        for cl in range(4):
            sc = sc0 + cl
            exm = vec(lambda v, cl=cl, sc=sc, xtm=xtm: v.ts(
                t_xm[:, 0:1024], xtm[:, cl, :], st["nmu1"][:, sc:sc + 1],
                st["rstd1"][:, sc:sc + 1], OP.add, OP.mult),
                waits=[e_vev[0], e_vev[1], e_rs] if cl == 0 else [e_mps])
            for half in range(2):
                e_mps = mm(lambda t, half=half, cl=cl: t.matmul(
                    ps_b[0:1, 1024 + 512 * half:1024 + 512 * (half + 1)], t_onescr[:],
                    t_xm[:, 512 * half:512 * (half + 1)],
                    start=(cl == 0), stop=(cl == 3)), waits=[exm])
        e_mac = vec(lambda v, b=b: v.tt(t_macc[0:1, 1024 * b:1024 * (b + 1)],
                                        t_macc[0:1, 1024 * b:1024 * (b + 1)],
                                        ps_b[0:1, 1024:2048], OP.add), waits=[e_mps])
        e_tile_done[i] = (e_mac, e_vdma[-1], e_qkstat[3][3])
        e_prev_stats = e_qkstat[2] + e_qkstat[3] + [e_mac]
        P.barrier()
    if PHASE_LIM <= 1:
        _finish(nc, P, VEC, locals());  return _replay(nc, P, VEC, locals(), es, sems)

    dma("sync", t_proj[:], proj_w.rearrange("(c p) o -> p c o", p=128))
    dma("sync", t_chw1[:], ch_w1.rearrange("(c p) o -> p c o", p=128))
    dma("sync", t_chw2[:], ch_w2.rearrange("(c p) o -> p c o", p=128))
    e_wgr = dma("sync", t_wgrow[:], wg_in.rearrange("a b -> b a"))
    e_pbr = dma("sync", t_bias_r[:], proj_b.rearrange("(o a) -> o a", o=1))
    e_swg = vec(lambda v: v.red(t_sc1[0:1, 0:1], t_wgrow[0:1, :], OP.add), waits=[e_wgr])
    e_swg2 = vec(lambda v: v.copy(t_smalls[0:1, 35:36], t_sc1[0:1, 0:1]), waits=[e_swg])
    e1 = mm(lambda t: t.matmul(ps_b[:, 0:64], t_onesrf[0:1, :], t_smalls[0:1, :], start=True, stop=True), waits=[e_swg2])
    vec(lambda v: v.copy(t_smbc[:], ps_b[:, 0:64]), waits=[e1])
    e1 = mm(lambda t: t.matmul(ps_b[:, 512:1024], t_onesrf[0:1, :], t_bias_r[0:1, 0:512], start=True, stop=True), waits=[e_pbr])
    e2 = mm(lambda t: t.matmul(ps_b[:, 1024:1536], t_onesrf[0:1, :], t_bias_r[0:1, 512:1024], start=True, stop=True))
    vec(lambda v: v.copy(t_biasbc[:], ps_b[:, 512:1536]), waits=[e1, e2])
    e1 = mm(lambda t: t.matmul(ps_b[:, 1536:2048], t_onesrf[0:1, :], t_wgrow[0:1, 0:512], start=True, stop=True))
    vec(lambda v: v.copy(t_wgbc[:, 0:512], ps_b[:, 1536:2048]), waits=[e1])
    e2 = mm(lambda t: t.matmul(ps_a[:, 0:512], t_onesrf[0:1, :], t_wgrow[0:1, 512:1024], start=True, stop=True))
    vec(lambda v: v.copy(t_wgbc[:, 512:1024], ps_a[:, 0:512]), waits=[e2])
    P.barrier()

    A = st
    sm = lambda k_: t_smbc[:, k_:k_ + 1]
    vec(lambda v: v.tt(A["scr1"][:], A["rstd1"][:], A["rstd1"][:], OP.mult))
    vec(lambda v: v.ts(A["sqm"][:], A["sqm"][:], 1.0 / D, None, OP.mult))
    vec(lambda v: v.ts(A["skm"][:], A["skm"][:], 1.0 / D, None, OP.mult))
    vec(lambda v: v.tt(A["sqm"][:], A["sqm"][:], A["rstd1"][:], OP.mult))
    vec(lambda v: v.tt(A["skm"][:], A["skm"][:], A["rstd1"][:], OP.mult))
    vec(lambda v: v.tt(A["sqq"][:], A["sqq"][:], A["scr1"][:], OP.mult))
    vec(lambda v: v.tt(A["skk"][:], A["skk"][:], A["scr1"][:], OP.mult))
    vec(lambda v: v.tt(A["sqk"][:], A["sqk"][:], A["scr1"][:], OP.mult))
    vec(lambda v: v.tt(A["scr1"][:], A["sqm"][:], A["sqm"][:], OP.mult))
    vec(lambda v: v.tt(A["scr2"][:], A["skm"][:], A["skm"][:], OP.mult))
    vec(lambda v: v.stt(A["sqq"][:], A["scr1"][:], -float(D), A["sqq"][:], OP.mult, OP.add))
    vec(lambda v: v.stt(A["skk"][:], A["scr2"][:], -float(D), A["skk"][:], OP.mult, OP.add))
    vec(lambda v: v.tt(t_sa[:], A["sqm"][:], A["skm"][:], OP.mult))
    vec(lambda v: v.stt(A["sqk"][:], t_sa[:], -float(D), A["sqk"][:], OP.mult, OP.add))
    vec(lambda v: v.ts(t_sb2[:], t_sa[:], 2.0, None, OP.mult))
    vec(lambda v: v.ts(t_sb2[:], t_sb2[:], sm(0), None, OP.add))
    vec(lambda v: v.ts(A["sqk"][:], A["sqk"][:], 2.0 * INVCM1, None, OP.mult))
    vec(lambda v: v.ts(A["sqk"][:], A["sqk"][:], sm(1), None, OP.add))
    vec(lambda v: v.tt(t_sb2[:], t_sb2[:], A["sqk"][:], OP.mult))
    vec(lambda v: v.tt(A["scr1"][:], A["scr1"][:], A["scr2"][:], OP.add))
    vec(lambda v: v.ts(A["scr1"][:], A["scr1"][:], sm(0), None, OP.add))
    vec(lambda v: v.tt(A["sqq"][:], A["sqq"][:], A["skk"][:], OP.add))
    vec(lambda v: v.ts(A["sqq"][:], A["sqq"][:], INVCM1, None, OP.mult))
    vec(lambda v: v.ts(A["sqq"][:], A["sqq"][:], sm(1), None, OP.add))
    vec(lambda v: v.tt(A["scr1"][:], A["scr1"][:], A["sqq"][:], OP.mult))
    vec(lambda v: v.ts(A["scr1"][:], A["scr1"][:], 1e-7, None, OP.add))
    e_r = vec(lambda v: v.recip(A["scr2"][:], A["scr1"][:]))
    vec(lambda v: v.tt(t_sb2[:], t_sb2[:], A["scr2"][:], OP.mult))
    e_ab = vec(lambda v: v.tt(A["ab"][:], t_sb2[:], t_sb2[:], OP.mult))

    e_wsmT = []
    for b in range(NB):
        bs = slice(32 * b, 32 * b + 32)
        e_t = mm(lambda t, bs=bs: t.transpose(ps_a[0:32, 0:128], A["ab"][:, bs], t_ident[:]),
                 waits=[e_ab] + (e_wsmT[-1:] if e_wsmT else []))
        e_c = vec(lambda v: v.copy(t_ode["a0"][:], ps_a[0:32, 0:128]), waits=[e_t])
        vec(lambda v: v.memset(t_ode["tmp"][:, 0:1], 0.0))
        e_z1 = vec(lambda v: v.copy(t_ode["tmp"][0:1, 1:2], t_ode["tmp"][0:1, 0:1]))
        e_sp = dma("scalar", ab_pad[b, 64:64 + N].rearrange("(c p) -> c p", p=128), t_ode["a0"][:], waits=[e_c])
        e_zl = dma("scalar", ab_pad[b, 63:64].rearrange("(o a) -> o a", o=1), t_ode["tmp"][0:1, 0:1], waits=[e_z1])
        e_zr = dma("scalar", ab_pad[b, 64 + N:65 + N].rearrange("(o a) -> o a", o=1), t_ode["tmp"][0:1, 0:1], waits=[e_z1])
        e_m1 = dma("sync", t_ode["am1"][:], ab_pad[b, 63:63 + N].rearrange("(c p) -> c p", p=128), waits=[e_sp, e_zl, e_zr])
        e_p1 = dma("sync", t_ode["ap1"][:], ab_pad[b, 65:65 + N].rearrange("(c p) -> c p", p=128), waits=[e_sp, e_zl, e_zr])
        e_h = []
        for o in range(4):
            hh = t_ode[f"hh{o}"]
            vec(lambda v, hh=hh, o=o: v.ts(hh[:], t_ode["am1"][:], t_smbc[0:32, 2 + 3 * o:3 + 3 * o], None, OP.mult), waits=[e_m1, e_p1])
            vec(lambda v, hh=hh, o=o: v.stt(hh[:], t_ode["a0"][:], t_smbc[0:32, 3 + 3 * o:4 + 3 * o], hh[:], OP.mult, OP.add))
            e_h.append(vec(lambda v, hh=hh, o=o: v.stt(hh[:], t_ode["ap1"][:], t_smbc[0:32, 4 + 3 * o:5 + 3 * o], hh[:], OP.mult, OP.add)))
        e_gs = []
        for o in range(4):
            e1_ = vec(lambda v, o=o: v.red(t_vred[0:32, 18 + o:19 + o], t_ode[f"hh{o}"][:], OP.add), waits=[e_h[o]])
            e2_ = act(lambda s, o=o: s.activation(t_ode["tmp"][:], t_ode[f"hh{o}"][:], AF.Square,
                                                  accum_out=t_vred[0:32, 22 + o:23 + o]), waits=[e_h[o]])
            e_gs += [e1_, e2_]
        e_gm = mm(lambda t: t.matmul(ps_a[0:1, 0:8], t_onescf[0:32, :], t_vred[0:32, 18:26], start=True, stop=True), waits=e_gs)
        e_cp = vec(lambda v: v.copy(t_sc1[0:1, 8:16], ps_a[0:1, 0:8]), waits=[e_gm])
        vec(lambda v: v.tt(t_sc1[0:1, 16:17], t_sc1[0:1, 8:9], t_sc1[0:1, 9:10], OP.add))
        vec(lambda v: v.tt(t_sc1[0:1, 17:18], t_sc1[0:1, 10:11], t_sc1[0:1, 11:12], OP.add))
        vec(lambda v: v.tt(t_sc1[0:1, 18:19], t_sc1[0:1, 12:13], t_sc1[0:1, 13:14], OP.add))
        vec(lambda v: v.tt(t_sc1[0:1, 19:20], t_sc1[0:1, 14:15], t_sc1[0:1, 15:16], OP.add))
        vec(lambda v: v.ts(t_sc1[0:1, 16:20], t_sc1[0:1, 16:20], 1.0 / (2 * N), None, OP.mult))
        vec(lambda v: v.tt(t_sc2[0:1, 0:2], t_sc1[0:1, 16:18], t_sc1[0:1, 16:18], OP.mult))
        vec(lambda v: v.tt(t_sc2[0:1, 0:2], t_sc1[0:1, 18:20], t_sc2[0:1, 0:2], OP.subtract))
        e_v2 = vec(lambda v: v.ts(t_sc2[0:1, 0:2], t_sc2[0:1, 0:2], EPS, None, OP.add))
        e_sqg = act(lambda s: s.activation(t_sc2[0:1, 2:4], t_sc2[0:1, 0:2], AF.Sqrt), waits=[e_v2])
        e_rg = vec(lambda v: v.recip(t_sc2[0:1, 4:6], t_sc2[0:1, 2:4]), waits=[e_sqg])
        for ch_ in range(4):
            g_ = ch_ // 2
            vec(lambda v, ch_=ch_, g_=g_: v.tt(t_sc2[0:1, 8 + ch_:9 + ch_], t_sc2[0:1, 4 + g_:5 + g_], t_smalls[0:1, 14 + ch_:15 + ch_], OP.mult))
            vec(lambda v, ch_=ch_, g_=g_: v.tt(t_sc2[0:1, 12 + ch_:13 + ch_], t_sc2[0:1, 8 + ch_:9 + ch_], t_sc1[0:1, 16 + g_:17 + g_], OP.mult))
            vec(lambda v, ch_=ch_: v.tt(t_sc2[0:1, 12 + ch_:13 + ch_], t_smalls[0:1, 18 + ch_:19 + ch_], t_sc2[0:1, 12 + ch_:13 + ch_], OP.subtract))
        e_ab8 = vec(lambda v: v.copy(t_sc2[0:1, 16:24], t_sc2[0:1, 8:16]))
        e_bc8 = mm(lambda t: t.matmul(ps_a[0:32, 16:24], t_onesrf[0:1, 0:32], t_sc2[0:1, 16:24], start=True, stop=True), waits=[e_ab8])
        e_od = vec(lambda v: v.copy(t_odes[:, 0:8], ps_a[0:32, 16:24]), waits=[e_bc8])
        e_hr = []
        for o in range(4):
            e_hr.append(act(lambda s, o=o: s.activation(
                t_ode[f"hh{o}"][:], t_ode[f"hh{o}"][:], AF.Relu,
                bias=t_odes[:, 4 + o:5 + o], scale=t_odes[:, o:o + 1]), waits=[e_od]))
        e_yy = vec(lambda v: v.memset(t_ode["yy"][:], 0.0))
        for o in range(4):
            e_sp2 = dma("scalar", hh_pad[b, o, 64:64 + N].rearrange("(c p) -> c p", p=128), t_ode[f"hh{o}"][:], waits=[e_hr[o]])
            dma("scalar", hh_pad[b, o, 63:64].rearrange("(o a) -> o a", o=1), t_ode["tmp"][0:1, 0:1])
            e_zz2 = dma("scalar", hh_pad[b, o, 64 + N:65 + N].rearrange("(o a) -> o a", o=1), t_ode["tmp"][0:1, 0:1])
            for tap, off in [(0, 63), (1, 64), (2, 65)]:
                e_ld = dma("sync", t_ode["am1"][:] if tap == 0 else (t_ode["a0"][:] if tap == 1 else t_ode["ap1"][:]),
                           hh_pad[b, o, off:off + N].rearrange("(c p) -> c p", p=128), waits=[e_sp2, e_zz2, e_yy])
                src = {0: "am1", 1: "a0", 2: "ap1"}[tap]
                e_yy = vec(lambda v, src=src, o=o, tap=tap: v.stt(
                    t_ode["yy"][:], t_ode[src][:], t_smbc[0:32, 22 + 3 * o + tap:23 + 3 * o + tap],
                    t_ode["yy"][:], OP.mult, OP.add), waits=[e_ld, e_yy])
        vec(lambda v: v.ts(t_ode["yy"][:], t_ode["yy"][:], t_smbc[0:32, 34:35], None, OP.add), waits=[e_yy])
        e_t2 = mm(lambda t, bs=bs: t.transpose(ps_a[0:32, 32:160], A["ab"][:, bs], t_ident[:]), waits=[e_od])
        e_c2 = vec(lambda v: v.tt(t_ode["yy"][:], t_ode["yy"][:], ps_a[0:32, 32:160], OP.add), waits=[e_t2])
        if DUMP_XC and b == 0:
            e_c2 = dma("gpsimd", ffn_sp[512:544, 0:128], t_ode["yy"][:], waits=[e_c2])
            e_hh = dma("gpsimd", ffn_sp[544:576, 0:128], t_ode["hh0"][:], waits=[e_c2])
        e_fa = act(lambda s: s.activation(t_ode["fa"][:], t_ode["yy"][:], AF.Sigmoid), waits=[e_c2])
        e_ex = act(lambda s: s.activation(t_ode["tmp"][:], t_ode["fa"][:], AF.Exp), waits=[e_fa])
        e_rd = vec(lambda v: v.red(t_vred[0:32, 26:27], t_ode["tmp"][:], OP.add), waits=[e_ex])
        e_sm = mm(lambda t: t.matmul(ps_a[0:1, 24:25], t_onescf[0:32, :], t_vred[0:32, 26:27], start=True, stop=True), waits=[e_rd])
        e_rc = vec(lambda v: v.copy(t_sc1[0:1, 20:21], ps_a[0:1, 24:25]), waits=[e_sm])
        e_rc2 = vec(lambda v: v.recip(t_sc1[0:1, 21:22], t_sc1[0:1, 20:21]), waits=[e_rc])
        e_bc2 = mm(lambda t: t.matmul(ps_a[0:32, 25:26], t_onesrf[0:1, 0:32], t_sc1[0:1, 21:22], start=True, stop=True), waits=[e_rc2])
        e_cb2 = vec(lambda v: v.copy(t_vred[0:32, 27:28], ps_a[0:32, 25:26]), waits=[e_bc2])
        e_w = vec(lambda v: v.ts(t_ode["fa"][:], t_ode["tmp"][:], t_vred[0:32, 27:28], None, OP.mult), waits=[e_cb2])
        e_t3 = mm(lambda t: t.transpose(ps_b[0:128, 0:32], t_ode["fa"][:], t_ident[0:32, 0:32]), waits=[e_w])
        e_wT = vec(lambda v, bs=bs: v.copy(st["ssc"][:, bs], ps_b[0:128, 0:32]), waits=[e_t3])
        e_wsmT.append(e_wT)
    e_ssc = vec(lambda v: v.tt(st["ssc"][:], st["ssc"][:], st["rstd1"][:], OP.mult), waits=e_wsmT)
    P.barrier()
    if PHASE_LIM <= 2:
        _finish(nc, P, VEC, locals());  return _replay(nc, P, VEC, locals(), es, sems)

    vec(lambda v: v.ts(t_macc[:], t_macc[:], 1.0 / N, None, OP.mult))
    e_mT = []
    for b in range(NB):
        for cc in range(8):
            e_t = mm(lambda t, b=b, cc=cc: t.transpose(
                ps_a[0:128, 0:1], t_macc[0:1, 1024 * b + 128 * cc:1024 * b + 128 * (cc + 1)], t_ident[0:1, 0:1]),
                waits=e_mT[-1:])
            e_mT.append(vec(lambda v, b=b, cc=cc: v.copy(t_mrow[:, cc, b:b + 1], ps_a[0:128, 0:1]), waits=[e_t]))
    e_b1 = dma("sync", t_vred[:, 18:20], ch_b1.rearrange("(c p) -> p c", p=128), allow_slow_non_contiguous=True)
    e_b2 = dma("sync", t_vred[:, 20:28], ch_b2.rearrange("(c p) -> p c", p=128), allow_slow_non_contiguous=True)
    for hc in range(2):
        e_mm2 = None
        for cc in range(8):
            e_mm2 = mm(lambda t, hc=hc, cc=cc: t.matmul(
                ps_a[0:128, 4 + 2 * hc:6 + 2 * hc], t_chw1[:, cc, 128 * hc:128 * (hc + 1)],
                t_mrow[:, cc, :], start=(cc == 0), stop=(cc == 7)), waits=e_mT + [e_b1])
        act(lambda s, hc=hc: s.activation(t_hrel[:, hc, :], ps_a[0:128, 4 + 2 * hc:6 + 2 * hc],
                                          AF.Relu, bias=t_vred[:, 18 + hc:19 + hc]), waits=[e_mm2])
    e_hr2 = P.emit("scalar", lambda s: s.nop(), inc="ac")
    for oc in range(8):
        e_mm3 = None
        for hc in range(2):
            e_mm3 = mm(lambda t, oc=oc, hc=hc: t.matmul(
                ps_a[0:128, 8 + 2 * oc:10 + 2 * oc], t_chw2[:, hc, 128 * oc:128 * (oc + 1)],
                t_hrel[:, hc, :], start=(hc == 0), stop=(hc == 1)), waits=[e_hr2, e_b2])
        act(lambda s, oc=oc: s.activation(t_chT[:, oc, :], ps_a[0:128, 8 + 2 * oc:10 + 2 * oc],
                                          AF.Sigmoid, bias=t_vred[:, 20 + oc:21 + oc]), waits=[e_mm3])
    P.barrier()
    if PHASE_LIM <= 3:
        _finish(nc, P, VEC, locals());  return _replay(nc, P, VEC, locals(), es, sems)

    e_tile_done2 = [()] * NT
    prev_attn_rd = [[None] * 4, [None] * 4]
    prev_x2_rd = [None] * 4
    prev_ln2_dma = [None]
    prev_x2t = [None]
    dstate = [None] * NT
    x2_last = [None]
    rs2_last = [None]

    def d_stage0(i):
        b, t0 = tile_bt(i)
        col = b * N + t0
        sc0 = b * 32 + (t0 // 128)
        tp = i % 2
        wv = [dstate[i - 2]["e_mmlast"]] if i >= 2 else []
        wx = list(dstate[i - 2]["e_x2ev"]) if i >= 2 else []
        e_vt = dma("sync", t_xt[i % 2][:], vT_sp[:, col:col + 512].rearrange("(c p) t -> p c t", p=128), waits=wv)
        e_xtm = dma("sync", t_xtm[i % 2][:], x_tm[col:col + 512, :].rearrange("(c p) d -> p c d", p=128), waits=wx)
        vt = t_xt[i % 2]
        xtm = t_xtm[i % 2]
        e_ol = []
        for cc in range(8):
            e_ol.append(vec(lambda v, cc=cc, vt=vt, b=b: v.ts(
                vt[:, cc, :], vt[:, cc, :], t_chT[:, cc, b:b + 1], None, OP.mult),
                waits=[e_vt]))
        dstate[i] = dict(b=b, col=col, sc0=sc0, s4=slice(sc0, sc0 + 4), tp=tp,
                         ablk=(t_attnblk if tp == 0 else t_attnblkB),
                         vt=vt, xtm=xtm, e_xtm=e_xtm, e_ol=e_ol, e_x2ev=[])

    def d_stage1(i):
        ds = dstate[i]
        sc0, tp, ablk, vt = ds["sc0"], ds["tp"], ds["ablk"], ds["vt"]
        e_ol = ds["e_ol"]
        e_pr = []
        e_ev4 = []
        prev_ev4 = dstate[i - 1]["e_ev4"] if i >= 1 else []
        for cl in range(4):
            par = cl % 2
            pps = ps_a[:, 1024:2048] if par == 0 else ps_b[:, 0:1024]
            w_pre = list(e_ol) + prev_ev4
            if cl >= 2:
                w_pre += [e_pr[cl - 2]]
            e_mm4 = None
            for cc in range(8):
                for ot in range(2):
                    e_mm4 = mm(lambda t, cc=cc, cl=cl, ot=ot, pps=pps, vt=vt: t.matmul(
                        pps[:, 512 * ot:512 * (ot + 1)], vt[:, cc, 128 * cl:128 * (cl + 1)],
                        t_proj[:, cc, 512 * ot:512 * (ot + 1)],
                        start=(cc == 0), stop=(cc == 7)), waits=w_pre if (cc == 0 and ot == 0) else [])
            sc = sc0 + cl
            e_ev = vec(lambda v, cl=cl, sc=sc, pps=pps, ablk=ablk: v.stt(
                ablk[:, cl, :], pps, st["ssc"][:, sc:sc + 1], t_biasbc[:],
                OP.mult, OP.add, accum_out=st["scr1"][:, sc:sc + 1]),
                waits=[e_mm4, prev_attn_rd[tp][cl]])
            e_ev4.append(e_ev)
            e_pr.append(e_ev)
            eb = act(lambda s, cl=cl, sc=sc, ablk=ablk: s.activation(
                t_qksc[:, 0:1024], ablk[:, cl, :], AF.Square,
                accum_out=st["scr2"][:, sc:sc + 1]), waits=[e_ev])
            e_pr[-1] = eb
        ds["e_pr"] = e_pr
        ds["e_ev4"] = e_ev4[2:]
        ds["e_mmlast"] = e_mm4

    def d_stage2(i):
        ds = dstate[i]
        col, sc0, s4, tp, ablk = ds["col"], ds["sc0"], ds["s4"], ds["tp"], ds["ablk"]
        xtm, e_xtm, e_pr = ds["xtm"], ds["e_xtm"], ds["e_pr"]
        ev = vec(lambda v, s4=s4: v.ts(st["scr1"][:, s4], st["scr1"][:, s4], 1.0 / D, None, OP.mult), waits=e_pr)
        vec(lambda v, s4=s4: v.ts(st["scr2"][:, s4], st["scr2"][:, s4], 1.0 / D, None, OP.mult))
        vec(lambda v, s4=s4: v.tt(t_sa[:, s4], st["scr1"][:, s4], st["scr1"][:, s4], OP.mult))
        vec(lambda v, s4=s4: v.tt(st["scr2"][:, s4], st["scr2"][:, s4], t_sa[:, s4], OP.subtract))
        e_var = vec(lambda v, s4=s4: v.ts(st["scr2"][:, s4], st["scr2"][:, s4], EPS, None, OP.add))
        e_sq = act(lambda s, s4=s4: s.activation(t_sb2[:, s4], st["scr2"][:, s4], AF.Sqrt), waits=[e_var])
        e_rs = vec(lambda v, s4=s4: v.recip(t_sb2[:, s4], t_sb2[:, s4]), waits=[e_sq])
        e_nm = vec(lambda v, s4=s4: v.ts(st["scr1"][:, s4], st["scr1"][:, s4], -1.0, None, OP.mult), waits=[e_rs])
        e_nb2 = vec(lambda v, s4=s4: v.tt(st["scr1"][:, s4], st["scr1"][:, s4], t_sb2[:, s4], OP.mult))
        e_ch_done = []
        for cl in range(4):
            sc = sc0 + cl
            e_l = act(lambda s, cl=cl, sc=sc, ablk=ablk: s.activation(
                t_x2t[:, 0:1024], ablk[:, cl, :], AF.Identity,
                bias=st["scr1"][:, sc:sc + 1], scale=t_sb2[:, sc:sc + 1]),
                waits=[e_nb2, prev_x2t[0]])
            prev_attn_rd[tp][cl] = e_l
            e_x2 = vec(lambda v, cl=cl, sc=sc, xtm=xtm: v.stt(
                t_x2blk[:, cl, :], t_x2t[:, 0:1024], 1.0, xtm[:, cl, :],
                OP.mult, OP.add, accum_out=st["sqm"][:, sc:sc + 1]),
                waits=[e_xtm, prev_x2_rd[cl], e_l])
            ds["e_x2ev"].append(e_x2)
            prev_x2t[0] = e_x2
            e_st = dma("scalar", out_t[col + 128 * cl:col + 128 * (cl + 1), :], t_x2blk[:, cl, :], waits=[e_x2])
            x2_last[0] = e_st
            if DBG:
                dma("scalar", x2_sp[col + 128 * cl:col + 128 * (cl + 1), :], t_x2blk[:, cl, :], waits=[e_x2])
            eb = act(lambda s, cl=cl, sc=sc: s.activation(
                t_qksc[:, 0:1024], t_x2blk[:, cl, :], AF.Square,
                accum_out=st["skm"][:, sc:sc + 1]), waits=[e_x2])
            e_ch_done.append((e_st, eb))
        e_prs = [tup[1] for tup in e_ch_done]
        ev = vec(lambda v, s4=s4: v.ts(st["sqm"][:, s4], st["sqm"][:, s4], 1.0 / D, None, OP.mult), waits=e_prs)
        vec(lambda v, s4=s4: v.ts(st["nmu2"][:, s4], st["sqm"][:, s4], -1.0, None, OP.mult))
        vec(lambda v, s4=s4: v.tt(t_sa[:, s4], st["sqm"][:, s4], st["sqm"][:, s4], OP.mult))
        vec(lambda v, s4=s4: v.ts(st["skm"][:, s4], st["skm"][:, s4], 1.0 / D, None, OP.mult))
        vec(lambda v, s4=s4: v.tt(st["skm"][:, s4], st["skm"][:, s4], t_sa[:, s4], OP.subtract))
        e_var2 = vec(lambda v, s4=s4: v.ts(st["skm"][:, s4], st["skm"][:, s4], EPS, None, OP.add))
        e_sq2 = act(lambda s, s4=s4: s.activation(st["rstd2"][:, s4], st["skm"][:, s4], AF.Sqrt), waits=[e_var2])
        e_rs2 = vec(lambda v, s4=s4: v.recip(st["rstd2"][:, s4], st["rstd2"][:, s4]), waits=[e_sq2])
        rs2_last[0] = e_rs2
        e_zi = []
        for cl in range(4):
            sc = sc0 + cl
            e_l2 = vec(lambda v, cl=cl, sc=sc: v.ts(
                t_ln2t[:, 0:1024], t_x2blk[:, cl, :], st["nmu2"][:, sc:sc + 1],
                st["rstd2"][:, sc:sc + 1], OP.add, OP.mult), waits=[e_rs2, prev_ln2_dma[0]])
            e_sl = dma("scalar", ln2_sp[col + 128 * cl:col + 128 * (cl + 1), :], t_ln2t[:, 0:1024], waits=[e_l2])
            prev_ln2_dma[0] = e_sl
            e_zr = vec(lambda v, cl=cl, sc=sc: v.stt(
                t_zscr[:, 0:1024], t_x2blk[:, cl, :], 1.0, t_wgbc[:], OP.mult, OP.mult,
                accum_out=st["zz"][:, sc:sc + 1]))
            prev_x2_rd[cl] = e_zr
            e_zi.append((e_sl, e_zr))
        e_tile_done2[i] = (e_zi[3][0], e_zi[3][1], e_ch_done[3][0])

    for i in range(NT):
        d_stage0(i)
        if i >= 1:
            d_stage2(i - 1)
        d_stage1(i)
    d_stage2(NT - 1)
    e_rs2 = rs2_last[0]
    e_x2_last = x2_last[0]
    e_z1 = vec(lambda v: v.ts(t_sa[:], st["nmu2"][:], t_smbc[:, 35:36], None, OP.mult))
    e_z2 = vec(lambda v: v.tt(st["zz"][:], st["zz"][:], t_sa[:], OP.add), waits=[e_z1])
    e_z3 = vec(lambda v: v.tt(st["zz"][:], st["zz"][:], st["rstd2"][:], OP.mult), waits=[e_rs2])
    P.barrier()
    if PHASE_LIM <= 4:
        _finish(nc, P, VEC, locals());  return _replay(nc, P, VEC, locals(), es, sems)

    vec(lambda v: v.memset(t_lo[:], -64.0))
    vec(lambda v: v.memset(t_hi[:], 64.0))
    vec(lambda v: v.memset(t_tm[:], 0.0))
    vec(lambda v: v.memset(t_tbc[:], 0.0))
    e_big = vec(lambda v: v.memset(t_bigt[:], BIGF))
    e_zz = vec(lambda v: v.memset(t_zeroE[:], 0.0))
    for r in range(ROUNDS):
        e_pc = None
        for b in range(NB):
            bs = slice(32 * b, 32 * b + 32)
            vec(lambda v, b=b, bs=bs: v.ts(t_mask[:, bs], st["zz"][:, bs], t_tbc[:, b:b + 1], None, OP.is_gt))
            e_pc = vec(lambda v, b=b, bs=bs: v.red(t_pcnt[:, b:b + 1], t_mask[:, bs], OP.add))
        e_cm = mm(lambda t: t.matmul(ps_a[0:1, 0:2], t_onescf[:], t_pcnt[:], start=True, stop=True), waits=[e_pc])
        e_cc = vec(lambda v: v.copy(t_cn[:], ps_a[0:1, 0:2]), waits=[e_cm])
        vec(lambda v: v.ts(t_pr2[:], t_cn[:], float(C), None, OP.is_ge))
        vec(lambda v: v.cpred(t_lo[:], t_pr2[:], t_tm[:]))
        vec(lambda v: v.ts(t_pr2[:], t_cn[:], float(C), None, OP.is_lt))
        vec(lambda v: v.cpred(t_hi[:], t_pr2[:], t_tm[:]))
        vec(lambda v: v.tt(t_tm[:], t_lo[:], t_hi[:], OP.add))
        e_t = vec(lambda v: v.ts(t_tm[:], t_tm[:], 0.5, None, OP.mult))
        e_bc = mm(lambda t: t.matmul(ps_a[0:128, 4:6], t_onesrf[0:1, :], t_tm[0:1, :], start=True, stop=True), waits=[e_t])
        vec(lambda v: v.copy(t_tbc[:], ps_a[0:128, 4:6]), waits=[e_bc])
    e_lo = vec(lambda v: v.copy(t_tm[:], t_lo[:]))
    e_bc = mm(lambda t: t.matmul(ps_a[0:128, 4:6], t_onesrf[0:1, :], t_tm[0:1, :], start=True, stop=True), waits=[e_lo])
    vec(lambda v: v.copy(t_tbc[:], ps_a[0:128, 4:6]), waits=[e_bc])
    e_pc = None
    for b in range(NB):
        bs = slice(32 * b, 32 * b + 32)
        vec(lambda v, b=b, bs=bs: v.ts(t_mask[:, bs], st["zz"][:, bs], t_tbc[:, b:b + 1], None, OP.is_gt))
        e_pc = vec(lambda v, b=b, bs=bs: v.red(t_pcnt[:, b:b + 1], t_mask[:, bs], OP.add))
    e_off = mm(lambda t: t.matmul(ps_a[0:128, 8:10], t_tri[:], t_pcnt[:], start=True, stop=True), waits=[e_pc])
    e_tot = mm(lambda t: t.matmul(ps_a[0:1, 0:2], t_onescf[:], t_pcnt[:], start=True, stop=True))
    vec(lambda v: v.copy(t_poff[:], ps_a[0:128, 8:10]), waits=[e_off])
    e_cc = vec(lambda v: v.copy(t_cn[:], ps_a[0:1, 0:2]), waits=[e_tot])
    e_cmin = vec(lambda v: v.ts(t_cn[:], t_cn[:], float(C), None, OP.min), waits=[e_cc])
    e_cb = mm(lambda t: t.matmul(ps_a[0:128, 12:14], t_onesrf[0:1, :], t_cn[0:1, :], start=True, stop=True), waits=[e_cmin])
    e_cbv = vec(lambda v: v.copy(t_cntb[:], ps_a[0:128, 12:14]), waits=[e_cb])
    vec(lambda v: v.copy(t_sa[:], t_mask[:]))
    cur, nxt = t_sa, t_sb2
    for sh in [1, 2, 4, 8, 16]:
        for b in range(NB):
            o = 32 * b
            vec(lambda v, cur=cur, nxt=nxt, o=o, sh=sh: v.copy(nxt[:, o:o + sh], cur[:, o:o + sh]))
            vec(lambda v, cur=cur, nxt=nxt, o=o, sh=sh: v.tt(nxt[:, o + sh:o + 32], cur[:, o + sh:o + 32], cur[:, o:o + 32 - sh], OP.add))
        cur, nxt = nxt, cur
    vec(lambda v, cur=cur: v.tt(cur[:], cur[:], t_mask[:], OP.subtract))
    for b in range(NB):
        bs = slice(32 * b, 32 * b + 32)
        vec(lambda v, cur=cur, b=b, bs=bs: v.ts(st["dest"][:, bs], cur[:, bs], t_poff[:, b:b + 1], None, OP.add))
        if b == 1:
            vec(lambda v, bs=bs: v.ts(st["dest"][:, bs], st["dest"][:, bs], float(C), None, OP.add))
    vec(lambda v: v.ts(t_umask[:], t_mask[:], 0.5, None, OP.is_lt))
    vec(lambda v: v.cpred(st["dest"][:], t_umask[:], t_bigt[:]), waits=[e_big])
    for b in range(NB):
        bs = slice(32 * b, 32 * b + 32)
        vec(lambda v, b=b, bs=bs: v.ts(t_umask[:, bs], st["dest"][:, bs], float((b + 1) * C), None, OP.is_ge))
        vec(lambda v, bs=bs: v.cpred(st["dest"][:, bs], t_umask[:, bs], t_bigt[:, bs]))
    e_di = vec(lambda v: v.copy(t_desti[:], st["dest"][:]))
    e_tk = None
    for k4 in range(4):
        e_tk = vec(lambda v, k4=k4: v.ts(t_tok64[:, 16 * k4:16 * (k4 + 1)], t_slotid[:],
                                         2048.0 * k4, None, OP.add))
    e_b32 = vec(lambda v: v.memset(t_big32[:], BIGF))
    e_ti = [dma("gpsimd", tokid_sp.rearrange("(c p) o -> c (p o)", p=128), t_big32[:], waits=[e_b32])]
    e_kb = []
    for b in range(NB):
        for kc in range(16):
            cix = 16 * b + kc
            vec(lambda v, cix=cix, kc=kc, b=b: v.ts(t_umask[:, cix:cix + 1], t_slotid[:, kc:kc + 1], t_cntb[:, b:b + 1], None, OP.is_ge), waits=[e_cbv])
            vec(lambda v, cix=cix: v.memset(t_kbias[:, cix:cix + 1], 0.0))
            e_kb.append(vec(lambda v, cix=cix: v.cpred(t_kbias[:, cix:cix + 1], t_umask[:, cix:cix + 1], t_negt[:])))
    P.barrier()

    for j in range(32):
        dma("gpsimd", comp[128 * j:128 * (j + 1), :], t_zeroE[:])
    e_breg = [None]
    e_breg2 = [None]

    def setup_breg(g):
        r = g.alloc_register()
        g.reg_mov(r, NB * C - 1)
        e_breg[0] = r
        r2 = g.alloc_register()
        g.reg_mov(r2, NB * N - 1)
        e_breg2[0] = r2
        return None
    P.emit("gpsimd", lambda g: setup_breg(g))
    e_sc_chain = [None, None]
    rows_b = [t_rows, t_rows1]
    dcol_b = [t_dcol, t_dcol1]
    for j in range(64):
        b, cix = j // 32, j % 32
        col = b * N + 128 * cix
        scc = b * 32 + cix
        par = j % 2
        rows, dcol = rows_b[par], dcol_b[par]
        e_r = dma("gpsimd", rows[:], ln2_sp[col:col + 128, :], waits=[e_sc_chain[par]])
        e_dc = gps(lambda g, scc=scc, dcol=dcol: g.tensor_copy(dcol[:], t_desti[:, scc:scc + 1]))
        e_sc_chain[par] = P.emit("gpsimd", lambda g, rows=rows, dcol=dcol: g.indirect_dma_start(
            comp, IndirectOffsetOnAxis(dcol[:], 0), rows[:], None,
            bounds_check=e_breg[0], oob_is_err=False), waits=[e_r, e_dc], inc="dmg", k=16)
        if j == 31:
            e_sc_b0 = list(e_sc_chain)
    e_sc_prev = e_sc_chain
    e_sc_bybatch = [e_sc_b0, e_sc_chain]
    P.barrier()
    if PHASE_LIM <= 5:
        _finish(nc, P, VEC, locals());  return _replay(nc, P, VEC, locals(), es, sems)

    if not SKIP_EF:
        _phase_E_F(nc, P, VEC, locals())
    _finish(nc, P, VEC, locals())
    return _replay(nc, P, VEC, locals(), es, sems)


def _finish(nc, P, VEC, env):
    if DBG:
        dma, st, dbg_st, t_macc, t_gden = env["dma"], env["st"], env["dbg_st"], env["t_macc"], env["t_gden"]
        for di, nm in enumerate(["mu1", "nmu1", "rstd1", "sqm", "skm", "sqq", "skk",
                                 "sqk", "ab", "ssc", "nmu2", "rstd2", "zz", "dest"]):
            dma("scalar", dbg_st[di], st[nm][:])
        dma("scalar", dbg_st[14].rearrange("p c -> (p c)")[0:2048].rearrange("(o a) -> o a", o=1), t_macc[0:1, :])
        dma("scalar", dbg_st[15][:, 0:16], t_gden[:])
    P.emit("sync", lambda s: s.nop(), waits=[("dmsc", P.counts.get("dmsc", 0)),
                                             ("dmg", P.counts.get("dmg", 0)),
                                             ("dms", P.counts.get("dms", 0))])
    P.barrier()


def _replay(nc, P, VEC, env, es, sems):
    P.finalize()
    with nc.Block() as block:
        def section(eng_name):
            def run(eng):
                if eng_name == "vector":
                    VEC[0] = Vec(eng)
                for fn, waits, inc, k, v in P.ops[eng_name]:
                    if fn == "BARRIER":
                        eng.sem_inc(sems["bar"], 1)
                        eng.wait_ge(sems["bar"], inc)
                        if eng_name == "vector" and VEC[0] is not None:
                            VEC[0].dirty.clear()
                        continue
                    for (s, vv) in waits:
                        eng.wait_ge(sems[s], vv)
                    ins = fn(eng)
                    if inc is not None and ins is not None:
                        ins.then_inc(sems[inc], k)
                    elif inc is not None and ins is None:
                        eng.sem_inc(sems[inc], k)
            return run

        block.sync(section("sync"))
        block.scalar(section("scalar"))
        block.vector(section("vector"))
        block.tensor(section("tensor"))
        block.gpsimd(section("gpsimd"))

    es.close()
    return nc


def _phase_E_F(nc, P, VEC, env):
    g = env
    sems = g["sems"]
    t_l2sT, t_wffn, t_vffn, t_aoT, t_expS = g["t_l2sT"], g["t_wffn"], g["t_vffn"], g["t_aoT"], g["t_expS"]
    t_vffn4 = g["t_vffn4"]
    t_kh, t_qh, t_wo2, t_orow = g["t_kh"], g["t_qh"], g["t_wo2"], g["t_orow"]
    t_rows, t_facc, t_fx2, t_drow = g["t_rows"], g["t_facc"], g["t_fx2"], g["t_drow"]
    t_kbias, t_gden, t_gsel, t_wgbf = g["t_kbias"], g["t_gden"], g["t_gsel"], g["t_wgbf"]
    t_onesall, t_ident, t_desti = g["t_onesall"], g["t_ident"], g["t_desti"]
    t_qksc, t_dcol = g["t_qksc"], g["t_dcol"]
    st = g["st"]
    ps_a, ps_b = g["ps_a"], g["ps_b"]
    comp, qT_sp, kT_sp, ffn_sp, x2_sp, out_t = g["comp"], g["qT_sp"], g["kT_sp"], g["ffn_sp"], g["x2_sp"], g["out_t"]
    vT_sp = g["vT_sp"]
    wo_bf = g["wo_bf"]
    t_lnrow = g["t_lnrow"]
    wq_in, wk_in, wv_in, wo_in = g["wq_in"], g["wk_in"], g["wv_in"], g["wo_in"]
    dma, dmat, mm, act, vec, gps = g["dma"], g["dmat"], g["mm"], g["act"], g["vec"], g["gps"]
    e_breg = g["e_breg"]
    e_breg2 = g["e_breg2"]
    tokid_sp, t_stok, t_stokT = g["tokid_sp"], g["t_stok"], g["t_stokT"]
    t_tok64, t_dcol2 = g["t_tok64"], g["t_dcol2"]
    e_x2_last = g["e_x2_last"]
    e_sc_last = list(g["e_sc_prev"])
    e_tok_last = None
    for j in range(64):
        scc = j
        e_dc2 = gps(lambda gg, scc=scc: gg.tensor_copy(t_dcol2[:], t_desti[:, scc:scc + 1]),
                    waits=(g["e_ti"] + [g["e_tk"], g["e_di"]]) if j == 0 else [])
        e_tok_last = P.emit("gpsimd", lambda gg, scc=scc: gg.indirect_dma_start(
            tokid_sp, IndirectOffsetOnAxis(t_dcol2[:], 0), t_tok64[:, scc:scc + 1], None,
            bounds_check=g["e_breg"][0], oob_is_err=False),
            waits=[e_dc2], inc="dmg", k=16)

    g["dbg_evs"] = []
    e_prev_fdma = None
    for b in range(NB):
        if b > 0:
            P.barrier()
        e_tl = []
        for cc in range(8):
            e_tl.append(dmat("sync", t_l2sT[:, cc, :], comp[b * C:(b + 1) * C, 128 * cc:128 * (cc + 1)],
                             waits=list(g["dbg_evs"]) + list(g["e_sc_bybatch"][b])))
        e_on = [vec(lambda v: v.memset(t_vffn[:, :, :], 1.0))]
        e_zm = None
        for qt in range(4):
            for cc in range(8):
                e_zm = mm(lambda t, qt=qt, cc=cc: t.matmul(
                    ps_b[0:1, 512 * qt:512 * (qt + 1)], t_wgbf[:, cc, :],
                    t_l2sT[:, cc, 512 * qt:512 * (qt + 1)],
                    start=(cc == 0), stop=(cc == 7)), waits=e_tl)
        e_gs = act(lambda s: s.activation(t_gsel[0:1, :], ps_b[0:1, 0:2048], AF.Sigmoid), waits=[e_zm])
        if b == 0 and DBG:
            P.emit("gpsimd", lambda g: g.dma_start(vT_sp[0:1, 0:2048], t_gsel[0:1, :]), waits=[e_gs], inc="dmg", k=16)
            P.emit("gpsimd", lambda g: g.dma_start(vT_sp[1:3, 0:2048], t_l2sT[0:2, 0, :]), waits=list(e_tl), inc="dmg", k=16)
            P.emit("gpsimd", lambda g: g.dma_start(vT_sp[3:5, 0:2048], t_l2sT[0:2, 7, :]), waits=list(e_tl), inc="dmg", k=16)
        e_gt = []
        for kc in range(16):
            e_t = mm(lambda t, kc=kc: t.transpose(ps_a[0:128, 0:1],
                                                  t_gsel[0:1, 128 * kc:128 * (kc + 1)], t_ident[0:1, 0:1]),
                     waits=[e_gs] + ([e_gt[-1]] if e_gt else []))
            e_gt.append(vec(lambda v, kc=kc: v.copy(t_gden[:, kc:kc + 1], ps_a[0:128, 0:1]), waits=[e_t]))
        e_spill = []
        e_last_wmm = e_zm
        for w_i, (w_dram, dst) in enumerate([(wq_in, qT_sp), (wk_in, kT_sp)]):
            e_w = dma("gpsimd", t_wffn[:], w_dram.rearrange("(c p) o -> p c o", p=128),
                      waits=[e_last_wmm])
            e_par = [e_gt[-1], e_gt[-1]]
            for oc in range(8):
                par = oc % 2
                pp = ps_a[:, 0:512] if par == 0 else ps_a[:, 512:1024]
                for qt in range(4):
                    e_mm = None
                    for cc in range(8):
                        e_mm = mm(lambda t, oc=oc, qt=qt, cc=cc, pp=pp: t.matmul(
                            pp, t_wffn[:, cc, 128 * oc:128 * (oc + 1)],
                            t_l2sT[:, cc, 512 * qt:512 * (qt + 1)],
                            start=(cc == 0), stop=(cc == 7)),
                            waits=([e_w] + e_tl + [e_par[par]]) if cc == 0 else [])
                    e_ev = act(lambda s, pp=pp: s.activation(t_qksc[:, 0:512], pp, AF.Identity),
                               waits=[e_mm] + (e_spill[-1:] if e_spill else []))
                    e_par[par] = e_ev
                    e_last_wmm = e_mm
                    e_spill.append(dma("scalar", dst[b, 128 * oc:128 * (oc + 1), 512 * qt:512 * (qt + 1)],
                                       t_qksc[:, 0:512], waits=[e_ev]))
        e_wv = dma("gpsimd", t_wffn[:], wv_in.rearrange("(c p) o -> p c o", p=128),
                   waits=[e_last_wmm])
        e_vgrp = []
        for kc in range(16):
            for ot in range(2):
                gi = 2 * kc + ot
                par = gi % 2
                pp = ps_b[:, 0:512] if par == 0 else ps_b[:, 512:1024]
                w0 = [e_wv, e_gs] + e_on
                if gi >= 2:
                    w0 = list(e_vgrp[gi - 2])
                e_mm = None
                for cc in range(8):
                    e_mm = mm(lambda t, kc=kc, ot=ot, cc=cc, pp=pp: t.matmul(
                        pp, t_l2sT[:, cc, 128 * kc:128 * (kc + 1)],
                        t_wffn[:, cc, 512 * ot:512 * (ot + 1)],
                        start=(cc == 0), stop=(cc == 7)), waits=w0 if cc == 0 else [])
                grp = [vec(lambda v, kc=kc, ot=ot, pp=pp: v.copy(
                    t_vffn4[:, kc, 8 * ot:8 * (ot + 1), 0:64], pp[:, 0:512]), waits=[e_mm])]
                e_vgrp.append(grp)
        e_wo0 = dma("gpsimd", wo_bf, wo_in)
        e_wo = dma("sync", t_wo2[:], wo_bf.rearrange("(h p) o -> p h o", p=64), waits=[e_wo0])
        e_prev_av = e_vgrp[-1][-1]
        e_na = e_vgrp[-1][-1]
        e_prev_scores = None
        e_prev_avmm = None
        for h in range(H):
            e_kl = dma("sync", t_kh[:], kT_sp[b, DH * h:DH * (h + 1), :],
                       waits=[e_spill[-1], e_prev_scores])
            e_ql = dma("sync", t_qh[:], qT_sp[b, DH * h:DH * (h + 1), :],
                       waits=[e_spill[-1], e_prev_scores])
            e_qt_av = []
            for qt in range(4):
                e_ex = []
                for kc in range(16):
                    sps = [ps_a[:, 0:512], ps_a[:, 512:1024],
                           ps_b[:, 0:512], ps_b[:, 512:1024]][kc % 4]
                    w_s = [e_kl, e_ql] + ([e_ex[kc - 4]] if kc >= 4 else [])
                    if h == 0 and qt == 0 and kc in (2, 3):
                        w_s += [e_prev_av]
                    e_s = mm(lambda t, kc=kc, qt=qt, sps=sps: t.matmul(
                        sps, t_kh[:, 128 * kc:128 * (kc + 1)], t_qh[:, 512 * qt:512 * (qt + 1)],
                        start=True, stop=True), waits=w_s)
                    e_ex.append(act(lambda s, kc=kc, b=b, sps=sps: s.activation(
                        t_expS[:, kc, :], sps, AF.Exp,
                        bias=t_kbias[:, 16 * b + kc:16 * b + kc + 1],
                        scale=0.125), waits=[e_s, e_prev_avmm]))
                e_prev_scores = e_ex[-1]
                e_mm = None
                for kc in range(16):
                    e_mm = mm(lambda t, kc=kc, h=h: t.matmul(
                        ps_a[0:65, 1024:1536], t_vffn[:, kc, 65 * h:65 * (h + 1)],
                        t_expS[:, kc, :], start=(kc == 0), stop=(kc == 15)),
                        waits=[e_ex[kc]] + ([e_prev_av] if kc == 0 else []))
                e_prev_avmm = e_mm
                e_a1 = vec(lambda v, h=h, qt=qt: v.copy(t_aoT[:, h, 512 * qt:512 * (qt + 1)],
                                                        ps_a[0:64, 1024:1536]), waits=[e_mm])
                e_d1 = vec(lambda v, qt=qt: v.copy(t_drow[64:65, 512 * qt:512 * (qt + 1)],
                                                   ps_a[64:65, 1024:1536]), waits=[e_mm])
                e_prev_av = e_d1
                e_qt_av.append(e_a1)
            e_rc0 = act(lambda s: s.activation(t_drow[64:65, :], t_drow[64:65, :], AF.Ln),
                        waits=[e_prev_av])
            e_rc = act(lambda s: s.activation(t_drow[64:65, :], t_drow[64:65, :], AF.Exp,
                                              scale=-1.0), waits=[e_rc0])
            e_ntt = [None, None]
            for qt in range(4):
                par = qt % 2
                npp = ps_b[0:64, 1024 + 512 * par:1536 + 512 * par]
                e_nb = mm(lambda t, qt=qt, npp=npp: t.matmul(
                    npp, t_onesall[64:65, 0:64], t_drow[64:65, 512 * qt:512 * (qt + 1)],
                    start=True, stop=True), waits=[e_rc] + ([e_ntt[par]] if e_ntt[par] else []))
                e_ntt[par] = vec(lambda v, h=h, qt=qt, npp=npp: v.tt(
                    t_aoT[:, h, 512 * qt:512 * (qt + 1)],
                    t_aoT[:, h, 512 * qt:512 * (qt + 1)], npp, OP.mult),
                    waits=[e_nb, e_qt_av[qt]])
            e_na = e_ntt[1]
            if h == 0 and b == 0 and DBG:
                e_dbg1 = P.emit("gpsimd", lambda g: g.dma_start(vT_sp[8:72, 0:2048], t_aoT[:, 0, :]),
                                waits=[e_na], inc="dmg", k=16)
                e_dbg2 = P.emit("gpsimd", lambda g: g.dma_start(vT_sp[5:6, 0:2048], t_drow[64:65, :]),
                                waits=[e_prev_av], inc="dmg", k=16)
                g["dbg_evs"].extend([e_dbg1, e_dbg2])
        if b == 0 and g.get("dbg_big") is not None:
            dbg_big = g["dbg_big"]
            for hh2 in range(16):
                e_dmp = P.emit("gpsimd", lambda g2, hh2=hh2, dbg_big=dbg_big: g2.dma_start(
                    dbg_big[64 * hh2:64 * (hh2 + 1), 0:2048], t_aoT[:, hh2, :]),
                    waits=[e_na], inc="dmg", k=16)
                g["dbg_evs"].append(e_dmp)
        e_ix0 = dma("sync", t_stokT[:],
                    tokid_sp[b * C:(b + 1) * C, :].rearrange("(c p) o -> c (p o)", p=128),
                    waits=[e_tok_last])
        e_trx = mm(lambda t: t.transpose(ps_a[0:128, 1536:1552], t_stokT[:], t_ident[0:16, 0:16]),
                   waits=[e_ix0])
        e_ix = vec(lambda v, b=b: v.copy(t_stok[:, 16 * b:16 * (b + 1)], ps_a[0:128, 1536:1552]),
                   waits=[e_trx])
        e_prev_ev = [e_na, e_na]
        e_prev_eb = None
        for sl in range(16):
            par = sl % 2
            pp = ps_b[:, 0:1024] if par == 0 else ps_b[:, 1024:2048]
            e_mm = None
            for ot in range(2):
                for h in range(H):
                    e_mm = mm(lambda t, sl=sl, h=h, ot=ot, pp=pp: t.matmul(
                        pp[:, 512 * ot:512 * (ot + 1)],
                        t_aoT[:, h, 128 * sl:128 * (sl + 1)],
                        t_wo2[:, h, 512 * ot:512 * (ot + 1)],
                        start=(h == 0), stop=(h == H - 1)),
                        waits=[e_na, e_wo, e_prev_ev[par]] if (h == 0 and ot == 0) else [])
            e_g = vec(lambda v, sl=sl, pp=pp: v.ts(t_orow[:], pp, t_gden[:, sl:sl + 1], None, OP.mult),
                      waits=[e_mm, e_gt[-1], e_prev_eb])
            if b == 0 and sl == 2 and g.get("dbg_big") is not None:
                e_g = P.emit("gpsimd", lambda g2: g2.dma_start(g["dbg_big"][1024:1152, 0:1024], t_orow[:]),
                             waits=[e_g], inc="dmg", k=16)
                g["dbg_evs"].append(e_g)
            if b == 0 and sl == 0:
                e_gd = P.emit("gpsimd", lambda g2: g2.dma_start(vT_sp[80:208, 0:16], t_gden[:]),
                              waits=[e_gt[-1]], inc="dmg", k=16)
                g["dbg_evs"].append(e_gd)
            e_prev_ev[par] = e_g
            ea = act(lambda s: s.activation(t_qksc[:, 0:1024], t_orow[:], AF.Identity,
                                            accum_out=t_drow[0:128, 0:1]), waits=[e_g])
            eb = act(lambda s: s.activation(t_qksc[:, 0:1024], t_orow[:], AF.Square,
                                            accum_out=t_drow[0:128, 1:2]), waits=[e_g])
            e_prev_eb = eb
            vec(lambda v: v.ts(t_drow[0:128, 0:1], t_drow[0:128, 0:1], 1.0 / D, None, OP.mult), waits=[ea, eb])
            vec(lambda v: v.tt(t_drow[0:128, 2:3], t_drow[0:128, 0:1], t_drow[0:128, 0:1], OP.mult))
            vec(lambda v: v.ts(t_drow[0:128, 1:2], t_drow[0:128, 1:2], 1.0 / D, None, OP.mult))
            vec(lambda v: v.tt(t_drow[0:128, 1:2], t_drow[0:128, 1:2], t_drow[0:128, 2:3], OP.subtract))
            ev5 = vec(lambda v: v.ts(t_drow[0:128, 1:2], t_drow[0:128, 1:2], EPS, None, OP.add))
            ev6 = act(lambda s: s.activation(t_drow[0:128, 1:2], t_drow[0:128, 1:2], AF.Sqrt), waits=[ev5])
            ev7 = vec(lambda v: v.recip(t_drow[0:128, 1:2], t_drow[0:128, 1:2]), waits=[ev6])
            ev8 = vec(lambda v: v.ts(t_drow[0:128, 0:1], t_drow[0:128, 0:1], -1.0, None, OP.mult))
            e_ln = vec(lambda v: v.ts(t_lnrow[:], t_orow[:], t_drow[0:128, 0:1], t_drow[0:128, 1:2],
                                      OP.add, OP.mult), waits=[ev7, ev8, e_prev_fdma])
            e_prev_fdma = P.emit("gpsimd", lambda gg, b=b, sl=sl: gg.indirect_dma_start(
                out_t, IndirectOffsetOnAxis(t_stok[:, 16 * b + sl:16 * b + sl + 1], 0),
                t_lnrow[:], None, bounds_check=e_breg2[0], oob_is_err=False,
                compute_op=mybir.AluOpType.add),
                waits=[e_ln, e_ix, e_x2_last], inc="dmg", k=16)
    P.barrier()


_CACHE = {}


def kernel(**inputs):
    B = 16
    x = np.asarray(inputs["L0_attn_x"], dtype=np.float32)
    if "nc" not in _CACHE:
        _CACHE["nc"] = build_nc()
    nc = _CACHE["nc"]

    tri = (np.arange(128)[:, None] < np.arange(128)[None, :]).astype(np.float32)
    onescf = np.ones((128, 1), np.float32)
    onesrf = np.ones((1, 128), np.float32)
    onesall = np.ones((128, 128), np.float32)
    ident = np.eye(128, dtype=np.float32)
    slotid = (np.arange(128)[:, None] + 128.0 * np.arange(16)[None, :]).astype(np.float32)

    shared = dict(
        qkv_w=np.ascontiguousarray(inputs["L0_attn_qkv_w"], dtype=np.float32),
        proj_w=np.ascontiguousarray(inputs["L0_attn_proj_w"], dtype=np.float32),
        ch_w1=np.ascontiguousarray(inputs["L0_attn_ch_w1"], dtype=np.float32),
        ch_w2=np.ascontiguousarray(inputs["L0_attn_ch_w2"], dtype=np.float32),
        ch_b1=np.ascontiguousarray(inputs["L0_attn_ch_b1"], dtype=np.float32),
        ch_b2=np.ascontiguousarray(inputs["L0_attn_ch_b2"], dtype=np.float32),
        proj_b=np.ascontiguousarray(inputs["L0_attn_proj_b"], dtype=np.float32),
        wg_in=np.ascontiguousarray(inputs["L0_ffn_Wg"], dtype=np.float32),
        wq_in=np.ascontiguousarray(inputs["L0_ffn_Wq"], dtype=np.float32),
        wk_in=np.ascontiguousarray(inputs["L0_ffn_Wk"], dtype=np.float32),
        wv_in=np.ascontiguousarray(inputs["L0_ffn_Wv"], dtype=np.float32),
        wo_in=np.ascontiguousarray(inputs["L0_ffn_Wo"], dtype=np.float32),
        in_c1=np.asarray(inputs["L0_attn_c1"], dtype=np.float32).reshape(1),
        in_c2=np.asarray(inputs["L0_attn_c2"], dtype=np.float32).reshape(1),
        in_conv1=np.ascontiguousarray(inputs["L0_attn_conv1_w"], dtype=np.float32),
        in_gn_g=np.ascontiguousarray(inputs["L0_attn_gn_g"], dtype=np.float32),
        in_gn_b=np.ascontiguousarray(inputs["L0_attn_gn_b"], dtype=np.float32),
        in_conv2=np.ascontiguousarray(inputs["L0_attn_conv2_w"], dtype=np.float32),
        in_conv2b=np.asarray(inputs["L0_attn_conv2_b"], dtype=np.float32).reshape(1),
        tri_in=tri, onescf_in=onescf, onescr_in=onescf, onesrf_in=onesrf,
        onesrr_in=onesrf, onesall_in=onesall, ident_in=ident, slotid_in=slotid,
    )
    in_maps = []
    for c in range(8):
        xs = x[2 * c:2 * c + 2]
        x_tm = np.ascontiguousarray(xs.reshape(2 * N, D))
        xT = np.ascontiguousarray(np.moveaxis(xs, 2, 0).reshape(D, 2 * N))
        m = dict(shared)
        m["x_tm"] = x_tm
        m["xT"] = xT
        in_maps.append(m)

    res = run_bass_kernel_spmd(nc, in_maps, core_ids=list(range(8)))
    out = np.concatenate([r["out_t"].reshape(2, N, D) for r in res.results], axis=0)
    return out
